# version 09
# baseline (speedup 1.0000x reference)
import math
import os
from contextlib import ExitStack

import numpy as np

import concourse.bass as bass
import concourse.mybir as mybir
from concourse.bass_utils import run_bass_kernel_spmd

F32 = mybir.dt.float32
BF16 = mybir.dt.bfloat16
I32 = mybir.dt.int32
ALU = mybir.AluOpType
ACT = mybir.ActivationFunctionType
AX = mybir.AxisListType

B, T, D = 2, 4096, 2048
NT = T // 128
NQT = 8
NI = D // 128
QO = 512
KVO = 256
NHL = 4
IO = 1408
NIO = IO // 128
NE = 8
EPS_LN = 1e-5
EPS_RMS = float(np.finfo(np.float32).eps)
SQRT_HD = math.sqrt(128.0)
TWO_PI = float(2.0 * math.pi)
PI = float(math.pi)
LOG_BASE = float(math.log(10000.0))

ENGINES = ("sync", "scalar", "vector", "tensor", "gpsimd")


class Sem:
    def __init__(self, es, nc, name):
        self.h = es.enter_context(nc.semaphore(name))
        self.n = 0


class Alloc:

    def __init__(self, nc):
        self.nc = nc
        self.live = {}
        self.order = []

    def sbuf(self, name, shape, dt, side="left"):
        g = self.nc.sbuf_tensor(name, shape, dt, side=side)
        h = g.__enter__()
        self.live[name] = g
        self.order.append(name)
        return h

    def psum(self, name, shape, dt, side="left"):
        g = self.nc.psum_tensor(name, shape, dt, side=side)
        h = g.__enter__()
        self.live[name] = g
        self.order.append(name)
        return h

    def free(self, *names):
        for n in names:
            self.live.pop(n).__exit__(None, None, None)
            self.order.remove(n)

    def close(self):
        for n in reversed(self.order):
            self.live.pop(n).__exit__(None, None, None)
        self.order = []


class Prog:
    def __init__(self):
        self.thunks = {e: [] for e in ENGINES}
        self.label = "P0"

    def emit(self, eng, f):
        self.thunks[eng].append((self.label, f))

    def wait(self, eng, sem, val=None):
        v = sem.n if val is None else val
        if v > 0:
            self.emit(eng, lambda e, h=sem.h, v=v: e.wait_ge(h, v))

    def op(self, eng, builder, inc=None, k=1):
        if inc is not None:
            inc.n += k

        def th(e, b=builder, s=inc, kk=k):
            ins = b(e)
            if s is not None:
                ins.then_inc(s.h, kk)

        self.emit(eng, th)
        return inc.n if inc is not None else None


def _build_program(upto=99, debug=()):
    nc = bass.Bass("TRN2", target_bir_lowering=False)

    d_xf = nc.dram_tensor("xf", [T, D], F32, kind="ExternalInput").ap()
    d_xo = nc.dram_tensor("xo", [1024, D], F32, kind="ExternalInput").ap()
    d_qg = nc.dram_tensor("qgg", [1, NHL], F32, kind="ExternalInput").ap()
    d_wq = nc.dram_tensor("wq_r", [128, NI, QO], F32, kind="ExternalInput").ap()
    d_kv = nc.dram_tensor("kv_r", [128, NI, KVO], F32, kind="ExternalInput").ap()
    d_wp = nc.dram_tensor("wp_r", [128, 4, D], F32, kind="ExternalInput").ap()
    d_rw = nc.dram_tensor("rw_r", [128, NI, NE], F32, kind="ExternalInput").ap()
    d_sg = nc.dram_tensor("sg_r", [NIO, 128, NI, 128], F32, kind="ExternalInput").ap()
    d_su = nc.dram_tensor("su_r", [NIO, 128, NI, 128], F32, kind="ExternalInput").ap()
    d_sd = nc.dram_tensor("sd_r", [4, 128, NIO, 512], F32, kind="ExternalInput").ap()
    d_wg = nc.dram_tensor("wg_r", [NE, NIO, 128, NI, 128], F32, kind="ExternalInput").ap()
    d_wu = nc.dram_tensor("wu_r", [NE, NIO, 128, NI, 128], F32, kind="ExternalInput").ap()
    d_wd = nc.dram_tensor("wd_r", [NE, 4, 128, NIO, 512], F32, kind="ExternalInput").ap()
    d_prs = nc.dram_tensor("prs", [4, 4, 256, D], BF16).ap()
    d_pq = nc.dram_tensor("pq", [1024, D], BF16).ap()
    d_out = nc.dram_tensor("out", [1024, D], F32, kind="ExternalOutput").ap()
    d_dbg = {}
    for name, shape, dt in debug:
        d_dbg[name] = nc.dram_tensor("dbg_" + name, list(shape), dt,
                                     kind="ExternalOutput").ap()

    es = ExitStack()
    A = Alloc(nc)

    cpool = A.sbuf("cpool", [128, 8], F32)
    ident = A.sbuf("ident", [128, 128], BF16)
    onesb = A.sbuf("onesb", [128, 8], BF16)
    onesf32 = A.sbuf("onesf32", [1, 128], F32)
    bcast16 = A.sbuf("bcast16", [128, 16], F32)
    srow = A.sbuf("srow", [128, 16], F32)
    rp = A.sbuf("rp", [128, 96], F32)
    qgrow = A.sbuf("qgrow", [1, NHL], F32)
    lrow = A.sbuf("lrow", [1, 512], F32)
    rrow = A.sbuf("rrow", [1, 512], F32)
    rbc = A.sbuf("rbc", [128, 512], F32)
    LG = A.sbuf("LG", [128, 64], F32)
    combW = A.sbuf("combW", [128, 64], F32)
    rscr = A.sbuf("rscr", [128, 336], F32)
    sgb = A.sbuf("sgb", [128, 1024], F32)
    masks = A.sbuf("masks", [128, 4 * 512], BF16, side="right")
    cosT = A.sbuf("cosT", [128, NT * 8], F32, side="right")
    sinT = A.sbuf("sinT", [128, NT * 8], F32, side="right")
    tbl = A.sbuf("tbl", [128, 320], F32, side="right")
    tbli = A.sbuf("tbli", [128, 256], I32, side="right")
    tscr = A.sbuf("tscr", [128, 512], F32, side="right")

    CONSTS = {0.0: 0, 1.0: 1, EPS_LN: 2, EPS_RMS: 3,
              -LOG_BASE / 8.0: 4}
    for val, i in CONSTS.items():
        nc.const_aps.register(F32, val, cpool[:, i:i + 1])

    P = Prog()
    s_pe = Sem(es, nc, "s_pe")
    s_dv = Sem(es, nc, "s_dv")
    s_sc = Sem(es, nc, "s_sc")
    s_gp = Sem(es, nc, "s_gp")
    s_cc = Sem(es, nc, "s_cc")
    s_sx = [Sem(es, nc, f"s_sx{i}") for i in range(4)]
    s_gx = [Sem(es, nc, f"s_gx{i}") for i in range(8)]
    s_out = Sem(es, nc, "s_out")

    def chain(eng, sem):
        P.wait(eng, sem)

    def dbg_dump(name, src_ap):
        if name in d_dbg:
            for s in (s_dv, s_pe, s_sc, s_gp):
                P.wait("sync", s)
            P.op("sync", lambda e, d=d_dbg[name], s=src_ap: e.dma_start(
                out=d, in_=s), inc=s_out, k=16)

    for val, i in CONSTS.items():
        P.op("vector", lambda e, a=cpool[:, i:i + 1], v=val: e.memset(a, v))
    P.op("vector", lambda e: e.memset(srow[:], 0.0))
    P.op("vector", lambda e: e.memset(onesf32[:], 1.0))
    P.op("vector", lambda e: e.memset(rp[:], 0.0))
    v_consts = P.op("vector", lambda e: e.memset(tbl[:], 0.0), inc=s_dv)

    P.op("gpsimd", lambda e: e.memset(onesb[:], 1.0), inc=s_gp)
    chain("gpsimd", s_gp)
    P.op("gpsimd", lambda e: e.affine_select(
        out=ident[:], in_=onesb[:, 0:1].broadcast_to([128, 128]),
        pattern=[[1, 128]], compare_op=ALU.is_equal, fill=0.0,
        base=0, channel_multiplier=-1), inc=s_gp)
    chain("gpsimd", s_gp)
    for j in range(4):
        P.op("gpsimd", lambda e, j=j: e.affine_select(
            out=masks[:, j * 512:(j + 1) * 512],
            in_=onesb[:, 0:1].broadcast_to([128, 512]),
            pattern=[[1, 512]], compare_op=ALU.is_ge, fill=0.0,
            base=-128 * j, channel_multiplier=-1), inc=s_gp)
        chain("gpsimd", s_gp)
    P.wait("gpsimd", s_dv, v_consts)
    P.op("gpsimd", lambda e: e.iota(
        tbl[:, 0:32], pattern=[[128, 32]], base=0, channel_multiplier=1,
        allow_small_or_imprecise_dtypes=True), inc=s_gp)
    P.op("gpsimd", lambda e: e.iota(
        tbl[0:1, 32:40], pattern=[[1, 8]], base=0, channel_multiplier=0,
        allow_small_or_imprecise_dtypes=True), inc=s_gp)
    g_iota = s_gp.n

    P.wait("scalar", s_gp, g_iota)
    P.wait("scalar", s_dv, v_consts)
    sc_inv = P.op("scalar", lambda e: e.activation(
        out=tbl[0:1, 48:56], in_=tbl[0:1, 32:40], func=ACT.Exp,
        scale=-LOG_BASE / 8.0), inc=s_sc)
    P.op("sync", lambda e: e.dma_start(out=qgrow[:], in_=d_qg), inc=s_sx[2], k=16)
    P.wait("scalar", s_sx[2], 16)
    sc_absqg = P.op("scalar", lambda e: e.activation(
        out=tbl[0:1, 56:60], in_=qgrow[:], func=ACT.Abs), inc=s_sc)
    P.wait("vector", s_sc, sc_absqg)
    P.op("vector", lambda e: e.tensor_scalar(
        out=tbl[0:1, 56:60], in0=tbl[0:1, 56:60], scalar1=-SQRT_HD,
        scalar2=None, op0=ALU.mult), inc=s_dv)
    P.op("vector", lambda e: e.tensor_scalar(
        out=tbl[0:1, 60:64], in0=qgrow[:], scalar1=1.0 / SQRT_HD,
        scalar2=None, op0=ALU.mult), inc=s_dv)
    v_qg = s_dv.n
    ps0 = A.psum("ps0tmp", [128, 512], F32)
    P.wait("tensor", s_dv, v_qg)
    P.wait("tensor", s_sc, sc_inv)
    pe_bc = P.op("tensor", lambda e: e.matmul(
        ps0[:, 0:16], onesf32[0:1, :], tbl[0:1, 48:64],
        start=True, stop=True), inc=s_pe)
    P.wait("vector", s_pe, pe_bc)
    v_bc16 = P.op("vector", lambda e: e.tensor_copy(
        bcast16[:], ps0[:, 0:16]), inc=s_dv)
    chain("vector", s_dv)

    fv = tbl[:, 64:320]
    fv3 = fv.rearrange("p (a b) -> p a b", a=NT)
    P.op("vector", lambda e: e.tensor_tensor(
        out=fv3, in0=tbl[:, 0:32].unsqueeze(2).broadcast_to([128, NT, 8]),
        in1=bcast16[:, 0:8].unsqueeze(1).broadcast_to([128, NT, 8]),
        op=ALU.mult), inc=s_dv)
    chain("vector", s_dv)
    kf = tscr[:, 0:256]
    gt = tscr[:, 256:512]
    for dst, shift in ((sinT, 0.0), (cosT, PI / 2.0)):
        if shift != 0.0:
            P.op("vector", lambda e, s=shift: e.tensor_scalar(
                out=kf, in0=fv, scalar1=s, scalar2=float(1.0 / TWO_PI),
                op0=ALU.add, op1=ALU.mult), inc=s_dv)
        else:
            P.op("vector", lambda e: e.tensor_scalar(
                out=kf, in0=fv, scalar1=float(1.0 / TWO_PI), scalar2=None,
                op0=ALU.mult), inc=s_dv)
        chain("vector", s_dv)
        P.op("vector", lambda e: e.tensor_copy(tbli[:], kf), inc=s_dv)
        chain("vector", s_dv)
        P.op("vector", lambda e: e.tensor_copy(kf, tbli[:]), inc=s_dv)
        chain("vector", s_dv)
        if shift != 0.0:
            P.op("vector", lambda e, s=shift: e.tensor_scalar(
                out=gt, in0=fv, scalar1=s, scalar2=None, op0=ALU.add),
                inc=s_dv)
            chain("vector", s_dv)
            P.op("vector", lambda e, d=dst: e.scalar_tensor_tensor(
                out=d[:], in0=kf, scalar=-TWO_PI, in1=gt,
                op0=ALU.mult, op1=ALU.add), inc=s_dv)
        else:
            P.op("vector", lambda e, d=dst: e.scalar_tensor_tensor(
                out=d[:], in0=kf, scalar=-TWO_PI, in1=fv,
                op0=ALU.mult, op1=ALU.add), inc=s_dv)
        chain("vector", s_dv)
        P.op("vector", lambda e, d=dst: e.tensor_scalar(
            out=gt, in0=d[:], scalar1=PI, scalar2=None, op0=ALU.is_ge),
            inc=s_dv)
        chain("vector", s_dv)
        P.op("vector", lambda e, d=dst: e.scalar_tensor_tensor(
            out=d[:], in0=gt, scalar=-TWO_PI, in1=d[:],
            op0=ALU.mult, op1=ALU.add), inc=s_dv)
        chain("vector", s_dv)
    v_angles = s_dv.n
    P.wait("scalar", s_dv, v_angles)
    for dst in (sinT, cosT):
        P.op("scalar", lambda e, d=dst: e.activation(
            out=d[:], in_=d[:], func=ACT.Sin), inc=s_sc)
        chain("scalar", s_sc)
    sc_tables = s_sc.n

    g_setup = s_gp.n
    v_setup = s_dv.n
    A.free("ps0tmp")
    A.free("tscr", "tbli", "tbl")

    YFM = A.sbuf("YFM", [128, NHL * T], BF16)
    KVTM = A.sbuf("KVTM", [128, NT * KVO], BF16)
    QFM = A.sbuf("QFM", [128, NHL * T], BF16)
    KFM = A.sbuf("KFM", [128, T], BF16)
    wq_sb = A.sbuf("wq_sb", [128, NI * QO], BF16)
    kv_sb = A.sbuf("kv_sb", [128, NI * KVO], BF16)
    P.op("gpsimd", lambda e: e.dma_start(
        out=wq_sb[:], in_=d_wq.rearrange("p a b -> p (a b)")), inc=s_gx[0], k=16)
    g_wq = s_gx[0].n
    P.op("gpsimd", lambda e: e.dma_start(
        out=kv_sb[:], in_=d_kv.rearrange("p a b -> p (a b)")), inc=s_gx[1], k=16)
    g_kv = s_gx[1].n

    QTM = A.sbuf("QTM", [128, NT * QO], BF16)
    xbuf = [A.sbuf(f"xbuf{i}", [128, 2048], F32) for i in range(2)]
    sq2 = [A.sbuf(f"sq2{i}", [128, 704], F32) for i in range(2)]
    z1tm = [A.sbuf(f"z1tm{i}", [128, 2048], BF16) for i in range(2)]
    z1f = [A.sbuf(f"z1f{i}", [128, 2048], BF16) for i in range(2)]
    psT = [A.psum(f"psT{i}", [128, 2048], BF16) for i in range(2)]
    psQ = [A.psum(f"psQ{i}", [128, 512], F32) for i in range(2)]
    psKV = [A.psum(f"psKV{i}", [128, 512], F32) for i in range(2)]

    P.wait("tensor", s_gx[0], g_wq)
    P.wait("tensor", s_gx[1], g_kv)
    P.wait("tensor", s_gp, g_setup)
    P.wait("tensor", s_dv, v_setup)
    P.wait("scalar", s_dv, v_angles)

    done_x_free = [0, 0]
    done_dv_sq = [0, 0]
    done_sc_z1f = [0, 0]
    done_sc_evac = [0, 0]
    done_pe_T = [0, 0]
    done_dv_z1f = [0, 0]
    done_pe_qkv = [0, 0]
    done_dv_evac = [0, 0]

    def emit_ln(tt, sl, src_tile, zdst, c0, sqbuf, prev_dv=0):
        P.op("vector", lambda e, s=src_tile, c=c0: e.tensor_reduce(
            out=srow[:, c:c + 1], in_=s, axis=AX.X, op=ALU.add), inc=s_dv)
        if prev_dv:
            P.wait("scalar", s_dv, prev_dv)
        chain("scalar", s_sc)
        P.op("scalar", lambda e, s=src_tile, c=c0, q=sqbuf: e.activation(
            out=q, in_=s, func=ACT.Square,
            accum_out=srow[:, c + 1:c + 2]), inc=s_sc)
        sc_sq = s_sc.n
        chain("vector", s_dv)
        P.op("vector", lambda e, c=c0: e.tensor_scalar(
            out=srow[:, c:c + 1], in0=srow[:, c:c + 1],
            scalar1=1.0 / 2048.0, scalar2=None, op0=ALU.mult), inc=s_dv)
        P.wait("vector", s_sc, sc_sq)
        P.op("vector", lambda e, c=c0: e.tensor_scalar(
            out=srow[:, c + 1:c + 2], in0=srow[:, c + 1:c + 2],
            scalar1=1.0 / 2048.0, scalar2=None, op0=ALU.mult), inc=s_dv)
        chain("vector", s_dv)
        P.op("vector", lambda e, c=c0: e.tensor_tensor(
            out=srow[:, c + 2:c + 3], in0=srow[:, c:c + 1],
            in1=srow[:, c:c + 1], op=ALU.mult), inc=s_dv)
        chain("vector", s_dv)
        P.op("vector", lambda e, c=c0: e.tensor_tensor(
            out=srow[:, c + 2:c + 3], in0=srow[:, c + 1:c + 2],
            in1=srow[:, c + 2:c + 3], op=ALU.subtract), inc=s_dv)
        v_var = s_dv.n
        P.wait("scalar", s_dv, v_var)
        P.op("scalar", lambda e, c=c0: e.activation(
            out=srow[:, c + 2:c + 3], in_=srow[:, c + 2:c + 3], func=ACT.Sqrt,
            bias=EPS_LN), inc=s_sc)
        sc_sd = s_sc.n
        P.wait("vector", s_sc, sc_sd)
        P.op("vector", lambda e, c=c0: e.reciprocal(
            out=srow[:, c + 2:c + 3], in_=srow[:, c + 2:c + 3]), inc=s_dv)
        P.op("vector", lambda e, c=c0: e.tensor_scalar(
            out=srow[:, c:c + 1], in0=srow[:, c:c + 1],
            scalar1=-1.0, scalar2=None, op0=ALU.mult), inc=s_dv)
        chain("vector", s_dv)
        P.op("vector", lambda e, s=src_tile, z=zdst, c=c0: e.tensor_scalar(
            out=z, in0=s, scalar1=srow[:, c:c + 1],
            scalar2=srow[:, c + 2:c + 3], op0=ALU.add, op1=ALU.mult),
            inc=s_dv)
        return s_dv.n

    P.label = "P1"
    for tt in range(NT):
        sl = tt % 2
        if done_x_free[sl]:
            P.wait("sync", s_dv, done_x_free[sl])
        P.op("sync", lambda e, tt=tt, sl=sl: e.dma_start(
            out=xbuf[sl][:], in_=d_xf[tt * 128:(tt + 1) * 128, :]),
            inc=s_sx[sl], k=16)
        v_x = s_sx[sl].n
        P.wait("vector", s_sx[sl], v_x)
        P.wait("scalar", s_sx[sl], v_x)
        if done_pe_T[sl]:
            P.wait("vector", s_pe, done_pe_T[sl])
            P.wait("scalar", s_pe, done_pe_T[sl])
        v_apply = emit_ln(tt, sl, xbuf[sl][:], z1tm[sl][:],
                          0 if sl == 0 else 4, z1tm[sl][:],
                          prev_dv=done_x_free[sl])
        done_x_free[sl] = v_apply

        P.wait("tensor", s_dv, v_apply)
        if done_dv_z1f[sl]:
            P.wait("tensor", s_dv, done_dv_z1f[sl])
        for ic in range(NI):
            P.op("tensor", lambda e, ic=ic, sl=sl: e.transpose(
                psT[sl][:, ic * 128:(ic + 1) * 128],
                z1tm[sl][:, ic * 128:(ic + 1) * 128], ident[:]),
                inc=s_pe if ic == NI - 1 else None)
        done_pe_T[sl] = s_pe.n
        P.wait("scalar", s_pe, done_pe_T[sl])
        if done_pe_qkv[sl]:
            P.wait("scalar", s_pe, done_pe_qkv[sl])
        P.op("scalar", lambda e, sl=sl: e.copy(
            out=z1f[sl][:], in_=psT[sl][:]), inc=s_sc)
        done_sc_z1f[sl] = s_sc.n
        P.wait("tensor", s_sc, done_sc_z1f[sl])
        if done_sc_evac[sl]:
            P.wait("tensor", s_sc, done_sc_evac[sl])
        for ic in range(NI):
            P.op("tensor", lambda e, ic=ic, sl=sl: e.matmul(
                psQ[sl][:], z1f[sl][:, ic * 128:(ic + 1) * 128],
                wq_sb[:, ic * QO:(ic + 1) * QO],
                start=(ic == 0), stop=(ic == NI - 1)), inc=None)
            P.op("tensor", lambda e, ic=ic, sl=sl: e.matmul(
                psKV[sl][:, 0:KVO], z1f[sl][:, ic * 128:(ic + 1) * 128],
                kv_sb[:, ic * KVO:(ic + 1) * KVO],
                start=(ic == 0), stop=(ic == NI - 1)),
                inc=s_pe if ic == NI - 1 else None)
        done_pe_qkv[sl] = s_pe.n
        P.wait("scalar", s_pe, done_pe_qkv[sl])
        P.op("scalar", lambda e, tt=tt, sl=sl: e.copy(
            out=QTM[:, tt * QO:(tt + 1) * QO], in_=psQ[sl][:]), inc=s_sc)
        P.op("scalar", lambda e, tt=tt, sl=sl: e.copy(
            out=KVTM[:, tt * KVO:(tt + 1) * KVO], in_=psKV[sl][:, 0:KVO]),
            inc=s_sc)
        done_sc_evac[sl] = s_sc.n
        qs = tt * QO
        ks = tt * KVO
        sq = sq2[sl]
        if done_dv_sq[sl]:
            P.wait("scalar", s_dv, done_dv_sq[sl])
        chain("scalar", s_sc)
        P.op("scalar", lambda e, qs=qs, sq=sq: e.activation(
            out=sq[:, 0:512], in_=QTM[:, qs:qs + 512], func=ACT.Square),
            inc=s_sc)
        P.op("scalar", lambda e, ks=ks, sq=sq: e.activation(
            out=sq[:, 512:640], in_=KVTM[:, ks:ks + 128], func=ACT.Square),
            inc=s_sc)
        sc_q = s_sc.n
        P.wait("vector", s_sc, sc_q)
        P.op("vector", lambda e, sq=sq: e.tensor_reduce(
            out=srow[:, 8:12],
            in_=sq[:, 0:512].rearrange("p (h d) -> p h d", h=4),
            axis=AX.X, op=ALU.add), inc=s_dv)
        P.op("vector", lambda e, sq=sq: e.tensor_reduce(
            out=srow[:, 12:13], in_=sq[:, 512:640], axis=AX.X, op=ALU.add),
            inc=s_dv)
        chain("vector", s_dv)
        P.op("vector", lambda e: e.tensor_scalar(
            out=srow[:, 8:13], in0=srow[:, 8:13], scalar1=1.0 / 128.0,
            scalar2=None, op0=ALU.mult), inc=s_dv)
        v_ms = s_dv.n
        P.wait("scalar", s_dv, v_ms)
        P.op("scalar", lambda e: e.activation(
            out=srow[:, 8:13], in_=srow[:, 8:13], func=ACT.Sqrt,
            bias=EPS_RMS), inc=s_sc)
        sc_rs = s_sc.n
        P.wait("vector", s_sc, sc_rs)
        P.op("vector", lambda e: e.reciprocal(
            out=srow[:, 8:13], in_=srow[:, 8:13]), inc=s_dv)
        chain("vector", s_dv)
        qv = QTM[:, qs:qs + 512].rearrange("p (h d) -> p h d", h=4)
        P.op("vector", lambda e, qv=qv: e.scalar_tensor_tensor(
            out=qv, in0=qv, scalar=1.0,
            in1=srow[:, 8:12].unsqueeze(2).broadcast_to([128, 4, 128]),
            op0=ALU.mult, op1=ALU.mult), inc=s_dv)
        P.op("vector", lambda e, ks=ks: e.tensor_scalar(
            out=KVTM[:, ks:ks + 128], in0=KVTM[:, ks:ks + 128],
            scalar1=srow[:, 12:13], scalar2=None, op0=ALU.mult), inc=s_dv)
        chain("vector", s_dv)
        qA = qv[:, :, 0:8]
        qB = qv[:, :, 8:16]
        kA = KVTM[:, ks:ks + 8]
        kB = KVTM[:, ks + 8:ks + 16]
        cosb = cosT[:, tt * 8:(tt + 1) * 8]
        sinb = sinT[:, tt * 8:(tt + 1) * 8]
        cosb4 = cosb.unsqueeze(1).broadcast_to([128, 4, 8])
        sinb4 = sinb.unsqueeze(1).broadcast_to([128, 4, 8])
        rA = rp[:, 0:32].rearrange("p (h d) -> p h d", h=4)
        rB = rp[:, 32:64].rearrange("p (h d) -> p h d", h=4)
        sA = sq[:, 640:672].rearrange("p (h d) -> p h d", h=4)
        sB = sq[:, 672:704].rearrange("p (h d) -> p h d", h=4)
        P.op("vector", lambda e, qA=qA, cosb4=cosb4, rA=rA: e.tensor_tensor(
            out=rA, in0=qA, in1=cosb4, op=ALU.mult))
        P.op("vector", lambda e, qB=qB, sinb4=sinb4, rB=rB: e.tensor_tensor(
            out=rB, in0=qB, in1=sinb4, op=ALU.mult))
        P.op("vector", lambda e, qA=qA, sinb4=sinb4, sA=sA: e.tensor_tensor(
            out=sA, in0=qA, in1=sinb4, op=ALU.mult))
        P.op("vector", lambda e, qB=qB, cosb4=cosb4, sB=sB: e.tensor_tensor(
            out=sB, in0=qB, in1=cosb4, op=ALU.mult))
        P.op("vector", lambda e, kA=kA, cosb=cosb: e.tensor_tensor(
            out=rp[:, 64:72], in0=kA, in1=cosb, op=ALU.mult))
        P.op("vector", lambda e, kB=kB, sinb=sinb: e.tensor_tensor(
            out=rp[:, 72:80], in0=kB, in1=sinb, op=ALU.mult))
        P.op("vector", lambda e, kA=kA, sinb=sinb: e.tensor_tensor(
            out=rp[:, 80:88], in0=kA, in1=sinb, op=ALU.mult))
        P.op("vector", lambda e, kB=kB, cosb=cosb: e.tensor_tensor(
            out=rp[:, 88:96], in0=kB, in1=cosb, op=ALU.mult), inc=s_dv)
        chain("vector", s_dv)
        P.op("vector", lambda e, qA=qA, rA=rA, rB=rB: e.tensor_tensor(
            out=qA, in0=rA, in1=rB, op=ALU.add))
        P.op("vector", lambda e, qB=qB, sA=sA, sB=sB: e.tensor_tensor(
            out=qB, in0=sA, in1=sB, op=ALU.subtract))
        P.op("vector", lambda e, kA=kA: e.tensor_tensor(
            out=kA, in0=rp[:, 64:72], in1=rp[:, 72:80], op=ALU.add))
        P.op("vector", lambda e, kB=kB: e.tensor_tensor(
            out=kB, in0=rp[:, 80:88], in1=rp[:, 88:96], op=ALU.subtract),
            inc=s_dv)
        chain("vector", s_dv)
        done_dv_sq[0] = s_dv.n

    if upto == 1:
        dbg_dump("QTM", QTM[:])
        dbg_dump("KVTM", KVTM[:])

    if upto == 2:
        dbg_dump("QTM", QTM[:])
        dbg_dump("KVTM", KVTM[:])
    v_p2_done = s_dv.n

    P.label = "P3"
    if upto >= 3:
        P.wait("tensor", s_dv, v_p2_done)
        strip_free = [0] * 4
        strips = []
        for ttg in range(8):
            for h in range(NHL):
                strips.append(("q", ttg, h))
            strips.append(("k", ttg, 0))
        for sidx, (kind, ttg, h) in enumerate(strips):
            slot = sidx % 4
            bank = psT[slot // 2]
            coff = (slot % 2) * 1024
            if strip_free[slot]:
                P.wait("tensor", s_dv, strip_free[slot])
            for k in range(4):
                tt = ttg * 4 + k
                if kind == "q":
                    src = QTM[:, tt * QO + h * 128: tt * QO + (h + 1) * 128]
                else:
                    src = KVTM[:, tt * KVO: tt * KVO + 128]
                P.op("tensor", lambda e, b=bank, c=coff, k=k, s=src: e.transpose(
                    b[:, c + k * 128:c + (k + 1) * 128], s, ident[:]),
                    inc=s_pe if k == 3 else None)
            pe_s = s_pe.n
            P.wait("vector", s_pe, pe_s)
            if kind == "q":
                dst = QFM[:, h * T + ttg * 512: h * T + (ttg + 1) * 512]
            else:
                dst = KFM[:, ttg * 512:(ttg + 1) * 512]
            P.op("vector", lambda e, d=dst, b=bank, c=coff: e.tensor_copy(
                d, b[:, c:c + 512]), inc=s_dv)
            strip_free[slot] = s_dv.n
    v_p3_done = s_dv.n
    pe_p3 = s_pe.n
    if upto == 3:
        dbg_dump("QFM", QFM[:])
        dbg_dump("KFM", KFM[:])

    A.free("z1f1", "z1f0", "z1tm1", "z1tm0", "sq21", "sq20",
           "xbuf1", "xbuf0", "QTM", "kv_sb", "wq_sb")
    A.free("psKV1", "psKV0", "psQ1", "psQ0", "psT1", "psT0")

    wp_sb = A.sbuf("wp_sb", [128, 4 * D], BF16)
    Pst = [A.sbuf(f"Pst{i}", [128, 2048], BF16) for i in range(2)]
    PT = [A.sbuf(f"PT{i}", [128, 1024], BF16) for i in range(2)]
    P.label = "P4"
    if upto >= 4:
        psST = [A.psum(f"psST{i}", [128, 1024], F32) for i in range(2)]
        psY = [A.psum(f"psY{i}", [128, 512], F32) for i in range(2)]
        psl = [A.psum(f"psl{i}", [128, 512], F32) for i in range(2)]

        P.wait("tensor", s_dv, v_p3_done)
        done_sc_exp = [0, 0]
        done_pe_pt = [0, 0]
        done_dv_y = [0, 0]
        uidx = 0
        for h in range(NHL):
            for qc in range(8):
                n_st = 4 * (qc + 1)
                npair = n_st // 2
                par = uidx % 2
                qsl = QFM[:, h * T + qc * 512: h * T + (qc + 1) * 512]
                pair_sc = [0] * npair
                pair_dv = [0] * npair

                def emit_scores(p):
                    slp = p % 2
                    if done_sc_exp[slp]:
                        P.wait("tensor", s_sc, done_sc_exp[slp])
                    for z in range(2):
                        st = p * 2 + z
                        P.op("tensor", lambda e, st=st, z=z, slp=slp, q=qsl: e.matmul(
                            psST[slp][:, z * 512:(z + 1) * 512],
                            KFM[:, st * 128:(st + 1) * 128], q,
                            start=True, stop=True),
                            inc=s_pe if z == 1 else None)
                    return s_pe.n

                def emit_exp_mask(p, pe_val):
                    slp = p % 2
                    P.wait("scalar", s_pe, pe_val)
                    if done_pe_pt[slp]:
                        P.wait("scalar", s_pe, done_pe_pt[slp])
                    P.op("scalar", lambda e, slp=slp, h=h: e.activation(
                        out=PT[slp][:], in_=psST[slp][:], func=ACT.Exp,
                        bias=bcast16[:, 8 + h:9 + h],
                        scale=bcast16[:, 12 + h:13 + h]),
                        inc=s_sc)
                    done_sc_exp[slp] = s_sc.n
                    pair_sc[p] = s_sc.n
                    dvv = 0
                    for z in range(2):
                        st = p * 2 + z
                        j = st - 4 * qc
                        if 0 <= j < 4:
                            P.wait("vector", s_sc, pair_sc[p])
                            P.op("vector", lambda e, slp=slp, z=z, j=j: e.tensor_tensor(
                                out=PT[slp][:, z * 512:(z + 1) * 512],
                                in0=PT[slp][:, z * 512:(z + 1) * 512],
                                in1=masks[:, j * 512:(j + 1) * 512],
                                op=ALU.mult), inc=s_dv)
                            dvv = s_dv.n
                    pair_dv[p] = dvv

                def emit_lpv(p):
                    slp = p % 2
                    P.wait("tensor", s_sc, pair_sc[p])
                    if pair_dv[p]:
                        P.wait("tensor", s_dv, pair_dv[p])
                    for z in range(2):
                        st = p * 2 + z
                        first = st == 0
                        last = st == n_st - 1
                        P.op("tensor", lambda e, slp=slp, z=z, par=par,
                             first=first, last=last: e.matmul(
                                 psl[par][0:1, :], onesb[:, 0:1],
                                 PT[slp][:, z * 512:(z + 1) * 512],
                                 start=first, stop=last), inc=None)
                        P.op("tensor", lambda e, slp=slp, z=z, st=st, par=par,
                             first=first, last=last: e.matmul(
                                 psY[par][:],
                                 KVTM[:, st * KVO + 128:(st + 1) * KVO],
                                 PT[slp][:, z * 512:(z + 1) * 512],
                                 start=first, stop=last),
                             inc=s_pe if z == 1 else None)
                    done_pe_pt[slp] = s_pe.n
                    return s_pe.n

                if done_dv_y[par]:
                    P.wait("tensor", s_dv, done_dv_y[par])
                pev = emit_scores(0)
                emit_exp_mask(0, pev)
                for p in range(1, npair):
                    pev = emit_scores(p)
                    emit_exp_mask(p, pev)
                    emit_lpv(p - 1)
                pe_last = emit_lpv(npair - 1)

                P.wait("vector", s_pe, pe_last)
                P.op("vector", lambda e, par=par: e.tensor_copy(
                    lrow[:], psl[par][0:1, :]), inc=s_dv)
                chain("vector", s_dv)
                P.op("vector", lambda e: e.reciprocal(
                    out=rrow[:], in_=lrow[:]), inc=s_dv)
                v_r = s_dv.n
                P.wait("tensor", s_dv, v_r)
                pe_b = P.op("tensor", lambda e, par=par: e.matmul(
                    psl[par][:], onesf32[0:1, :], rrow[0:1, :],
                    start=True, stop=True), inc=s_pe)
                P.wait("vector", s_pe, pe_b)
                P.op("vector", lambda e, par=par: e.tensor_copy(
                    rbc[:], psl[par][:]), inc=s_dv)
                chain("vector", s_dv)
                P.op("vector", lambda e, h=h, qc=qc, par=par: e.scalar_tensor_tensor(
                    out=YFM[:, h * T + qc * 512:h * T + (qc + 1) * 512],
                    in0=psY[par][:], scalar=1.0, in1=rbc[:],
                    op0=ALU.mult, op1=ALU.mult), inc=s_dv)
                done_dv_y[par] = s_dv.n
                uidx += 1
        A.free("psl1", "psl0", "psY1", "psY0", "psST1", "psST0")
    v_p4_done = s_dv.n
    if upto == 4:
        dbg_dump("YFM", YFM[:])
    A.free("PT1", "PT0")
    A.free("sinT", "cosT", "masks")

    P.label = "P5"
    if upto >= 5:
        psW = [A.psum(f"psW{i}", [128, 512], F32) for i in range(2)]
        P.wait("gpsimd", s_pe, pe_p3)
        P.op("gpsimd", lambda e: e.dma_start(
            out=wp_sb[:], in_=d_wp.rearrange("p a b -> p (a b)")),
            inc=s_gx[2], k=16)
        g_wp = s_gx[2].n
        P.wait("tensor", s_gx[2], g_wp)
        P.wait("tensor", s_dv, v_p4_done)
        done_dv_w = [0, 0]
        done_pst_dma = [0, 0]
        chunk_ready = {}
        tt_order = [8 * r + 2 * k + z for k in range(4) for r in range(4)
                    for z in range(2)]
        for ti, tt in enumerate(tt_order):
            sl = ti % 2
            if done_pst_dma[sl]:
                P.wait("vector", s_sx[2 + sl], done_pst_dma[sl])
            for hoq in range(4):
                wsl = (tt * 4 + hoq) % 2
                if done_dv_w[wsl]:
                    P.wait("tensor", s_dv, done_dv_w[wsl])
                for dc in range(4):
                    P.op("tensor", lambda e, tt=tt, hoq=hoq, dc=dc, wsl=wsl: e.matmul(
                        psW[wsl][:],
                        YFM[:, dc * T + tt * 128: dc * T + (tt + 1) * 128],
                        wp_sb[:, dc * D + hoq * 512: dc * D + (hoq + 1) * 512],
                        start=(dc == 0), stop=(dc == 3)),
                        inc=s_pe if dc == 3 else None)
                pe_w = s_pe.n
                P.wait("vector", s_pe, pe_w)
                P.op("vector", lambda e, sl=sl, hoq=hoq, wsl=wsl: e.tensor_copy(
                    Pst[sl][:, hoq * 512:(hoq + 1) * 512], psW[wsl][:]),
                    inc=s_dv)
                done_dv_w[wsl] = s_dv.n
            v_tile = s_dv.n
            P.wait("sync", s_dv, v_tile)
            kq, rq, zq = (tt % 8) // 2, tt // 8, tt % 2
            P.op("sync", lambda e, kq=kq, rq=rq, zq=zq, sl=sl: e.dma_start(
                out=d_prs[kq, rq, zq * 128:(zq + 1) * 128, :], in_=Pst[sl][:]),
                inc=s_sx[2 + sl], k=16)
            done_pst_dma[sl] = s_sx[2 + sl].n
            if ti % 8 == 7:
                chunk_ready[ti // 8] = (s_sx[2].n, s_sx[3].n)
        A.free("psW1", "psW0")
    sx2_prs, sx3_prs = s_sx[2].n, s_sx[3].n
    A.free("Pst1", "Pst0", "wp_sb", "kv_sb_dummy") if False else None
    A.free("Pst1", "Pst0", "wp_sb")
    A.free("KFM", "QFM", "KVTM", "YFM")

    rw_sb = A.sbuf("rw_sb", [128, NI * NE], BF16, side="right")
    sgu_sb = [A.sbuf(f"sgu_sb{i}", [128, 2 * NI * 128], BF16, side="right") for i in range(2)]
    sd_sb = [A.sbuf(f"sd_sb{i}", [128, NIO * 512], BF16, side="right") for i in range(2)]
    sgu_ready = {}
    sd_ready = {}
    cc_done = 0
    P.label = "PX"
    if upto >= 5:
        P.wait("gpsimd", s_dv, v_p4_done)
        P.op("gpsimd", lambda e: e.dma_start(
            out=rw_sb[:], in_=d_rw.rearrange("p a b -> p (a b)")),
            inc=s_gx[3], k=16)
        g_rw = s_gx[3].n
        P.op("gpsimd", lambda e: e.dma_start(
            out=sgu_sb[0][:, 0:NI * 128],
            in_=d_sg[0].rearrange("p a b -> p (a b)")), inc=s_gx[4], k=16)
        P.op("gpsimd", lambda e: e.dma_start(
            out=sgu_sb[0][:, NI * 128:],
            in_=d_su[0].rearrange("p a b -> p (a b)")), inc=s_gx[4], k=16)
        sgu_ready[0] = (4, s_gx[4].n)
        P.op("gpsimd", lambda e: e.dma_start(
            out=sd_sb[0][:], in_=d_sd[0].rearrange("p a b -> p (a b)")),
            inc=s_gx[5], k=16)
        sd_ready[0] = (5, s_gx[5].n)
        cc_chunk = {}
        for k in range(4):
            P.wait("gpsimd", s_sx[2], chunk_ready[k][0])
            P.wait("gpsimd", s_sx[3], chunk_ready[k][1])
            cc_chunk[k] = P.op("gpsimd", lambda e, k=k: e.collective_compute(
                "ReduceScatter", ALU.add,
                replica_groups=[[0, 1, 2, 3], [4, 5, 6, 7]],
                ins=[d_prs[k]],
                outs=[d_pq[k * 256:(k + 1) * 256, :]]), inc=s_cc)
        cc_done = s_cc.n

    X2 = A.sbuf("X2", [128, NQT * 2048], F32)
    z2f = A.sbuf("z2f", [128, NI * 1024], BF16)
    P.label = "P6"
    if upto >= 6:
        z2tm = [A.sbuf(f"z2tm{i}", [128, 2048], BF16) for i in range(2)]
        sqscr6 = A.sbuf("sqscr6", [128, 2048], F32)
        psT2 = [A.psum(f"psT2{i}", [128, 2048], BF16) for i in range(2)]
        done_pe_T2 = [0, 0]
        done_dv_z2 = [0, 0]
        done_accum = [0, 0]
        ln_guard = [0]
        P.wait("sync", s_sx[2], sx2_prs)
        P.wait("sync", s_sx[3], sx3_prs)
        P.wait("sync", s_pe, s_pe.n)
        for tt in range(NQT):
            sl = tt % 2
            X2t = X2[:, tt * 2048:(tt + 1) * 2048]
            if done_accum[sl]:
                P.wait("sync", s_gx[sl], done_accum[sl])
            P.op("sync", lambda e, tt=tt: e.dma_start(
                out=X2[:, tt * 2048:(tt + 1) * 2048],
                in_=d_xo[tt * 128:(tt + 1) * 128, :]), inc=s_sx[sl], k=16)
            sxv = s_sx[sl].n
            P.wait("gpsimd", s_sx[sl], sxv)
            P.wait("gpsimd", s_cc, cc_chunk[tt // 2])
            P.op("gpsimd", lambda e, tt=tt: e.dma_start(
                out=X2[:, tt * 2048:(tt + 1) * 2048],
                in_=d_pq[tt * 128:(tt + 1) * 128, :],
                accum_op=ALU.add), inc=s_gx[sl], k=16)
            done_accum[sl] = s_gx[sl].n
            P.wait("vector", s_gx[sl], done_accum[sl])
            P.wait("scalar", s_gx[sl], done_accum[sl])
            if done_pe_T2[sl]:
                P.wait("vector", s_pe, done_pe_T2[sl])
                P.wait("scalar", s_pe, done_pe_T2[sl])
            v_apply = emit_ln(tt, sl, X2t, z2tm[sl][:],
                              0 if sl == 0 else 4, z2tm[sl][:],
                              prev_dv=ln_guard[0])
            c0 = 0 if sl == 0 else 4
            chain("vector", s_dv)
            P.op("vector", lambda e, X2t=X2t, c=c0: e.tensor_scalar(
                out=sqscr6[:, 0:2048], in0=X2t, scalar1=srow[:, c:c + 1],
                scalar2=srow[:, c + 2:c + 3], op0=ALU.add, op1=ALU.mult),
                inc=s_dv)
            chain("vector", s_dv)
            P.op("vector", lambda e, X2t=X2t: e.tensor_tensor(
                out=X2t, in0=X2t, in1=sqscr6[:, 0:2048], op=ALU.add), inc=s_dv)
            ln_guard[0] = s_dv.n
            P.wait("tensor", s_dv, v_apply)
            if done_dv_z2[sl]:
                P.wait("tensor", s_dv, done_dv_z2[sl])
            for ic in range(NI):
                P.op("tensor", lambda e, ic=ic, sl=sl: e.transpose(
                    psT2[sl][:, ic * 128:(ic + 1) * 128],
                    z2tm[sl][:, ic * 128:(ic + 1) * 128], ident[:]),
                    inc=s_pe if ic == NI - 1 else None)
            done_pe_T2[sl] = s_pe.n
            P.wait("vector", s_pe, done_pe_T2[sl])
            zv = z2f[:].rearrange("p (a b) -> p a b", a=NI)[:, :, tt * 128:(tt + 1) * 128]
            P.op("vector", lambda e, sl=sl, zv=zv: e.tensor_copy(
                zv, psT2[sl][:].rearrange("p (a b) -> p a b", a=NI)), inc=s_dv)
            done_dv_z2[sl] = s_dv.n
        A.free("sqscr6")
        A.free("z2tm1", "z2tm0")
        A.free("psT21", "psT20")
    v_p6_done = s_dv.n
    if upto == 6:
        dbg_dump("X2", X2[:])
        dbg_dump("z2f", z2f[:])

    done_sgu_use_g = [0, 0]
    done_sd_use_g = [0, 0]
    HSH = A.sbuf("HSH", [128, NIO * 1024], BF16)
    P.label = "P7"
    if upto >= 7:
        psR = [A.psum(f"psR{i}", [128, 512], F32) for i in range(2)]
        psG = [A.psum(f"psG{i}", [128, 512], F32) for i in range(2)]
        psU = [A.psum(f"psU{i}", [128, 512], F32) for i in range(2)]
        psD = [A.psum(f"psD{i}", [128, 512], F32) for i in range(2)]
        P.wait("tensor", s_dv, v_p6_done)
        P.wait("tensor", s_gx[3], g_rw)
        done_dv_r = [0, 0]
        for tt in range(NQT):
            rsl = tt % 2
            if done_dv_r[rsl]:
                P.wait("tensor", s_dv, done_dv_r[rsl])
            for hc in range(NI):
                P.op("tensor", lambda e, tt=tt, hc=hc, rsl=rsl: e.matmul(
                    psR[rsl][:, 0:8],
                    z2f[:, hc * 1024 + tt * 128: hc * 1024 + (tt + 1) * 128],
                    rw_sb[:, hc * 8:(hc + 1) * 8],
                    start=(hc == 0), stop=(hc == NI - 1)),
                    inc=s_pe if hc == NI - 1 else None)
            pe_r = s_pe.n
            P.wait("vector", s_pe, pe_r)
            P.op("vector", lambda e, tt=tt, rsl=rsl: e.tensor_copy(
                LG[:, tt * 8:(tt + 1) * 8], psR[rsl][:, 0:8]), inc=s_dv)
            done_dv_r[rsl] = s_dv.n
        chain("vector", s_dv)
        LG3 = LG[:].rearrange("p (a b) -> p a b", a=8)
        CW3 = combW[:].rearrange("p (a b) -> p a b", a=8)
        Rr = lambda c0: rscr[:, c0 * 8:c0 * 8 + 8]
        R3 = lambda c0: rscr[:, 8 + c0 * 64:8 + c0 * 64 + 64].rearrange(
            "p (a b) -> p a b", a=8)
        bc = lambda ap: ap.unsqueeze(2).broadcast_to([128, 8, 8])
        P.op("vector", lambda e: e.tensor_reduce(
            out=Rr(0), in_=LG3, axis=AX.X, op=ALU.max), inc=s_dv)
        chain("vector", s_dv)
        P.op("vector", lambda e: e.tensor_tensor(
            out=R3(0), in0=LG3, in1=bc(Rr(0)), op=ALU.subtract), inc=s_dv)
        v_sub = s_dv.n
        P.wait("scalar", s_dv, v_sub)
        P.op("scalar", lambda e: e.activation(
            out=rscr[:, 8:72], in_=rscr[:, 8:72], func=ACT.Exp), inc=s_sc)
        sc_e = s_sc.n
        P.wait("vector", s_sc, sc_e)
        P.op("vector", lambda e: e.tensor_reduce(
            out=Rr(0), in_=R3(0), axis=AX.X, op=ALU.add), inc=s_dv)
        chain("vector", s_dv)
        P.op("vector", lambda e: e.reciprocal(out=Rr(0), in_=Rr(0)), inc=s_dv)
        chain("vector", s_dv)
        P.op("vector", lambda e: e.tensor_tensor(
            out=R3(0), in0=R3(0), in1=bc(Rr(0)), op=ALU.mult), inc=s_dv)
        chain("vector", s_dv)
        P.op("vector", lambda e: e.tensor_reduce(
            out=Rr(0), in_=R3(0), axis=AX.X, op=ALU.max), inc=s_dv)
        chain("vector", s_dv)
        P.op("vector", lambda e: e.tensor_tensor(
            out=R3(1), in0=R3(0), in1=bc(Rr(0)), op=ALU.is_ge), inc=s_dv)
        chain("vector", s_dv)
        P.op("vector", lambda e: e.tensor_scalar(
            out=R3(2), in0=R3(1), scalar1=-1.0, scalar2=1.0,
            op0=ALU.mult, op1=ALU.add), inc=s_dv)
        chain("vector", s_dv)
        P.op("vector", lambda e: e.tensor_tensor(
            out=R3(3), in0=R3(0), in1=R3(2), op=ALU.mult), inc=s_dv)
        chain("vector", s_dv)
        P.op("vector", lambda e: e.tensor_reduce(
            out=Rr(0), in_=R3(3), axis=AX.X, op=ALU.max), inc=s_dv)
        chain("vector", s_dv)
        P.op("vector", lambda e: e.tensor_tensor(
            out=R3(4), in0=R3(3), in1=bc(Rr(0)), op=ALU.is_ge), inc=s_dv)
        chain("vector", s_dv)
        P.op("vector", lambda e: e.tensor_tensor(
            out=R3(1), in0=R3(1), in1=R3(4), op=ALU.add), inc=s_dv)
        chain("vector", s_dv)
        P.op("vector", lambda e: e.tensor_tensor(
            out=CW3, in0=R3(0), in1=R3(1), op=ALU.mult), inc=s_dv)
        v_comb = s_dv.n
        if "combW" in d_dbg:
            dbg_dump("combW", combW[:])
            dbg_dump("LG", LG[:])

        done_sgu_use_g[0] = done_sgu_use_g[1] = 0
        done_sgu_use = done_sgu_use_g
        done_gu_cons = [[0, 0], [0, 0]]
        for c in range(NIO):
            sl = c % 2
            if c + 1 < NIO:
                osl = (c + 1) % 2
                if done_sgu_use[osl]:
                    P.wait("gpsimd", s_pe, done_sgu_use[osl])
                gsem = 4 if osl == 0 else 6
                P.op("gpsimd", lambda e, c=c, osl=osl: e.dma_start(
                    out=sgu_sb[osl][:, 0:NI * 128],
                    in_=d_sg[c + 1].rearrange("p a b -> p (a b)")),
                    inc=s_gx[gsem], k=16)
                P.op("gpsimd", lambda e, c=c, osl=osl: e.dma_start(
                    out=sgu_sb[osl][:, NI * 128:],
                    in_=d_su[c + 1].rearrange("p a b -> p (a b)")),
                    inc=s_gx[gsem], k=16)
                sgu_ready[c + 1] = (gsem, s_gx[gsem].n)
            P.wait("tensor", s_gx[sgu_ready[c][0]], sgu_ready[c][1])
            for half in range(2):
                if done_gu_cons[half][0]:
                    P.wait("tensor", s_sc, done_gu_cons[half][0])
                    P.wait("tensor", s_dv, done_gu_cons[half][1])
                for hc in range(NI):
                    P.op("tensor", lambda e, sl=sl, hc=hc, half=half: e.matmul(
                        psG[half][:],
                        sgu_sb[sl][:, hc * 128:(hc + 1) * 128],
                        z2f[:, hc * 1024 + half * 512: hc * 1024 + half * 512 + 512],
                        start=(hc == 0), stop=(hc == NI - 1)), inc=None)
                    P.op("tensor", lambda e, sl=sl, hc=hc, half=half: e.matmul(
                        psU[half][:],
                        sgu_sb[sl][:, NI * 128 + hc * 128: NI * 128 + (hc + 1) * 128],
                        z2f[:, hc * 1024 + half * 512: hc * 1024 + half * 512 + 512],
                        start=(hc == 0), stop=(hc == NI - 1)),
                        inc=s_pe if hc == NI - 1 else None)
                pe_gu = s_pe.n
                P.wait("scalar", s_pe, pe_gu)
                P.op("scalar", lambda e, half=half: e.activation(
                    out=sgb[:, half * 512:(half + 1) * 512], in_=psG[half][:],
                    func=ACT.Sigmoid), inc=s_sc)
                sc_sg = s_sc.n
                P.wait("vector", s_sc, sc_sg)
                P.op("vector", lambda e, c=c, half=half: e.scalar_tensor_tensor(
                    out=HSH[:, c * 1024 + half * 512: c * 1024 + half * 512 + 512],
                    in0=psU[half][:], scalar=1.0,
                    in1=sgb[:, half * 512:(half + 1) * 512],
                    op0=ALU.mult, op1=ALU.mult), inc=s_dv)
                done_gu_cons[half] = [sc_sg, s_dv.n]
            done_sgu_use[sl] = s_pe.n
        v_hsh = s_dv.n

        done_dv_d = [0, 0]
        done_sd_use_g[0] = done_sd_use_g[1] = 0
        done_sd_use = done_sd_use_g
        for hoq in range(4):
            sl = hoq % 2
            if hoq + 1 < 4:
                osl = (hoq + 1) % 2
                if done_sd_use[osl]:
                    P.wait("gpsimd", s_pe, done_sd_use[osl])
                dsem = 5 if osl == 0 else 7
                P.op("gpsimd", lambda e, hoq=hoq, osl=osl: e.dma_start(
                    out=sd_sb[osl][:],
                    in_=d_sd[hoq + 1].rearrange("p a b -> p (a b)")),
                    inc=s_gx[dsem], k=16)
                sd_ready[hoq + 1] = (dsem, s_gx[dsem].n)
            P.wait("tensor", s_gx[sd_ready[hoq][0]], sd_ready[hoq][1])
            P.wait("tensor", s_dv, v_hsh)
            for tt in range(NQT):
                dsl = (hoq * NQT + tt) % 2
                if done_dv_d[dsl]:
                    P.wait("tensor", s_dv, done_dv_d[dsl])
                for c in range(NIO):
                    P.op("tensor", lambda e, c=c, tt=tt, sl=sl, dsl=dsl: e.matmul(
                        psD[dsl][:],
                        HSH[:, c * 1024 + tt * 128: c * 1024 + (tt + 1) * 128],
                        sd_sb[sl][:, c * 512:(c + 1) * 512],
                        start=(c == 0), stop=(c == NIO - 1)),
                        inc=s_pe if c == NIO - 1 else None)
                pe_d = s_pe.n
                done_sd_use[sl] = pe_d
                P.wait("vector", s_pe, pe_d)
                P.op("vector", lambda e, tt=tt, hoq=hoq, dsl=dsl: e.tensor_tensor(
                    out=X2[:, tt * 2048 + hoq * 512: tt * 2048 + (hoq + 1) * 512],
                    in0=X2[:, tt * 2048 + hoq * 512: tt * 2048 + (hoq + 1) * 512],
                    in1=psD[dsl][:], op=ALU.add), inc=s_dv)
                done_dv_d[dsl] = s_dv.n
    v_p7_done = s_dv.n
    if upto == 7:
        dbg_dump("X2", X2[:])
    A.free("HSH")
    A.free("sd_sb1", "sd_sb0", "sgu_sb1", "sgu_sb0")

    P.label = "P8"
    if upto >= 8:
        HE = [A.sbuf(f"HE{i}", [128, NIO * 1024], BF16) for i in range(2)]
        wgu_sb = [A.sbuf(f"wgu_sb{i}", [128, 2 * NI * 128], BF16) for i in range(2)]
        wd_sb = [A.sbuf(f"wd_sb{i}", [128, NIO * 512], BF16) for i in range(2)]
        wgu_ready = {}
        wd_ready = {}
        done_wgu_use = list(done_sgu_use_g)
        done_wd_use = list(done_sd_use_g)
        done_he_read = [0, 0]
        done_gu_cons2 = [[0, 0], [0, 0]]
        done_dv_d2 = [0, 0]

        def prefetch_gu(idx):
            if idx >= NE * NIO:
                return
            ne_, nc_ = divmod(idx, NIO)
            osl = idx % 2
            if done_wgu_use[osl]:
                P.wait("gpsimd", s_pe, done_wgu_use[osl])
                done_wgu_use[osl] = 0
            gsem = 4 if osl == 0 else 6
            P.op("gpsimd", lambda e, ne_=ne_, nc_=nc_, osl=osl: e.dma_start(
                out=wgu_sb[osl][:, 0:NI * 128],
                in_=d_wg[ne_, nc_].rearrange("p a b -> p (a b)")),
                inc=s_gx[gsem], k=16)
            P.op("gpsimd", lambda e, ne_=ne_, nc_=nc_, osl=osl: e.dma_start(
                out=wgu_sb[osl][:, NI * 128:],
                in_=d_wu[ne_, nc_].rearrange("p a b -> p (a b)")),
                inc=s_gx[gsem], k=16)
            wgu_ready[idx] = (gsem, s_gx[gsem].n)

        def prefetch_wd(idx):
            if idx >= NE * 4:
                return
            ne_, nq_ = divmod(idx, 4)
            osl = idx % 2
            if done_wd_use[osl]:
                P.wait("gpsimd", s_pe, done_wd_use[osl])
                done_wd_use[osl] = 0
            dsem = 5 if osl == 0 else 7
            P.op("gpsimd", lambda e, ne_=ne_, nq_=nq_, osl=osl: e.dma_start(
                out=wd_sb[osl][:],
                in_=d_wd[ne_, nq_].rearrange("p a b -> p (a b)")),
                inc=s_gx[dsem], k=16)
            wd_ready[idx] = (dsem, s_gx[dsem].n)

        P.wait("gpsimd", s_pe, s_pe.n)
        P.wait("gpsimd", s_dv, s_dv.n)
        prefetch_gu(0)
        prefetch_wd(0)
        for e_i in range(NE):
            esl = e_i % 2
            if done_he_read[esl]:
                P.wait("vector", s_pe, done_he_read[esl])
            for c in range(NIO):
                gidx = e_i * NIO + c
                sl = gidx % 2
                prefetch_gu(gidx + 1)
                P.wait("tensor", s_gx[wgu_ready[gidx][0]], wgu_ready[gidx][1])
                for half in range(2):
                    if done_gu_cons2[half][0]:
                        P.wait("tensor", s_sc, done_gu_cons2[half][0])
                        P.wait("tensor", s_dv, done_gu_cons2[half][1])
                    for hc in range(NI):
                        P.op("tensor", lambda e, sl=sl, hc=hc, half=half: e.matmul(
                            psG[half][:],
                            wgu_sb[sl][:, hc * 128:(hc + 1) * 128],
                            z2f[:, hc * 1024 + half * 512: hc * 1024 + half * 512 + 512],
                            start=(hc == 0), stop=(hc == NI - 1)), inc=None)
                        P.op("tensor", lambda e, sl=sl, hc=hc, half=half: e.matmul(
                            psU[half][:],
                            wgu_sb[sl][:, NI * 128 + hc * 128: NI * 128 + (hc + 1) * 128],
                            z2f[:, hc * 1024 + half * 512: hc * 1024 + half * 512 + 512],
                            start=(hc == 0), stop=(hc == NI - 1)),
                            inc=s_pe if hc == NI - 1 else None)
                    pe_gu = s_pe.n
                    P.wait("scalar", s_pe, pe_gu)
                    P.op("scalar", lambda e, half=half: e.activation(
                        out=sgb[:, half * 512:(half + 1) * 512],
                        in_=psG[half][:], func=ACT.Sigmoid), inc=s_sc)
                    sc_sg = s_sc.n
                    P.wait("vector", s_sc, sc_sg)
                    P.op("vector", lambda e, c=c, half=half, esl=esl: e.scalar_tensor_tensor(
                        out=HE[esl][:, c * 1024 + half * 512: c * 1024 + half * 512 + 512],
                        in0=psU[half][:], scalar=1.0,
                        in1=sgb[:, half * 512:(half + 1) * 512],
                        op0=ALU.mult, op1=ALU.mult), inc=s_dv)
                    done_gu_cons2[half] = [sc_sg, s_dv.n]
                done_wgu_use[sl] = s_pe.n
            v_he = s_dv.n
            for hoq in range(4):
                didx = e_i * 4 + hoq
                sl = didx % 2
                prefetch_wd(didx + 1)
                P.wait("tensor", s_gx[wd_ready[didx][0]], wd_ready[didx][1])
                P.wait("tensor", s_dv, v_he)
                for tt in range(NQT):
                    dsl = tt % 2
                    if done_dv_d2[dsl]:
                        P.wait("tensor", s_dv, done_dv_d2[dsl])
                    for c in range(NIO):
                        P.op("tensor", lambda e, c=c, tt=tt, sl=sl, dsl=dsl,
                             esl=esl: e.matmul(
                                 psD[dsl][:],
                                 HE[esl][:, c * 1024 + tt * 128: c * 1024 + (tt + 1) * 128],
                                 wd_sb[sl][:, c * 512:(c + 1) * 512],
                                 start=(c == 0), stop=(c == NIO - 1)),
                             inc=s_pe if c == NIO - 1 else None)
                    pe_d = s_pe.n
                    done_wd_use[sl] = pe_d
                    P.wait("vector", s_pe, pe_d)
                    P.op("vector", lambda e, tt=tt, hoq=hoq, dsl=dsl, e_i=e_i: e.scalar_tensor_tensor(
                        out=X2[:, tt * 2048 + hoq * 512: tt * 2048 + (hoq + 1) * 512],
                        in0=psD[dsl][:],
                        scalar=combW[:, tt * 8 + e_i: tt * 8 + e_i + 1],
                        in1=X2[:, tt * 2048 + hoq * 512: tt * 2048 + (hoq + 1) * 512],
                        op0=ALU.mult, op1=ALU.add), inc=s_dv)
                    done_dv_d2[dsl] = s_dv.n
                done_he_read[esl] = s_pe.n

    if upto >= 8:
        v_final = s_dv.n
        P.wait("sync", s_dv, v_final)
        P.op("sync", lambda e: e.dma_start(
            out=d_out.rearrange("(a p) d -> p a d", p=128),
            in_=X2[:].rearrange("p (a d) -> p a d", a=NQT)), inc=s_out, k=16)
    if s_out.n:
        P.wait("sync", s_out, s_out.n)

    block = es.enter_context(nc.Block())
    thunks = P.thunks

    def replay(eng_name, e):
        cur = None
        sid = None
        for label, th in thunks[eng_name]:
            if label != cur:
                if cur is not None:
                    nc.leave_named_scope(f"{cur}.{eng_name}", sid, False)
                cur = label
                sid, _ = nc.enter_named_scope(f"{cur}.{eng_name}", False)
            th(e)
        if cur is not None:
            nc.leave_named_scope(f"{cur}.{eng_name}", sid, False)

    @block.sync
    def _(e):
        replay("sync", e)

    @block.scalar
    def _(e):
        replay("scalar", e)

    @block.vector
    def _(e):
        replay("vector", e)

    @block.tensor
    def _(e):
        replay("tensor", e)

    @block.gpsimd
    def _(e):
        replay("gpsimd", e)

    es.close()
    A.close()
    return nc


_NC_CACHE = {}


def _get_nc(upto=99, debug=()):
    key = (upto, tuple(debug))
    if key not in _NC_CACHE:
        _NC_CACHE[key] = _build_program(upto=upto, debug=debug)
    return _NC_CACHE[key]


def _host_layout(inputs):
    x = np.ascontiguousarray(np.asarray(inputs["L0_attn_x"]))
    wq = np.asarray(inputs["L0_attn_wq"])
    wk = np.asarray(inputs["L0_attn_wk"])
    wv = np.asarray(inputs["L0_attn_wv"])
    wp = np.asarray(inputs["L0_attn_wp"])
    qg = np.asarray(inputs["L0_attn_qg"])
    rw = np.asarray(inputs["L0_ffn_router_w"])
    sg = np.asarray(inputs["L0_ffn_shared_gate"])
    su = np.asarray(inputs["L0_ffn_shared_up"])
    sd = np.asarray(inputs["L0_ffn_shared_down"])
    wg = np.asarray(inputs["L0_ffn_w_gate"])
    wu = np.asarray(inputs["L0_ffn_w_up"])
    wd = np.asarray(inputs["L0_ffn_w_down"])

    rw_r = np.ascontiguousarray(rw.reshape(NE, NI, 128).transpose(2, 1, 0))
    sg_r = np.ascontiguousarray(sg.reshape(NIO, 128, NI, 128).transpose(0, 3, 2, 1))
    su_r = np.ascontiguousarray(su.reshape(NIO, 128, NI, 128).transpose(0, 3, 2, 1))
    sd_r = np.ascontiguousarray(sd.reshape(4, 512, NIO, 128).transpose(0, 3, 2, 1))
    wg_r = np.ascontiguousarray(wg.reshape(NE, NIO, 128, NI, 128).transpose(0, 1, 4, 3, 2))
    wu_r = np.ascontiguousarray(wu.reshape(NE, NIO, 128, NI, 128).transpose(0, 1, 4, 3, 2))
    wd_r = np.ascontiguousarray(wd.reshape(NE, 4, 512, NIO, 128).transpose(0, 1, 4, 3, 2))

    in_maps = []
    for c in range(8):
        b, g = c // 4, c % 4
        wq_s = wq[QO * g:QO * (g + 1), :]
        kv_s = np.concatenate([wk[128 * g:128 * (g + 1), :],
                               wv[128 * g:128 * (g + 1), :]], axis=0)
        wq_r = np.ascontiguousarray(wq_s.reshape(QO, NI, 128).transpose(2, 1, 0))
        kv_r = np.ascontiguousarray(kv_s.reshape(KVO, NI, 128).transpose(2, 1, 0))
        wp_s = wp[:, QO * g:QO * (g + 1)]
        wp_r = np.ascontiguousarray(wp_s.reshape(D, 4, 128).transpose(2, 1, 0))
        in_maps.append({
            "xf": np.ascontiguousarray(x[b]),
            "xo": np.ascontiguousarray(x[b, 1024 * g:1024 * (g + 1)]),
            "qgg": np.ascontiguousarray(qg[NHL * g:NHL * (g + 1)].reshape(1, NHL)),
            "wq_r": wq_r, "kv_r": kv_r, "wp_r": wp_r,
            "rw_r": rw_r, "sg_r": sg_r, "su_r": su_r, "sd_r": sd_r,
            "wg_r": wg_r, "wu_r": wu_r, "wd_r": wd_r,
        })
    return in_maps


def kernel(**inputs):
    nc = _get_nc()
    in_maps = _host_layout(inputs)
    res = run_bass_kernel_spmd(nc, in_maps, core_ids=list(range(8)))
    out = np.empty((B, T, D), np.float32)
    for c in range(8):
        b, g = c // 4, c % 4
        out[b, 1024 * g:1024 * (g + 1)] = res.results[c]["out"]
    return out


# version 10
# speedup vs baseline: 1.0119x; 1.0119x over previous
import math
import os
from contextlib import ExitStack

import numpy as np

import concourse.bass as bass
import concourse.mybir as mybir
from concourse.bass_utils import run_bass_kernel_spmd

F32 = mybir.dt.float32
BF16 = mybir.dt.bfloat16
I32 = mybir.dt.int32
ALU = mybir.AluOpType
ACT = mybir.ActivationFunctionType
AX = mybir.AxisListType

B, T, D = 2, 4096, 2048
NT = T // 128
NQT = 8
NI = D // 128
QO = 512
KVO = 256
NHL = 4
IO = 1408
NIO = IO // 128
NE = 8
EPS_LN = 1e-5
EPS_RMS = float(np.finfo(np.float32).eps)
SQRT_HD = math.sqrt(128.0)
TWO_PI = float(2.0 * math.pi)
PI = float(math.pi)
LOG_BASE = float(math.log(10000.0))

ENGINES = ("sync", "scalar", "vector", "tensor", "gpsimd")


class Sem:
    def __init__(self, es, nc, name):
        self.h = es.enter_context(nc.semaphore(name))
        self.n = 0


class Alloc:

    def __init__(self, nc):
        self.nc = nc
        self.live = {}
        self.order = []

    def sbuf(self, name, shape, dt, side="left"):
        g = self.nc.sbuf_tensor(name, shape, dt, side=side)
        h = g.__enter__()
        self.live[name] = g
        self.order.append(name)
        return h

    def psum(self, name, shape, dt, side="left"):
        g = self.nc.psum_tensor(name, shape, dt, side=side)
        h = g.__enter__()
        self.live[name] = g
        self.order.append(name)
        return h

    def free(self, *names):
        for n in names:
            self.live.pop(n).__exit__(None, None, None)
            self.order.remove(n)

    def close(self):
        for n in reversed(self.order):
            self.live.pop(n).__exit__(None, None, None)
        self.order = []


class Prog:
    def __init__(self):
        self.thunks = {e: [] for e in ENGINES}
        self.label = "P0"

    def emit(self, eng, f):
        self.thunks[eng].append((self.label, f))

    def wait(self, eng, sem, val=None):
        v = sem.n if val is None else val
        if v > 0:
            self.emit(eng, lambda e, h=sem.h, v=v: e.wait_ge(h, v))

    def op(self, eng, builder, inc=None, k=1):
        if inc is not None:
            inc.n += k

        def th(e, b=builder, s=inc, kk=k):
            ins = b(e)
            if s is not None:
                ins.then_inc(s.h, kk)

        self.emit(eng, th)
        return inc.n if inc is not None else None


def _build_program(upto=99, debug=()):
    nc = bass.Bass("TRN2", target_bir_lowering=False)

    d_xf = nc.dram_tensor("xf", [T, D], F32, kind="ExternalInput").ap()
    d_xo = nc.dram_tensor("xo", [1024, D], F32, kind="ExternalInput").ap()
    d_qg = nc.dram_tensor("qgg", [1, NHL], F32, kind="ExternalInput").ap()
    d_wq = nc.dram_tensor("wq_r", [128, NI, QO], F32, kind="ExternalInput").ap()
    d_kv = nc.dram_tensor("kv_r", [128, NI, KVO], F32, kind="ExternalInput").ap()
    d_wp = nc.dram_tensor("wp_r", [128, 4, D], F32, kind="ExternalInput").ap()
    d_rw = nc.dram_tensor("rw_r", [128, NI, NE], F32, kind="ExternalInput").ap()
    d_sg = nc.dram_tensor("sg_r", [NIO, 128, NI, 128], F32, kind="ExternalInput").ap()
    d_su = nc.dram_tensor("su_r", [NIO, 128, NI, 128], F32, kind="ExternalInput").ap()
    d_sd = nc.dram_tensor("sd_r", [4, 128, NIO, 512], F32, kind="ExternalInput").ap()
    d_wg = nc.dram_tensor("wg_r", [NE, NIO, 128, NI, 128], F32, kind="ExternalInput").ap()
    d_wu = nc.dram_tensor("wu_r", [NE, NIO, 128, NI, 128], F32, kind="ExternalInput").ap()
    d_wd = nc.dram_tensor("wd_r", [NE, 4, 128, NIO, 512], F32, kind="ExternalInput").ap()
    d_prs = nc.dram_tensor("prs", [4, 4, 256, D], BF16).ap()
    d_pq = nc.dram_tensor("pq", [1024, D], BF16).ap()
    d_out = nc.dram_tensor("out", [1024, D], F32, kind="ExternalOutput").ap()
    d_dbg = {}
    for name, shape, dt in debug:
        d_dbg[name] = nc.dram_tensor("dbg_" + name, list(shape), dt,
                                     kind="ExternalOutput").ap()

    es = ExitStack()
    A = Alloc(nc)

    cpool = A.sbuf("cpool", [128, 8], F32)
    ident = A.sbuf("ident", [128, 128], BF16)
    onesb = A.sbuf("onesb", [128, 8], BF16)
    onesf32 = A.sbuf("onesf32", [1, 128], F32)
    bcast16 = A.sbuf("bcast16", [128, 16], F32)
    srow = A.sbuf("srow", [128, 16], F32)
    rp = A.sbuf("rp", [128, 96], F32)
    qgrow = A.sbuf("qgrow", [1, NHL], F32)
    lrow = A.sbuf("lrow", [1, 512], F32)
    rrow = A.sbuf("rrow", [1, 512], F32)
    rbc = A.sbuf("rbc", [128, 512], F32)
    LG = A.sbuf("LG", [128, 64], F32)
    combW = A.sbuf("combW", [128, 64], F32)
    rscr = A.sbuf("rscr", [128, 336], F32)
    sgb = A.sbuf("sgb", [128, 1024], F32)
    masks = A.sbuf("masks", [128, 4 * 512], BF16, side="right")
    cosT = A.sbuf("cosT", [128, NT * 8], F32, side="right")
    sinT = A.sbuf("sinT", [128, NT * 8], F32, side="right")
    tbl = A.sbuf("tbl", [128, 320], F32, side="right")
    tbli = A.sbuf("tbli", [128, 256], I32, side="right")
    tscr = A.sbuf("tscr", [128, 512], F32, side="right")

    CONSTS = {0.0: 0, 1.0: 1, EPS_LN: 2, EPS_RMS: 3,
              -LOG_BASE / 8.0: 4}
    for val, i in CONSTS.items():
        nc.const_aps.register(F32, val, cpool[:, i:i + 1])

    P = Prog()
    s_pe = Sem(es, nc, "s_pe")
    s_dv = Sem(es, nc, "s_dv")
    s_sc = Sem(es, nc, "s_sc")
    s_gp = Sem(es, nc, "s_gp")
    s_cc = Sem(es, nc, "s_cc")
    s_sx = [Sem(es, nc, f"s_sx{i}") for i in range(4)]
    s_gx = [Sem(es, nc, f"s_gx{i}") for i in range(8)]
    s_out = Sem(es, nc, "s_out")

    def chain(eng, sem):
        P.wait(eng, sem)

    def dbg_dump(name, src_ap):
        if name in d_dbg:
            for s in (s_dv, s_pe, s_sc, s_gp):
                P.wait("sync", s)
            P.op("sync", lambda e, d=d_dbg[name], s=src_ap: e.dma_start(
                out=d, in_=s), inc=s_out, k=16)

    for val, i in CONSTS.items():
        P.op("vector", lambda e, a=cpool[:, i:i + 1], v=val: e.memset(a, v))
    P.op("vector", lambda e: e.memset(srow[:], 0.0))
    P.op("vector", lambda e: e.memset(onesf32[:], 1.0))
    P.op("vector", lambda e: e.memset(rp[:], 0.0))
    v_consts = P.op("vector", lambda e: e.memset(tbl[:], 0.0), inc=s_dv)

    P.op("gpsimd", lambda e: e.memset(onesb[:], 1.0), inc=s_gp)
    chain("gpsimd", s_gp)
    P.op("gpsimd", lambda e: e.affine_select(
        out=ident[:], in_=onesb[:, 0:1].broadcast_to([128, 128]),
        pattern=[[1, 128]], compare_op=ALU.is_equal, fill=0.0,
        base=0, channel_multiplier=-1), inc=s_gp)
    chain("gpsimd", s_gp)
    for j in range(4):
        P.op("gpsimd", lambda e, j=j: e.affine_select(
            out=masks[:, j * 512:(j + 1) * 512],
            in_=onesb[:, 0:1].broadcast_to([128, 512]),
            pattern=[[1, 512]], compare_op=ALU.is_ge, fill=0.0,
            base=-128 * j, channel_multiplier=-1), inc=s_gp)
        chain("gpsimd", s_gp)
    P.wait("gpsimd", s_dv, v_consts)
    P.op("gpsimd", lambda e: e.iota(
        tbl[:, 0:32], pattern=[[128, 32]], base=0, channel_multiplier=1,
        allow_small_or_imprecise_dtypes=True), inc=s_gp)
    P.op("gpsimd", lambda e: e.iota(
        tbl[0:1, 32:40], pattern=[[1, 8]], base=0, channel_multiplier=0,
        allow_small_or_imprecise_dtypes=True), inc=s_gp)
    g_iota = s_gp.n

    P.wait("scalar", s_gp, g_iota)
    P.wait("scalar", s_dv, v_consts)
    sc_inv = P.op("scalar", lambda e: e.activation(
        out=tbl[0:1, 48:56], in_=tbl[0:1, 32:40], func=ACT.Exp,
        scale=-LOG_BASE / 8.0), inc=s_sc)
    P.op("sync", lambda e: e.dma_start(out=qgrow[:], in_=d_qg), inc=s_sx[2], k=16)
    P.wait("scalar", s_sx[2], 16)
    sc_absqg = P.op("scalar", lambda e: e.activation(
        out=tbl[0:1, 56:60], in_=qgrow[:], func=ACT.Abs), inc=s_sc)
    P.wait("vector", s_sc, sc_absqg)
    P.op("vector", lambda e: e.tensor_scalar(
        out=tbl[0:1, 56:60], in0=tbl[0:1, 56:60], scalar1=-SQRT_HD,
        scalar2=None, op0=ALU.mult), inc=s_dv)
    P.op("vector", lambda e: e.tensor_scalar(
        out=tbl[0:1, 60:64], in0=qgrow[:], scalar1=1.0 / SQRT_HD,
        scalar2=None, op0=ALU.mult), inc=s_dv)
    v_qg = s_dv.n
    ps0 = A.psum("ps0tmp", [128, 512], F32)
    P.wait("tensor", s_dv, v_qg)
    P.wait("tensor", s_sc, sc_inv)
    pe_bc = P.op("tensor", lambda e: e.matmul(
        ps0[:, 0:16], onesf32[0:1, :], tbl[0:1, 48:64],
        start=True, stop=True), inc=s_pe)
    P.wait("vector", s_pe, pe_bc)
    v_bc16 = P.op("vector", lambda e: e.tensor_copy(
        bcast16[:], ps0[:, 0:16]), inc=s_dv)
    chain("vector", s_dv)

    fv = tbl[:, 64:320]
    fv3 = fv.rearrange("p (a b) -> p a b", a=NT)
    P.op("vector", lambda e: e.tensor_tensor(
        out=fv3, in0=tbl[:, 0:32].unsqueeze(2).broadcast_to([128, NT, 8]),
        in1=bcast16[:, 0:8].unsqueeze(1).broadcast_to([128, NT, 8]),
        op=ALU.mult), inc=s_dv)
    chain("vector", s_dv)
    kf = tscr[:, 0:256]
    gt = tscr[:, 256:512]
    for dst, shift in ((sinT, 0.0), (cosT, PI / 2.0)):
        if shift != 0.0:
            P.op("vector", lambda e, s=shift: e.tensor_scalar(
                out=kf, in0=fv, scalar1=s, scalar2=float(1.0 / TWO_PI),
                op0=ALU.add, op1=ALU.mult), inc=s_dv)
        else:
            P.op("vector", lambda e: e.tensor_scalar(
                out=kf, in0=fv, scalar1=float(1.0 / TWO_PI), scalar2=None,
                op0=ALU.mult), inc=s_dv)
        chain("vector", s_dv)
        P.op("vector", lambda e: e.tensor_copy(tbli[:], kf), inc=s_dv)
        chain("vector", s_dv)
        P.op("vector", lambda e: e.tensor_copy(kf, tbli[:]), inc=s_dv)
        chain("vector", s_dv)
        if shift != 0.0:
            P.op("vector", lambda e, s=shift: e.tensor_scalar(
                out=gt, in0=fv, scalar1=s, scalar2=None, op0=ALU.add),
                inc=s_dv)
            chain("vector", s_dv)
            P.op("vector", lambda e, d=dst: e.scalar_tensor_tensor(
                out=d[:], in0=kf, scalar=-TWO_PI, in1=gt,
                op0=ALU.mult, op1=ALU.add), inc=s_dv)
        else:
            P.op("vector", lambda e, d=dst: e.scalar_tensor_tensor(
                out=d[:], in0=kf, scalar=-TWO_PI, in1=fv,
                op0=ALU.mult, op1=ALU.add), inc=s_dv)
        chain("vector", s_dv)
        P.op("vector", lambda e, d=dst: e.tensor_scalar(
            out=gt, in0=d[:], scalar1=PI, scalar2=None, op0=ALU.is_ge),
            inc=s_dv)
        chain("vector", s_dv)
        P.op("vector", lambda e, d=dst: e.scalar_tensor_tensor(
            out=d[:], in0=gt, scalar=-TWO_PI, in1=d[:],
            op0=ALU.mult, op1=ALU.add), inc=s_dv)
        chain("vector", s_dv)
    v_angles = s_dv.n
    P.wait("scalar", s_dv, v_angles)
    for dst in (sinT, cosT):
        P.op("scalar", lambda e, d=dst: e.activation(
            out=d[:], in_=d[:], func=ACT.Sin), inc=s_sc)
        chain("scalar", s_sc)
    sc_tables = s_sc.n

    g_setup = s_gp.n
    v_setup = s_dv.n
    A.free("ps0tmp")
    A.free("tscr", "tbli", "tbl")

    YFM = A.sbuf("YFM", [128, NHL * T], BF16)
    KVTM = A.sbuf("KVTM", [128, NT * KVO], BF16)
    QFM = A.sbuf("QFM", [128, NHL * T], BF16)
    KFM = A.sbuf("KFM", [128, T], BF16)
    wq_sb = A.sbuf("wq_sb", [128, NI * QO], BF16)
    kv_sb = A.sbuf("kv_sb", [128, NI * KVO], BF16)
    P.op("gpsimd", lambda e: e.dma_start(
        out=wq_sb[:], in_=d_wq.rearrange("p a b -> p (a b)")), inc=s_gx[0], k=16)
    g_wq = s_gx[0].n
    P.op("gpsimd", lambda e: e.dma_start(
        out=kv_sb[:], in_=d_kv.rearrange("p a b -> p (a b)")), inc=s_gx[1], k=16)
    g_kv = s_gx[1].n

    QTM = A.sbuf("QTM", [128, NT * QO], BF16)
    xbuf = [A.sbuf(f"xbuf{i}", [128, 2048], F32) for i in range(2)]
    sq2 = [A.sbuf(f"sq2{i}", [128, 704], F32) for i in range(2)]
    z1tm = [A.sbuf(f"z1tm{i}", [128, 2048], BF16) for i in range(2)]
    z1f = [A.sbuf(f"z1f{i}", [128, 2048], BF16) for i in range(2)]
    psT = [A.psum(f"psT{i}", [128, 2048], BF16) for i in range(2)]
    psQ = [A.psum(f"psQ{i}", [128, 512], F32) for i in range(2)]
    psKV = [A.psum(f"psKV{i}", [128, 512], F32) for i in range(2)]

    P.wait("tensor", s_gx[0], g_wq)
    P.wait("tensor", s_gx[1], g_kv)
    P.wait("tensor", s_gp, g_setup)
    P.wait("tensor", s_dv, v_setup)
    P.wait("scalar", s_dv, v_angles)

    done_x_free = [0, 0]
    done_dv_sq = [0, 0]
    done_sc_z1f = [0, 0]
    done_sc_evac = [0, 0]
    done_pe_T = [0, 0]
    done_dv_z1f = [0, 0]
    done_pe_qkv = [0, 0]
    done_dv_evac = [0, 0]

    def emit_ln(tt, sl, src_tile, zdst, c0, sqbuf, prev_dv=0):
        P.op("vector", lambda e, s=src_tile, c=c0: e.tensor_reduce(
            out=srow[:, c:c + 1], in_=s, axis=AX.X, op=ALU.add), inc=s_dv)
        if prev_dv:
            P.wait("scalar", s_dv, prev_dv)
        chain("scalar", s_sc)
        P.op("scalar", lambda e, s=src_tile, c=c0, q=sqbuf: e.activation(
            out=q, in_=s, func=ACT.Square,
            accum_out=srow[:, c + 1:c + 2]), inc=s_sc)
        sc_sq = s_sc.n
        chain("vector", s_dv)
        P.op("vector", lambda e, c=c0: e.tensor_scalar(
            out=srow[:, c:c + 1], in0=srow[:, c:c + 1],
            scalar1=1.0 / 2048.0, scalar2=None, op0=ALU.mult), inc=s_dv)
        P.wait("vector", s_sc, sc_sq)
        P.op("vector", lambda e, c=c0: e.tensor_scalar(
            out=srow[:, c + 1:c + 2], in0=srow[:, c + 1:c + 2],
            scalar1=1.0 / 2048.0, scalar2=None, op0=ALU.mult), inc=s_dv)
        chain("vector", s_dv)
        P.op("vector", lambda e, c=c0: e.tensor_tensor(
            out=srow[:, c + 2:c + 3], in0=srow[:, c:c + 1],
            in1=srow[:, c:c + 1], op=ALU.mult), inc=s_dv)
        chain("vector", s_dv)
        P.op("vector", lambda e, c=c0: e.tensor_tensor(
            out=srow[:, c + 2:c + 3], in0=srow[:, c + 1:c + 2],
            in1=srow[:, c + 2:c + 3], op=ALU.subtract), inc=s_dv)
        v_var = s_dv.n
        P.wait("scalar", s_dv, v_var)
        P.op("scalar", lambda e, c=c0: e.activation(
            out=srow[:, c + 2:c + 3], in_=srow[:, c + 2:c + 3], func=ACT.Sqrt,
            bias=EPS_LN), inc=s_sc)
        sc_sd = s_sc.n
        P.wait("vector", s_sc, sc_sd)
        P.op("vector", lambda e, c=c0: e.reciprocal(
            out=srow[:, c + 2:c + 3], in_=srow[:, c + 2:c + 3]), inc=s_dv)
        P.op("vector", lambda e, c=c0: e.tensor_scalar(
            out=srow[:, c:c + 1], in0=srow[:, c:c + 1],
            scalar1=-1.0, scalar2=None, op0=ALU.mult), inc=s_dv)
        chain("vector", s_dv)
        P.op("vector", lambda e, s=src_tile, z=zdst, c=c0: e.tensor_scalar(
            out=z, in0=s, scalar1=srow[:, c:c + 1],
            scalar2=srow[:, c + 2:c + 3], op0=ALU.add, op1=ALU.mult),
            inc=s_dv)
        return s_dv.n

    P.label = "P1"
    for tt in range(NT):
        sl = tt % 2
        if done_x_free[sl]:
            P.wait("sync", s_dv, done_x_free[sl])
        P.op("sync", lambda e, tt=tt, sl=sl: e.dma_start(
            out=xbuf[sl][:], in_=d_xf[tt * 128:(tt + 1) * 128, :]),
            inc=s_sx[sl], k=16)
        v_x = s_sx[sl].n
        P.wait("vector", s_sx[sl], v_x)
        P.wait("scalar", s_sx[sl], v_x)
        if done_pe_T[sl]:
            P.wait("vector", s_pe, done_pe_T[sl])
            P.wait("scalar", s_pe, done_pe_T[sl])
        v_apply = emit_ln(tt, sl, xbuf[sl][:], z1tm[sl][:],
                          0 if sl == 0 else 4, z1tm[sl][:],
                          prev_dv=done_x_free[sl])
        done_x_free[sl] = v_apply

        P.wait("tensor", s_dv, v_apply)
        if done_dv_z1f[sl]:
            P.wait("tensor", s_dv, done_dv_z1f[sl])
        for ic in range(NI):
            P.op("tensor", lambda e, ic=ic, sl=sl: e.transpose(
                psT[sl][:, ic * 128:(ic + 1) * 128],
                z1tm[sl][:, ic * 128:(ic + 1) * 128], ident[:]),
                inc=s_pe if ic == NI - 1 else None)
        done_pe_T[sl] = s_pe.n
        P.wait("vector", s_pe, done_pe_T[sl])
        if done_pe_qkv[sl]:
            P.wait("vector", s_pe, done_pe_qkv[sl])
        P.op("vector", lambda e, sl=sl: e.tensor_copy(
            z1f[sl][:], psT[sl][:]), inc=s_dv)
        done_sc_z1f[sl] = s_dv.n
        P.wait("tensor", s_dv, done_sc_z1f[sl])
        if done_sc_evac[sl]:
            P.wait("tensor", s_dv, done_sc_evac[sl])
        for ic in range(NI):
            P.op("tensor", lambda e, ic=ic, sl=sl: e.matmul(
                psQ[sl][:], z1f[sl][:, ic * 128:(ic + 1) * 128],
                wq_sb[:, ic * QO:(ic + 1) * QO],
                start=(ic == 0), stop=(ic == NI - 1)), inc=None)
            P.op("tensor", lambda e, ic=ic, sl=sl: e.matmul(
                psKV[sl][:, 0:KVO], z1f[sl][:, ic * 128:(ic + 1) * 128],
                kv_sb[:, ic * KVO:(ic + 1) * KVO],
                start=(ic == 0), stop=(ic == NI - 1)),
                inc=s_pe if ic == NI - 1 else None)
        done_pe_qkv[sl] = s_pe.n
        P.wait("vector", s_pe, done_pe_qkv[sl])
        P.op("vector", lambda e, tt=tt, sl=sl: e.tensor_copy(
            QTM[:, tt * QO:(tt + 1) * QO], psQ[sl][:]), inc=s_dv)
        P.op("vector", lambda e, tt=tt, sl=sl: e.tensor_copy(
            KVTM[:, tt * KVO:(tt + 1) * KVO], psKV[sl][:, 0:KVO]), inc=s_dv)
        done_sc_evac[sl] = s_dv.n
        qs = tt * QO
        ks = tt * KVO
        sq = sq2[sl]
        if done_dv_sq[sl]:
            P.wait("scalar", s_dv, done_dv_sq[sl])
        P.wait("scalar", s_dv, done_sc_evac[sl])
        P.op("scalar", lambda e, qs=qs, sq=sq: e.activation(
            out=sq[:, 0:512], in_=QTM[:, qs:qs + 512], func=ACT.Square),
            inc=s_sc)
        P.op("scalar", lambda e, ks=ks, sq=sq: e.activation(
            out=sq[:, 512:640], in_=KVTM[:, ks:ks + 128], func=ACT.Square),
            inc=s_sc)
        sc_q = s_sc.n
        P.wait("vector", s_sc, sc_q)
        P.op("vector", lambda e, sq=sq: e.tensor_reduce(
            out=srow[:, 8:12],
            in_=sq[:, 0:512].rearrange("p (h d) -> p h d", h=4),
            axis=AX.X, op=ALU.add), inc=s_dv)
        P.op("vector", lambda e, sq=sq: e.tensor_reduce(
            out=srow[:, 12:13], in_=sq[:, 512:640], axis=AX.X, op=ALU.add),
            inc=s_dv)
        chain("vector", s_dv)
        P.op("vector", lambda e: e.tensor_scalar(
            out=srow[:, 8:13], in0=srow[:, 8:13], scalar1=1.0 / 128.0,
            scalar2=None, op0=ALU.mult), inc=s_dv)
        v_ms = s_dv.n
        P.wait("scalar", s_dv, v_ms)
        P.op("scalar", lambda e: e.activation(
            out=srow[:, 8:13], in_=srow[:, 8:13], func=ACT.Sqrt,
            bias=EPS_RMS), inc=s_sc)
        sc_rs = s_sc.n
        P.wait("vector", s_sc, sc_rs)
        P.op("vector", lambda e: e.reciprocal(
            out=srow[:, 8:13], in_=srow[:, 8:13]), inc=s_dv)
        chain("vector", s_dv)
        qv = QTM[:, qs:qs + 512].rearrange("p (h d) -> p h d", h=4)
        P.op("vector", lambda e, qv=qv: e.scalar_tensor_tensor(
            out=qv, in0=qv, scalar=1.0,
            in1=srow[:, 8:12].unsqueeze(2).broadcast_to([128, 4, 128]),
            op0=ALU.mult, op1=ALU.mult), inc=s_dv)
        P.op("vector", lambda e, ks=ks: e.tensor_scalar(
            out=KVTM[:, ks:ks + 128], in0=KVTM[:, ks:ks + 128],
            scalar1=srow[:, 12:13], scalar2=None, op0=ALU.mult), inc=s_dv)
        chain("vector", s_dv)
        qA = qv[:, :, 0:8]
        qB = qv[:, :, 8:16]
        kA = KVTM[:, ks:ks + 8]
        kB = KVTM[:, ks + 8:ks + 16]
        cosb = cosT[:, tt * 8:(tt + 1) * 8]
        sinb = sinT[:, tt * 8:(tt + 1) * 8]
        cosb4 = cosb.unsqueeze(1).broadcast_to([128, 4, 8])
        sinb4 = sinb.unsqueeze(1).broadcast_to([128, 4, 8])
        rA = rp[:, 0:32].rearrange("p (h d) -> p h d", h=4)
        rB = rp[:, 32:64].rearrange("p (h d) -> p h d", h=4)
        sA = sq[:, 640:672].rearrange("p (h d) -> p h d", h=4)
        sB = sq[:, 672:704].rearrange("p (h d) -> p h d", h=4)
        P.op("vector", lambda e, qA=qA, cosb4=cosb4, rA=rA: e.tensor_tensor(
            out=rA, in0=qA, in1=cosb4, op=ALU.mult))
        P.op("vector", lambda e, qB=qB, sinb4=sinb4, rB=rB: e.tensor_tensor(
            out=rB, in0=qB, in1=sinb4, op=ALU.mult))
        P.op("vector", lambda e, qA=qA, sinb4=sinb4, sA=sA: e.tensor_tensor(
            out=sA, in0=qA, in1=sinb4, op=ALU.mult))
        P.op("vector", lambda e, qB=qB, cosb4=cosb4, sB=sB: e.tensor_tensor(
            out=sB, in0=qB, in1=cosb4, op=ALU.mult))
        P.op("vector", lambda e, kA=kA, cosb=cosb: e.tensor_tensor(
            out=rp[:, 64:72], in0=kA, in1=cosb, op=ALU.mult))
        P.op("vector", lambda e, kB=kB, sinb=sinb: e.tensor_tensor(
            out=rp[:, 72:80], in0=kB, in1=sinb, op=ALU.mult))
        P.op("vector", lambda e, kA=kA, sinb=sinb: e.tensor_tensor(
            out=rp[:, 80:88], in0=kA, in1=sinb, op=ALU.mult))
        P.op("vector", lambda e, kB=kB, cosb=cosb: e.tensor_tensor(
            out=rp[:, 88:96], in0=kB, in1=cosb, op=ALU.mult), inc=s_dv)
        chain("vector", s_dv)
        P.op("vector", lambda e, qA=qA, rA=rA, rB=rB: e.tensor_tensor(
            out=qA, in0=rA, in1=rB, op=ALU.add))
        P.op("vector", lambda e, qB=qB, sA=sA, sB=sB: e.tensor_tensor(
            out=qB, in0=sA, in1=sB, op=ALU.subtract))
        P.op("vector", lambda e, kA=kA: e.tensor_tensor(
            out=kA, in0=rp[:, 64:72], in1=rp[:, 72:80], op=ALU.add))
        P.op("vector", lambda e, kB=kB: e.tensor_tensor(
            out=kB, in0=rp[:, 80:88], in1=rp[:, 88:96], op=ALU.subtract),
            inc=s_dv)
        chain("vector", s_dv)
        done_dv_sq[0] = s_dv.n

    if upto == 1:
        dbg_dump("QTM", QTM[:])
        dbg_dump("KVTM", KVTM[:])

    if upto == 2:
        dbg_dump("QTM", QTM[:])
        dbg_dump("KVTM", KVTM[:])
    v_p2_done = s_dv.n

    P.label = "P3"
    if upto >= 3:
        P.wait("tensor", s_dv, v_p2_done)
        strip_free = [0] * 4
        strips = []
        for ttg in range(8):
            for h in range(NHL):
                strips.append(("q", ttg, h))
            strips.append(("k", ttg, 0))
        for sidx, (kind, ttg, h) in enumerate(strips):
            slot = sidx % 4
            bank = psT[slot // 2]
            coff = (slot % 2) * 1024
            if strip_free[slot]:
                P.wait("tensor", s_dv, strip_free[slot])
            for k in range(4):
                tt = ttg * 4 + k
                if kind == "q":
                    src = QTM[:, tt * QO + h * 128: tt * QO + (h + 1) * 128]
                else:
                    src = KVTM[:, tt * KVO: tt * KVO + 128]
                P.op("tensor", lambda e, b=bank, c=coff, k=k, s=src: e.transpose(
                    b[:, c + k * 128:c + (k + 1) * 128], s, ident[:]),
                    inc=s_pe if k == 3 else None)
            pe_s = s_pe.n
            P.wait("vector", s_pe, pe_s)
            if kind == "q":
                dst = QFM[:, h * T + ttg * 512: h * T + (ttg + 1) * 512]
            else:
                dst = KFM[:, ttg * 512:(ttg + 1) * 512]
            P.op("vector", lambda e, d=dst, b=bank, c=coff: e.tensor_copy(
                d, b[:, c:c + 512]), inc=s_dv)
            strip_free[slot] = s_dv.n
    v_p3_done = s_dv.n
    pe_p3 = s_pe.n
    if upto == 3:
        dbg_dump("QFM", QFM[:])
        dbg_dump("KFM", KFM[:])

    A.free("z1f1", "z1f0", "z1tm1", "z1tm0", "sq21", "sq20",
           "xbuf1", "xbuf0", "QTM", "kv_sb", "wq_sb")
    A.free("psKV1", "psKV0", "psQ1", "psQ0", "psT1", "psT0")

    wp_sb = A.sbuf("wp_sb", [128, 4 * D], BF16)
    Pst = [A.sbuf(f"Pst{i}", [128, 2048], BF16) for i in range(2)]
    PT = [A.sbuf(f"PT{i}", [128, 1024], BF16) for i in range(2)]
    P.label = "P4"
    if upto >= 4:
        psST = [A.psum(f"psST{i}", [128, 1024], F32) for i in range(2)]
        psY = [A.psum(f"psY{i}", [128, 512], F32) for i in range(2)]
        psl = [A.psum(f"psl{i}", [128, 512], F32) for i in range(2)]

        P.wait("tensor", s_dv, v_p3_done)
        done_sc_exp = [0, 0]
        done_pe_pt = [0, 0]
        done_dv_y = [0, 0]
        uidx = 0
        for h in range(NHL):
            for qc in range(8):
                n_st = 4 * (qc + 1)
                npair = n_st // 2
                par = uidx % 2
                qsl = QFM[:, h * T + qc * 512: h * T + (qc + 1) * 512]
                pair_sc = [0] * npair
                pair_dv = [0] * npair

                def emit_scores(p):
                    slp = p % 2
                    if done_sc_exp[slp]:
                        P.wait("tensor", s_sc, done_sc_exp[slp])
                    for z in range(2):
                        st = p * 2 + z
                        P.op("tensor", lambda e, st=st, z=z, slp=slp, q=qsl: e.matmul(
                            psST[slp][:, z * 512:(z + 1) * 512],
                            KFM[:, st * 128:(st + 1) * 128], q,
                            start=True, stop=True),
                            inc=s_pe if z == 1 else None)
                    return s_pe.n

                def emit_exp_mask(p, pe_val):
                    slp = p % 2
                    P.wait("scalar", s_pe, pe_val)
                    if done_pe_pt[slp]:
                        P.wait("scalar", s_pe, done_pe_pt[slp])
                    P.op("scalar", lambda e, slp=slp, h=h: e.activation(
                        out=PT[slp][:], in_=psST[slp][:], func=ACT.Exp,
                        bias=bcast16[:, 8 + h:9 + h],
                        scale=bcast16[:, 12 + h:13 + h]),
                        inc=s_sc)
                    done_sc_exp[slp] = s_sc.n
                    pair_sc[p] = s_sc.n
                    dvv = 0
                    for z in range(2):
                        st = p * 2 + z
                        j = st - 4 * qc
                        if 0 <= j < 4:
                            P.wait("vector", s_sc, pair_sc[p])
                            P.op("vector", lambda e, slp=slp, z=z, j=j: e.tensor_tensor(
                                out=PT[slp][:, z * 512:(z + 1) * 512],
                                in0=PT[slp][:, z * 512:(z + 1) * 512],
                                in1=masks[:, j * 512:(j + 1) * 512],
                                op=ALU.mult), inc=s_dv)
                            dvv = s_dv.n
                    pair_dv[p] = dvv

                def emit_lpv(p):
                    slp = p % 2
                    P.wait("tensor", s_sc, pair_sc[p])
                    if pair_dv[p]:
                        P.wait("tensor", s_dv, pair_dv[p])
                    for z in range(2):
                        st = p * 2 + z
                        first = st == 0
                        last = st == n_st - 1
                        P.op("tensor", lambda e, slp=slp, z=z, par=par,
                             first=first, last=last: e.matmul(
                                 psl[par][0:1, :], onesb[:, 0:1],
                                 PT[slp][:, z * 512:(z + 1) * 512],
                                 start=first, stop=last), inc=None)
                        P.op("tensor", lambda e, slp=slp, z=z, st=st, par=par,
                             first=first, last=last: e.matmul(
                                 psY[par][:],
                                 KVTM[:, st * KVO + 128:(st + 1) * KVO],
                                 PT[slp][:, z * 512:(z + 1) * 512],
                                 start=first, stop=last),
                             inc=s_pe if z == 1 else None)
                    done_pe_pt[slp] = s_pe.n
                    return s_pe.n

                if done_dv_y[par]:
                    P.wait("tensor", s_dv, done_dv_y[par])
                pev = emit_scores(0)
                emit_exp_mask(0, pev)
                for p in range(1, npair):
                    pev = emit_scores(p)
                    emit_exp_mask(p, pev)
                    emit_lpv(p - 1)
                pe_last = emit_lpv(npair - 1)

                P.wait("vector", s_pe, pe_last)
                P.op("vector", lambda e, par=par: e.tensor_copy(
                    lrow[:], psl[par][0:1, :]), inc=s_dv)
                chain("vector", s_dv)
                P.op("vector", lambda e: e.reciprocal(
                    out=rrow[:], in_=lrow[:]), inc=s_dv)
                v_r = s_dv.n
                P.wait("tensor", s_dv, v_r)
                pe_b = P.op("tensor", lambda e, par=par: e.matmul(
                    psl[par][:], onesf32[0:1, :], rrow[0:1, :],
                    start=True, stop=True), inc=s_pe)
                P.wait("vector", s_pe, pe_b)
                P.op("vector", lambda e, par=par: e.tensor_copy(
                    rbc[:], psl[par][:]), inc=s_dv)
                chain("vector", s_dv)
                P.op("vector", lambda e, h=h, qc=qc, par=par: e.scalar_tensor_tensor(
                    out=YFM[:, h * T + qc * 512:h * T + (qc + 1) * 512],
                    in0=psY[par][:], scalar=1.0, in1=rbc[:],
                    op0=ALU.mult, op1=ALU.mult), inc=s_dv)
                done_dv_y[par] = s_dv.n
                uidx += 1
        A.free("psl1", "psl0", "psY1", "psY0", "psST1", "psST0")
    v_p4_done = s_dv.n
    if upto == 4:
        dbg_dump("YFM", YFM[:])
    A.free("PT1", "PT0")
    A.free("sinT", "cosT", "masks")

    P.label = "P5"
    if upto >= 5:
        psW = [A.psum(f"psW{i}", [128, 512], F32) for i in range(2)]
        P.wait("gpsimd", s_pe, pe_p3)
        P.op("gpsimd", lambda e: e.dma_start(
            out=wp_sb[:], in_=d_wp.rearrange("p a b -> p (a b)")),
            inc=s_gx[2], k=16)
        g_wp = s_gx[2].n
        P.wait("tensor", s_gx[2], g_wp)
        P.wait("tensor", s_dv, v_p4_done)
        done_dv_w = [0, 0]
        done_pst_dma = [0, 0]
        chunk_ready = {}
        tt_order = [8 * r + 2 * k + z for k in range(4) for r in range(4)
                    for z in range(2)]
        for ti, tt in enumerate(tt_order):
            sl = ti % 2
            if done_pst_dma[sl]:
                P.wait("vector", s_sx[2 + sl], done_pst_dma[sl])
            for hoq in range(4):
                wsl = (tt * 4 + hoq) % 2
                if done_dv_w[wsl]:
                    P.wait("tensor", s_dv, done_dv_w[wsl])
                for dc in range(4):
                    P.op("tensor", lambda e, tt=tt, hoq=hoq, dc=dc, wsl=wsl: e.matmul(
                        psW[wsl][:],
                        YFM[:, dc * T + tt * 128: dc * T + (tt + 1) * 128],
                        wp_sb[:, dc * D + hoq * 512: dc * D + (hoq + 1) * 512],
                        start=(dc == 0), stop=(dc == 3)),
                        inc=s_pe if dc == 3 else None)
                pe_w = s_pe.n
                P.wait("vector", s_pe, pe_w)
                P.op("vector", lambda e, sl=sl, hoq=hoq, wsl=wsl: e.tensor_copy(
                    Pst[sl][:, hoq * 512:(hoq + 1) * 512], psW[wsl][:]),
                    inc=s_dv)
                done_dv_w[wsl] = s_dv.n
            v_tile = s_dv.n
            P.wait("sync", s_dv, v_tile)
            kq, rq, zq = (tt % 8) // 2, tt // 8, tt % 2
            P.op("sync", lambda e, kq=kq, rq=rq, zq=zq, sl=sl: e.dma_start(
                out=d_prs[kq, rq, zq * 128:(zq + 1) * 128, :], in_=Pst[sl][:]),
                inc=s_sx[2 + sl], k=16)
            done_pst_dma[sl] = s_sx[2 + sl].n
            if ti % 8 == 7:
                chunk_ready[ti // 8] = (s_sx[2].n, s_sx[3].n)
        A.free("psW1", "psW0")
    sx2_prs, sx3_prs = s_sx[2].n, s_sx[3].n
    A.free("Pst1", "Pst0", "wp_sb", "kv_sb_dummy") if False else None
    A.free("Pst1", "Pst0", "wp_sb")
    A.free("KFM", "QFM", "KVTM", "YFM")

    rw_sb = A.sbuf("rw_sb", [128, NI * NE], BF16, side="right")
    sgu_sb = [A.sbuf(f"sgu_sb{i}", [128, 2 * NI * 128], BF16, side="right") for i in range(2)]
    sd_sb = [A.sbuf(f"sd_sb{i}", [128, NIO * 512], BF16, side="right") for i in range(2)]
    sgu_ready = {}
    sd_ready = {}
    cc_done = 0
    P.label = "PX"
    if upto >= 5:
        P.wait("gpsimd", s_dv, v_p4_done)
        P.op("gpsimd", lambda e: e.dma_start(
            out=rw_sb[:], in_=d_rw.rearrange("p a b -> p (a b)")),
            inc=s_gx[3], k=16)
        g_rw = s_gx[3].n
        P.op("gpsimd", lambda e: e.dma_start(
            out=sgu_sb[0][:, 0:NI * 128],
            in_=d_sg[0].rearrange("p a b -> p (a b)")), inc=s_gx[4], k=16)
        P.op("gpsimd", lambda e: e.dma_start(
            out=sgu_sb[0][:, NI * 128:],
            in_=d_su[0].rearrange("p a b -> p (a b)")), inc=s_gx[4], k=16)
        sgu_ready[0] = (4, s_gx[4].n)
        P.op("gpsimd", lambda e: e.dma_start(
            out=sd_sb[0][:], in_=d_sd[0].rearrange("p a b -> p (a b)")),
            inc=s_gx[5], k=16)
        sd_ready[0] = (5, s_gx[5].n)
        cc_chunk = {}
        for k in range(4):
            P.wait("gpsimd", s_sx[2], chunk_ready[k][0])
            P.wait("gpsimd", s_sx[3], chunk_ready[k][1])
            cc_chunk[k] = P.op("gpsimd", lambda e, k=k: e.collective_compute(
                "ReduceScatter", ALU.add,
                replica_groups=[[0, 1, 2, 3], [4, 5, 6, 7]],
                ins=[d_prs[k]],
                outs=[d_pq[k * 256:(k + 1) * 256, :]]), inc=s_cc)
        cc_done = s_cc.n

    X2 = A.sbuf("X2", [128, NQT * 2048], F32)
    z2f = A.sbuf("z2f", [128, NI * 1024], BF16)
    P.label = "P6"
    if upto >= 6:
        z2tm = [A.sbuf(f"z2tm{i}", [128, 2048], BF16) for i in range(2)]
        sqscr6 = A.sbuf("sqscr6", [128, 2048], F32)
        psT2 = [A.psum(f"psT2{i}", [128, 2048], BF16) for i in range(2)]
        done_pe_T2 = [0, 0]
        done_dv_z2 = [0, 0]
        done_accum = [0, 0]
        ln_guard = [0]
        P.wait("sync", s_sx[2], sx2_prs)
        P.wait("sync", s_sx[3], sx3_prs)
        P.wait("sync", s_pe, s_pe.n)
        for tt in range(NQT):
            sl = tt % 2
            X2t = X2[:, tt * 2048:(tt + 1) * 2048]
            if done_accum[sl]:
                P.wait("sync", s_gx[sl], done_accum[sl])
            P.op("sync", lambda e, tt=tt: e.dma_start(
                out=X2[:, tt * 2048:(tt + 1) * 2048],
                in_=d_xo[tt * 128:(tt + 1) * 128, :]), inc=s_sx[sl], k=16)
            sxv = s_sx[sl].n
            P.wait("gpsimd", s_sx[sl], sxv)
            P.wait("gpsimd", s_cc, cc_chunk[tt // 2])
            P.op("gpsimd", lambda e, tt=tt: e.dma_start(
                out=X2[:, tt * 2048:(tt + 1) * 2048],
                in_=d_pq[tt * 128:(tt + 1) * 128, :],
                accum_op=ALU.add), inc=s_gx[sl], k=16)
            done_accum[sl] = s_gx[sl].n
            P.wait("vector", s_gx[sl], done_accum[sl])
            P.wait("scalar", s_gx[sl], done_accum[sl])
            if done_pe_T2[sl]:
                P.wait("vector", s_pe, done_pe_T2[sl])
                P.wait("scalar", s_pe, done_pe_T2[sl])
            v_apply = emit_ln(tt, sl, X2t, z2tm[sl][:],
                              0 if sl == 0 else 4, z2tm[sl][:],
                              prev_dv=ln_guard[0])
            c0 = 0 if sl == 0 else 4
            chain("vector", s_dv)
            P.op("vector", lambda e, X2t=X2t, c=c0: e.tensor_scalar(
                out=sqscr6[:, 0:2048], in0=X2t, scalar1=srow[:, c:c + 1],
                scalar2=srow[:, c + 2:c + 3], op0=ALU.add, op1=ALU.mult),
                inc=s_dv)
            chain("vector", s_dv)
            P.op("vector", lambda e, X2t=X2t: e.tensor_tensor(
                out=X2t, in0=X2t, in1=sqscr6[:, 0:2048], op=ALU.add), inc=s_dv)
            ln_guard[0] = s_dv.n
            P.wait("tensor", s_dv, v_apply)
            if done_dv_z2[sl]:
                P.wait("tensor", s_dv, done_dv_z2[sl])
            for ic in range(NI):
                P.op("tensor", lambda e, ic=ic, sl=sl: e.transpose(
                    psT2[sl][:, ic * 128:(ic + 1) * 128],
                    z2tm[sl][:, ic * 128:(ic + 1) * 128], ident[:]),
                    inc=s_pe if ic == NI - 1 else None)
            done_pe_T2[sl] = s_pe.n
            P.wait("vector", s_pe, done_pe_T2[sl])
            zv = z2f[:].rearrange("p (a b) -> p a b", a=NI)[:, :, tt * 128:(tt + 1) * 128]
            P.op("vector", lambda e, sl=sl, zv=zv: e.tensor_copy(
                zv, psT2[sl][:].rearrange("p (a b) -> p a b", a=NI)), inc=s_dv)
            done_dv_z2[sl] = s_dv.n
        A.free("sqscr6")
        A.free("z2tm1", "z2tm0")
        A.free("psT21", "psT20")
    v_p6_done = s_dv.n
    if upto == 6:
        dbg_dump("X2", X2[:])
        dbg_dump("z2f", z2f[:])

    done_sgu_use_g = [0, 0]
    done_sd_use_g = [0, 0]
    HSH = A.sbuf("HSH", [128, NIO * 1024], BF16)
    P.label = "P7"
    if upto >= 7:
        psR = [A.psum(f"psR{i}", [128, 512], F32) for i in range(2)]
        psG = [A.psum(f"psG{i}", [128, 512], F32) for i in range(2)]
        psU = [A.psum(f"psU{i}", [128, 512], F32) for i in range(2)]
        psD = [A.psum(f"psD{i}", [128, 512], F32) for i in range(2)]
        P.wait("tensor", s_dv, v_p6_done)
        P.wait("tensor", s_gx[3], g_rw)
        done_dv_r = [0, 0]
        for tt in range(NQT):
            rsl = tt % 2
            if done_dv_r[rsl]:
                P.wait("tensor", s_dv, done_dv_r[rsl])
            for hc in range(NI):
                P.op("tensor", lambda e, tt=tt, hc=hc, rsl=rsl: e.matmul(
                    psR[rsl][:, 0:8],
                    z2f[:, hc * 1024 + tt * 128: hc * 1024 + (tt + 1) * 128],
                    rw_sb[:, hc * 8:(hc + 1) * 8],
                    start=(hc == 0), stop=(hc == NI - 1)),
                    inc=s_pe if hc == NI - 1 else None)
            pe_r = s_pe.n
            P.wait("vector", s_pe, pe_r)
            P.op("vector", lambda e, tt=tt, rsl=rsl: e.tensor_copy(
                LG[:, tt * 8:(tt + 1) * 8], psR[rsl][:, 0:8]), inc=s_dv)
            done_dv_r[rsl] = s_dv.n
        chain("vector", s_dv)
        LG3 = LG[:].rearrange("p (a b) -> p a b", a=8)
        CW3 = combW[:].rearrange("p (a b) -> p a b", a=8)
        Rr = lambda c0: rscr[:, c0 * 8:c0 * 8 + 8]
        R3 = lambda c0: rscr[:, 8 + c0 * 64:8 + c0 * 64 + 64].rearrange(
            "p (a b) -> p a b", a=8)
        bc = lambda ap: ap.unsqueeze(2).broadcast_to([128, 8, 8])
        P.op("vector", lambda e: e.tensor_reduce(
            out=Rr(0), in_=LG3, axis=AX.X, op=ALU.max), inc=s_dv)
        chain("vector", s_dv)
        P.op("vector", lambda e: e.tensor_tensor(
            out=R3(0), in0=LG3, in1=bc(Rr(0)), op=ALU.subtract), inc=s_dv)
        v_sub = s_dv.n
        P.wait("scalar", s_dv, v_sub)
        P.op("scalar", lambda e: e.activation(
            out=rscr[:, 8:72], in_=rscr[:, 8:72], func=ACT.Exp), inc=s_sc)
        sc_e = s_sc.n
        P.wait("vector", s_sc, sc_e)
        P.op("vector", lambda e: e.tensor_reduce(
            out=Rr(0), in_=R3(0), axis=AX.X, op=ALU.add), inc=s_dv)
        chain("vector", s_dv)
        P.op("vector", lambda e: e.reciprocal(out=Rr(0), in_=Rr(0)), inc=s_dv)
        chain("vector", s_dv)
        P.op("vector", lambda e: e.tensor_tensor(
            out=R3(0), in0=R3(0), in1=bc(Rr(0)), op=ALU.mult), inc=s_dv)
        chain("vector", s_dv)
        P.op("vector", lambda e: e.tensor_reduce(
            out=Rr(0), in_=R3(0), axis=AX.X, op=ALU.max), inc=s_dv)
        chain("vector", s_dv)
        P.op("vector", lambda e: e.tensor_tensor(
            out=R3(1), in0=R3(0), in1=bc(Rr(0)), op=ALU.is_ge), inc=s_dv)
        chain("vector", s_dv)
        P.op("vector", lambda e: e.tensor_scalar(
            out=R3(2), in0=R3(1), scalar1=-1.0, scalar2=1.0,
            op0=ALU.mult, op1=ALU.add), inc=s_dv)
        chain("vector", s_dv)
        P.op("vector", lambda e: e.tensor_tensor(
            out=R3(3), in0=R3(0), in1=R3(2), op=ALU.mult), inc=s_dv)
        chain("vector", s_dv)
        P.op("vector", lambda e: e.tensor_reduce(
            out=Rr(0), in_=R3(3), axis=AX.X, op=ALU.max), inc=s_dv)
        chain("vector", s_dv)
        P.op("vector", lambda e: e.tensor_tensor(
            out=R3(4), in0=R3(3), in1=bc(Rr(0)), op=ALU.is_ge), inc=s_dv)
        chain("vector", s_dv)
        P.op("vector", lambda e: e.tensor_tensor(
            out=R3(1), in0=R3(1), in1=R3(4), op=ALU.add), inc=s_dv)
        chain("vector", s_dv)
        P.op("vector", lambda e: e.tensor_tensor(
            out=CW3, in0=R3(0), in1=R3(1), op=ALU.mult), inc=s_dv)
        v_comb = s_dv.n
        if "combW" in d_dbg:
            dbg_dump("combW", combW[:])
            dbg_dump("LG", LG[:])

        done_sgu_use_g[0] = done_sgu_use_g[1] = 0
        done_sgu_use = done_sgu_use_g
        done_gu_cons = [[0, 0], [0, 0]]
        for c in range(NIO):
            sl = c % 2
            if c + 1 < NIO:
                osl = (c + 1) % 2
                if done_sgu_use[osl]:
                    P.wait("gpsimd", s_pe, done_sgu_use[osl])
                gsem = 4 if osl == 0 else 6
                P.op("gpsimd", lambda e, c=c, osl=osl: e.dma_start(
                    out=sgu_sb[osl][:, 0:NI * 128],
                    in_=d_sg[c + 1].rearrange("p a b -> p (a b)")),
                    inc=s_gx[gsem], k=16)
                P.op("gpsimd", lambda e, c=c, osl=osl: e.dma_start(
                    out=sgu_sb[osl][:, NI * 128:],
                    in_=d_su[c + 1].rearrange("p a b -> p (a b)")),
                    inc=s_gx[gsem], k=16)
                sgu_ready[c + 1] = (gsem, s_gx[gsem].n)
            P.wait("tensor", s_gx[sgu_ready[c][0]], sgu_ready[c][1])
            for half in range(2):
                if done_gu_cons[half][0]:
                    P.wait("tensor", s_sc, done_gu_cons[half][0])
                    P.wait("tensor", s_dv, done_gu_cons[half][1])
                for hc in range(NI):
                    P.op("tensor", lambda e, sl=sl, hc=hc, half=half: e.matmul(
                        psG[half][:],
                        sgu_sb[sl][:, hc * 128:(hc + 1) * 128],
                        z2f[:, hc * 1024 + half * 512: hc * 1024 + half * 512 + 512],
                        start=(hc == 0), stop=(hc == NI - 1)), inc=None)
                    P.op("tensor", lambda e, sl=sl, hc=hc, half=half: e.matmul(
                        psU[half][:],
                        sgu_sb[sl][:, NI * 128 + hc * 128: NI * 128 + (hc + 1) * 128],
                        z2f[:, hc * 1024 + half * 512: hc * 1024 + half * 512 + 512],
                        start=(hc == 0), stop=(hc == NI - 1)),
                        inc=s_pe if hc == NI - 1 else None)
                pe_gu = s_pe.n
                P.wait("scalar", s_pe, pe_gu)
                P.op("scalar", lambda e, half=half: e.activation(
                    out=sgb[:, half * 512:(half + 1) * 512], in_=psG[half][:],
                    func=ACT.Sigmoid), inc=s_sc)
                sc_sg = s_sc.n
                P.wait("vector", s_sc, sc_sg)
                P.op("vector", lambda e, c=c, half=half: e.scalar_tensor_tensor(
                    out=HSH[:, c * 1024 + half * 512: c * 1024 + half * 512 + 512],
                    in0=psU[half][:], scalar=1.0,
                    in1=sgb[:, half * 512:(half + 1) * 512],
                    op0=ALU.mult, op1=ALU.mult), inc=s_dv)
                done_gu_cons[half] = [sc_sg, s_dv.n]
            done_sgu_use[sl] = s_pe.n
        v_hsh = s_dv.n

        done_dv_d = [0, 0]
        done_sd_use_g[0] = done_sd_use_g[1] = 0
        done_sd_use = done_sd_use_g
        for hoq in range(4):
            sl = hoq % 2
            if hoq + 1 < 4:
                osl = (hoq + 1) % 2
                if done_sd_use[osl]:
                    P.wait("gpsimd", s_pe, done_sd_use[osl])
                dsem = 5 if osl == 0 else 7
                P.op("gpsimd", lambda e, hoq=hoq, osl=osl: e.dma_start(
                    out=sd_sb[osl][:],
                    in_=d_sd[hoq + 1].rearrange("p a b -> p (a b)")),
                    inc=s_gx[dsem], k=16)
                sd_ready[hoq + 1] = (dsem, s_gx[dsem].n)
            P.wait("tensor", s_gx[sd_ready[hoq][0]], sd_ready[hoq][1])
            P.wait("tensor", s_dv, v_hsh)
            for tt in range(NQT):
                dsl = (hoq * NQT + tt) % 2
                if done_dv_d[dsl]:
                    P.wait("tensor", s_dv, done_dv_d[dsl])
                for c in range(NIO):
                    P.op("tensor", lambda e, c=c, tt=tt, sl=sl, dsl=dsl: e.matmul(
                        psD[dsl][:],
                        HSH[:, c * 1024 + tt * 128: c * 1024 + (tt + 1) * 128],
                        sd_sb[sl][:, c * 512:(c + 1) * 512],
                        start=(c == 0), stop=(c == NIO - 1)),
                        inc=s_pe if c == NIO - 1 else None)
                pe_d = s_pe.n
                done_sd_use[sl] = pe_d
                P.wait("vector", s_pe, pe_d)
                P.op("vector", lambda e, tt=tt, hoq=hoq, dsl=dsl: e.tensor_tensor(
                    out=X2[:, tt * 2048 + hoq * 512: tt * 2048 + (hoq + 1) * 512],
                    in0=X2[:, tt * 2048 + hoq * 512: tt * 2048 + (hoq + 1) * 512],
                    in1=psD[dsl][:], op=ALU.add), inc=s_dv)
                done_dv_d[dsl] = s_dv.n
    v_p7_done = s_dv.n
    if upto == 7:
        dbg_dump("X2", X2[:])
    A.free("HSH")
    A.free("sd_sb1", "sd_sb0", "sgu_sb1", "sgu_sb0")

    P.label = "P8"
    if upto >= 8:
        HE = [A.sbuf(f"HE{i}", [128, NIO * 1024], BF16) for i in range(2)]
        wgu_sb = [A.sbuf(f"wgu_sb{i}", [128, 2 * NI * 128], BF16) for i in range(2)]
        wd_sb = [A.sbuf(f"wd_sb{i}", [128, NIO * 512], BF16) for i in range(2)]
        wgu_ready = {}
        wd_ready = {}
        done_wgu_use = list(done_sgu_use_g)
        done_wd_use = list(done_sd_use_g)
        done_he_read = [0, 0]
        done_gu_cons2 = [[0, 0], [0, 0]]
        done_dv_d2 = [0, 0]

        def prefetch_gu(idx):
            if idx >= NE * NIO:
                return
            ne_, nc_ = divmod(idx, NIO)
            osl = idx % 2
            if done_wgu_use[osl]:
                P.wait("gpsimd", s_pe, done_wgu_use[osl])
                done_wgu_use[osl] = 0
            gsem = 4 if osl == 0 else 6
            P.op("gpsimd", lambda e, ne_=ne_, nc_=nc_, osl=osl: e.dma_start(
                out=wgu_sb[osl][:, 0:NI * 128],
                in_=d_wg[ne_, nc_].rearrange("p a b -> p (a b)")),
                inc=s_gx[gsem], k=16)
            P.op("gpsimd", lambda e, ne_=ne_, nc_=nc_, osl=osl: e.dma_start(
                out=wgu_sb[osl][:, NI * 128:],
                in_=d_wu[ne_, nc_].rearrange("p a b -> p (a b)")),
                inc=s_gx[gsem], k=16)
            wgu_ready[idx] = (gsem, s_gx[gsem].n)

        def prefetch_wd(idx):
            if idx >= NE * 4:
                return
            ne_, nq_ = divmod(idx, 4)
            osl = idx % 2
            if done_wd_use[osl]:
                P.wait("gpsimd", s_pe, done_wd_use[osl])
                done_wd_use[osl] = 0
            dsem = 5 if osl == 0 else 7
            P.op("gpsimd", lambda e, ne_=ne_, nq_=nq_, osl=osl: e.dma_start(
                out=wd_sb[osl][:],
                in_=d_wd[ne_, nq_].rearrange("p a b -> p (a b)")),
                inc=s_gx[dsem], k=16)
            wd_ready[idx] = (dsem, s_gx[dsem].n)

        P.wait("gpsimd", s_pe, s_pe.n)
        P.wait("gpsimd", s_dv, s_dv.n)
        prefetch_gu(0)
        prefetch_wd(0)
        for e_i in range(NE):
            esl = e_i % 2
            if done_he_read[esl]:
                P.wait("vector", s_pe, done_he_read[esl])
            for c in range(NIO):
                gidx = e_i * NIO + c
                sl = gidx % 2
                prefetch_gu(gidx + 1)
                P.wait("tensor", s_gx[wgu_ready[gidx][0]], wgu_ready[gidx][1])
                for half in range(2):
                    if done_gu_cons2[half][0]:
                        P.wait("tensor", s_sc, done_gu_cons2[half][0])
                        P.wait("tensor", s_dv, done_gu_cons2[half][1])
                    for hc in range(NI):
                        P.op("tensor", lambda e, sl=sl, hc=hc, half=half: e.matmul(
                            psG[half][:],
                            wgu_sb[sl][:, hc * 128:(hc + 1) * 128],
                            z2f[:, hc * 1024 + half * 512: hc * 1024 + half * 512 + 512],
                            start=(hc == 0), stop=(hc == NI - 1)), inc=None)
                        P.op("tensor", lambda e, sl=sl, hc=hc, half=half: e.matmul(
                            psU[half][:],
                            wgu_sb[sl][:, NI * 128 + hc * 128: NI * 128 + (hc + 1) * 128],
                            z2f[:, hc * 1024 + half * 512: hc * 1024 + half * 512 + 512],
                            start=(hc == 0), stop=(hc == NI - 1)),
                            inc=s_pe if hc == NI - 1 else None)
                    pe_gu = s_pe.n
                    P.wait("scalar", s_pe, pe_gu)
                    P.op("scalar", lambda e, half=half: e.activation(
                        out=sgb[:, half * 512:(half + 1) * 512],
                        in_=psG[half][:], func=ACT.Sigmoid), inc=s_sc)
                    sc_sg = s_sc.n
                    P.wait("vector", s_sc, sc_sg)
                    P.op("vector", lambda e, c=c, half=half, esl=esl: e.scalar_tensor_tensor(
                        out=HE[esl][:, c * 1024 + half * 512: c * 1024 + half * 512 + 512],
                        in0=psU[half][:], scalar=1.0,
                        in1=sgb[:, half * 512:(half + 1) * 512],
                        op0=ALU.mult, op1=ALU.mult), inc=s_dv)
                    done_gu_cons2[half] = [sc_sg, s_dv.n]
                done_wgu_use[sl] = s_pe.n
            v_he = s_dv.n
            for hoq in range(4):
                didx = e_i * 4 + hoq
                sl = didx % 2
                prefetch_wd(didx + 1)
                P.wait("tensor", s_gx[wd_ready[didx][0]], wd_ready[didx][1])
                P.wait("tensor", s_dv, v_he)
                for tt in range(NQT):
                    dsl = tt % 2
                    if done_dv_d2[dsl]:
                        P.wait("tensor", s_dv, done_dv_d2[dsl])
                    for c in range(NIO):
                        P.op("tensor", lambda e, c=c, tt=tt, sl=sl, dsl=dsl,
                             esl=esl: e.matmul(
                                 psD[dsl][:],
                                 HE[esl][:, c * 1024 + tt * 128: c * 1024 + (tt + 1) * 128],
                                 wd_sb[sl][:, c * 512:(c + 1) * 512],
                                 start=(c == 0), stop=(c == NIO - 1)),
                             inc=s_pe if c == NIO - 1 else None)
                    pe_d = s_pe.n
                    done_wd_use[sl] = pe_d
                    P.wait("vector", s_pe, pe_d)
                    P.op("vector", lambda e, tt=tt, hoq=hoq, dsl=dsl, e_i=e_i: e.scalar_tensor_tensor(
                        out=X2[:, tt * 2048 + hoq * 512: tt * 2048 + (hoq + 1) * 512],
                        in0=psD[dsl][:],
                        scalar=combW[:, tt * 8 + e_i: tt * 8 + e_i + 1],
                        in1=X2[:, tt * 2048 + hoq * 512: tt * 2048 + (hoq + 1) * 512],
                        op0=ALU.mult, op1=ALU.add), inc=s_dv)
                    done_dv_d2[dsl] = s_dv.n
                done_he_read[esl] = s_pe.n

    if upto >= 8:
        v_final = s_dv.n
        P.wait("sync", s_dv, v_final)
        P.op("sync", lambda e: e.dma_start(
            out=d_out.rearrange("(a p) d -> p a d", p=128),
            in_=X2[:].rearrange("p (a d) -> p a d", a=NQT)), inc=s_out, k=16)
    if s_out.n:
        P.wait("sync", s_out, s_out.n)

    block = es.enter_context(nc.Block())
    thunks = P.thunks

    def replay(eng_name, e):
        cur = None
        sid = None
        for label, th in thunks[eng_name]:
            if label != cur:
                if cur is not None:
                    nc.leave_named_scope(f"{cur}.{eng_name}", sid, False)
                cur = label
                sid, _ = nc.enter_named_scope(f"{cur}.{eng_name}", False)
            th(e)
        if cur is not None:
            nc.leave_named_scope(f"{cur}.{eng_name}", sid, False)

    @block.sync
    def _(e):
        replay("sync", e)

    @block.scalar
    def _(e):
        replay("scalar", e)

    @block.vector
    def _(e):
        replay("vector", e)

    @block.tensor
    def _(e):
        replay("tensor", e)

    @block.gpsimd
    def _(e):
        replay("gpsimd", e)

    es.close()
    A.close()
    return nc


_NC_CACHE = {}


def _get_nc(upto=99, debug=()):
    key = (upto, tuple(debug))
    if key not in _NC_CACHE:
        _NC_CACHE[key] = _build_program(upto=upto, debug=debug)
    return _NC_CACHE[key]


def _host_layout(inputs):
    x = np.ascontiguousarray(np.asarray(inputs["L0_attn_x"]))
    wq = np.asarray(inputs["L0_attn_wq"])
    wk = np.asarray(inputs["L0_attn_wk"])
    wv = np.asarray(inputs["L0_attn_wv"])
    wp = np.asarray(inputs["L0_attn_wp"])
    qg = np.asarray(inputs["L0_attn_qg"])
    rw = np.asarray(inputs["L0_ffn_router_w"])
    sg = np.asarray(inputs["L0_ffn_shared_gate"])
    su = np.asarray(inputs["L0_ffn_shared_up"])
    sd = np.asarray(inputs["L0_ffn_shared_down"])
    wg = np.asarray(inputs["L0_ffn_w_gate"])
    wu = np.asarray(inputs["L0_ffn_w_up"])
    wd = np.asarray(inputs["L0_ffn_w_down"])

    rw_r = np.ascontiguousarray(rw.reshape(NE, NI, 128).transpose(2, 1, 0))
    sg_r = np.ascontiguousarray(sg.reshape(NIO, 128, NI, 128).transpose(0, 3, 2, 1))
    su_r = np.ascontiguousarray(su.reshape(NIO, 128, NI, 128).transpose(0, 3, 2, 1))
    sd_r = np.ascontiguousarray(sd.reshape(4, 512, NIO, 128).transpose(0, 3, 2, 1))
    wg_r = np.ascontiguousarray(wg.reshape(NE, NIO, 128, NI, 128).transpose(0, 1, 4, 3, 2))
    wu_r = np.ascontiguousarray(wu.reshape(NE, NIO, 128, NI, 128).transpose(0, 1, 4, 3, 2))
    wd_r = np.ascontiguousarray(wd.reshape(NE, 4, 512, NIO, 128).transpose(0, 1, 4, 3, 2))

    in_maps = []
    for c in range(8):
        b, g = c // 4, c % 4
        wq_s = wq[QO * g:QO * (g + 1), :]
        kv_s = np.concatenate([wk[128 * g:128 * (g + 1), :],
                               wv[128 * g:128 * (g + 1), :]], axis=0)
        wq_r = np.ascontiguousarray(wq_s.reshape(QO, NI, 128).transpose(2, 1, 0))
        kv_r = np.ascontiguousarray(kv_s.reshape(KVO, NI, 128).transpose(2, 1, 0))
        wp_s = wp[:, QO * g:QO * (g + 1)]
        wp_r = np.ascontiguousarray(wp_s.reshape(D, 4, 128).transpose(2, 1, 0))
        in_maps.append({
            "xf": np.ascontiguousarray(x[b]),
            "xo": np.ascontiguousarray(x[b, 1024 * g:1024 * (g + 1)]),
            "qgg": np.ascontiguousarray(qg[NHL * g:NHL * (g + 1)].reshape(1, NHL)),
            "wq_r": wq_r, "kv_r": kv_r, "wp_r": wp_r,
            "rw_r": rw_r, "sg_r": sg_r, "su_r": su_r, "sd_r": sd_r,
            "wg_r": wg_r, "wu_r": wu_r, "wd_r": wd_r,
        })
    return in_maps


def kernel(**inputs):
    nc = _get_nc()
    in_maps = _host_layout(inputs)
    res = run_bass_kernel_spmd(nc, in_maps, core_ids=list(range(8)))
    out = np.empty((B, T, D), np.float32)
    for c in range(8):
        b, g = c // 4, c % 4
        out[b, 1024 * g:1024 * (g + 1)] = res.results[c]["out"]
    return out


# version 11
# speedup vs baseline: 1.0306x; 1.0184x over previous
import math
import os
from contextlib import ExitStack

import numpy as np

import concourse.bass as bass
import concourse.mybir as mybir
from concourse.bass_utils import run_bass_kernel_spmd

F32 = mybir.dt.float32
BF16 = mybir.dt.bfloat16
I32 = mybir.dt.int32
ALU = mybir.AluOpType
ACT = mybir.ActivationFunctionType
AX = mybir.AxisListType

B, T, D = 2, 4096, 2048
NT = T // 128
NQT = 8
NI = D // 128
QO = 512
KVO = 256
NHL = 4
IO = 1408
NIO = IO // 128
NE = 8
EPS_LN = 1e-5
EPS_RMS = float(np.finfo(np.float32).eps)
SQRT_HD = math.sqrt(128.0)
TWO_PI = float(2.0 * math.pi)
PI = float(math.pi)
LOG_BASE = float(math.log(10000.0))

ENGINES = ("sync", "scalar", "vector", "tensor", "gpsimd")


class Sem:
    def __init__(self, es, nc, name):
        self.h = es.enter_context(nc.semaphore(name))
        self.n = 0


class Alloc:

    def __init__(self, nc):
        self.nc = nc
        self.live = {}
        self.order = []

    def sbuf(self, name, shape, dt, side="left"):
        g = self.nc.sbuf_tensor(name, shape, dt, side=side)
        h = g.__enter__()
        self.live[name] = g
        self.order.append(name)
        return h

    def psum(self, name, shape, dt, side="left"):
        g = self.nc.psum_tensor(name, shape, dt, side=side)
        h = g.__enter__()
        self.live[name] = g
        self.order.append(name)
        return h

    def free(self, *names):
        for n in names:
            self.live.pop(n).__exit__(None, None, None)
            self.order.remove(n)

    def close(self):
        for n in reversed(self.order):
            self.live.pop(n).__exit__(None, None, None)
        self.order = []


class Prog:
    def __init__(self):
        self.thunks = {e: [] for e in ENGINES}
        self.label = "P0"

    def emit(self, eng, f):
        self.thunks[eng].append((self.label, f))

    def wait(self, eng, sem, val=None):
        v = sem.n if val is None else val
        if v > 0:
            self.emit(eng, lambda e, h=sem.h, v=v: e.wait_ge(h, v))

    def op(self, eng, builder, inc=None, k=1):
        if inc is not None:
            inc.n += k

        def th(e, b=builder, s=inc, kk=k):
            ins = b(e)
            if s is not None:
                ins.then_inc(s.h, kk)

        self.emit(eng, th)
        return inc.n if inc is not None else None


def _build_program(upto=99, debug=()):
    nc = bass.Bass("TRN2", target_bir_lowering=False)

    d_xf = nc.dram_tensor("xf", [T, D], F32, kind="ExternalInput").ap()
    d_xo = nc.dram_tensor("xo", [1024, D], F32, kind="ExternalInput").ap()
    d_qg = nc.dram_tensor("qgg", [1, NHL], F32, kind="ExternalInput").ap()
    d_wq = nc.dram_tensor("wq_r", [128, NI, QO], F32, kind="ExternalInput").ap()
    d_kv = nc.dram_tensor("kv_r", [128, NI, KVO], F32, kind="ExternalInput").ap()
    d_wp = nc.dram_tensor("wp_r", [128, 4, D], F32, kind="ExternalInput").ap()
    d_rw = nc.dram_tensor("rw_r", [128, NI, NE], F32, kind="ExternalInput").ap()
    d_sg = nc.dram_tensor("sg_r", [NIO, 128, NI, 128], F32, kind="ExternalInput").ap()
    d_su = nc.dram_tensor("su_r", [NIO, 128, NI, 128], F32, kind="ExternalInput").ap()
    d_sd = nc.dram_tensor("sd_r", [4, 128, NIO, 512], F32, kind="ExternalInput").ap()
    d_wg = nc.dram_tensor("wg_r", [NE, NIO, 128, NI, 128], F32, kind="ExternalInput").ap()
    d_wu = nc.dram_tensor("wu_r", [NE, NIO, 128, NI, 128], F32, kind="ExternalInput").ap()
    d_wd = nc.dram_tensor("wd_r", [NE, 4, 128, NIO, 512], F32, kind="ExternalInput").ap()
    d_prs = nc.dram_tensor("prs", [4, 4, 256, D], BF16).ap()
    d_pq = nc.dram_tensor("pq", [1024, D], BF16).ap()
    d_out = nc.dram_tensor("out", [1024, D], F32, kind="ExternalOutput").ap()
    d_dbg = {}
    for name, shape, dt in debug:
        d_dbg[name] = nc.dram_tensor("dbg_" + name, list(shape), dt,
                                     kind="ExternalOutput").ap()

    es = ExitStack()
    A = Alloc(nc)

    cpool = A.sbuf("cpool", [128, 8], F32)
    ident = A.sbuf("ident", [128, 128], BF16)
    onesb = A.sbuf("onesb", [128, 8], BF16)
    onesf32 = A.sbuf("onesf32", [1, 128], F32)
    bcast16 = A.sbuf("bcast16", [128, 16], F32)
    srow = A.sbuf("srow", [128, 16], F32)
    rp = A.sbuf("rp", [128, 96], F32)
    qgrow = A.sbuf("qgrow", [1, NHL], F32)
    lrow = A.sbuf("lrow", [1, 512], F32)
    rrow = A.sbuf("rrow", [1, 512], F32)
    rbc = A.sbuf("rbc", [128, 512], F32)
    LG = A.sbuf("LG", [128, 64], F32)
    combW = A.sbuf("combW", [128, 64], F32)
    rscr = A.sbuf("rscr", [128, 336], F32)
    sgb = A.sbuf("sgb", [128, 1024], F32)
    masks = A.sbuf("masks", [128, 4 * 512], BF16, side="right")
    cosT = A.sbuf("cosT", [128, NT * 8], F32, side="right")
    sinT = A.sbuf("sinT", [128, NT * 8], F32, side="right")
    tbl = A.sbuf("tbl", [128, 320], F32, side="right")
    tbli = A.sbuf("tbli", [128, 256], I32, side="right")
    tscr = A.sbuf("tscr", [128, 512], F32, side="right")

    CONSTS = {0.0: 0, 1.0: 1, EPS_LN: 2, EPS_RMS: 3,
              -LOG_BASE / 8.0: 4}
    for val, i in CONSTS.items():
        nc.const_aps.register(F32, val, cpool[:, i:i + 1])

    P = Prog()
    s_pe = Sem(es, nc, "s_pe")
    s_dv = Sem(es, nc, "s_dv")
    s_sc = Sem(es, nc, "s_sc")
    s_gp = Sem(es, nc, "s_gp")
    s_cc = Sem(es, nc, "s_cc")
    s_sx = [Sem(es, nc, f"s_sx{i}") for i in range(4)]
    s_gx = [Sem(es, nc, f"s_gx{i}") for i in range(8)]
    s_out = Sem(es, nc, "s_out")

    def chain(eng, sem):
        P.wait(eng, sem)

    def dbg_dump(name, src_ap):
        if name in d_dbg:
            for s in (s_dv, s_pe, s_sc, s_gp):
                P.wait("sync", s)
            P.op("sync", lambda e, d=d_dbg[name], s=src_ap: e.dma_start(
                out=d, in_=s), inc=s_out, k=16)

    for val, i in CONSTS.items():
        P.op("vector", lambda e, a=cpool[:, i:i + 1], v=val: e.memset(a, v))
    P.op("vector", lambda e: e.memset(srow[:], 0.0))
    P.op("vector", lambda e: e.memset(onesf32[:], 1.0))
    P.op("vector", lambda e: e.memset(rp[:], 0.0))
    v_consts = P.op("vector", lambda e: e.memset(tbl[:], 0.0), inc=s_dv)

    P.op("gpsimd", lambda e: e.memset(onesb[:], 1.0), inc=s_gp)
    chain("gpsimd", s_gp)
    P.op("gpsimd", lambda e: e.affine_select(
        out=ident[:], in_=onesb[:, 0:1].broadcast_to([128, 128]),
        pattern=[[1, 128]], compare_op=ALU.is_equal, fill=0.0,
        base=0, channel_multiplier=-1), inc=s_gp)
    chain("gpsimd", s_gp)
    for j in range(4):
        P.op("gpsimd", lambda e, j=j: e.affine_select(
            out=masks[:, j * 512:(j + 1) * 512],
            in_=onesb[:, 0:1].broadcast_to([128, 512]),
            pattern=[[1, 512]], compare_op=ALU.is_ge, fill=0.0,
            base=-128 * j, channel_multiplier=-1), inc=s_gp)
        chain("gpsimd", s_gp)
    P.wait("gpsimd", s_dv, v_consts)
    P.op("gpsimd", lambda e: e.iota(
        tbl[:, 0:32], pattern=[[128, 32]], base=0, channel_multiplier=1,
        allow_small_or_imprecise_dtypes=True), inc=s_gp)
    P.op("gpsimd", lambda e: e.iota(
        tbl[0:1, 32:40], pattern=[[1, 8]], base=0, channel_multiplier=0,
        allow_small_or_imprecise_dtypes=True), inc=s_gp)
    g_iota = s_gp.n

    P.wait("scalar", s_gp, g_iota)
    P.wait("scalar", s_dv, v_consts)
    sc_inv = P.op("scalar", lambda e: e.activation(
        out=tbl[0:1, 48:56], in_=tbl[0:1, 32:40], func=ACT.Exp,
        scale=-LOG_BASE / 8.0), inc=s_sc)
    P.op("sync", lambda e: e.dma_start(out=qgrow[:], in_=d_qg), inc=s_sx[2], k=16)
    P.wait("scalar", s_sx[2], 16)
    sc_absqg = P.op("scalar", lambda e: e.activation(
        out=tbl[0:1, 56:60], in_=qgrow[:], func=ACT.Abs), inc=s_sc)
    P.wait("vector", s_sc, sc_absqg)
    P.op("vector", lambda e: e.tensor_scalar(
        out=tbl[0:1, 56:60], in0=tbl[0:1, 56:60], scalar1=-SQRT_HD,
        scalar2=None, op0=ALU.mult), inc=s_dv)
    P.op("vector", lambda e: e.tensor_scalar(
        out=tbl[0:1, 60:64], in0=qgrow[:], scalar1=1.0 / SQRT_HD,
        scalar2=None, op0=ALU.mult), inc=s_dv)
    v_qg = s_dv.n
    ps0 = A.psum("ps0tmp", [128, 512], F32)
    P.wait("tensor", s_dv, v_qg)
    P.wait("tensor", s_sc, sc_inv)
    pe_bc = P.op("tensor", lambda e: e.matmul(
        ps0[:, 0:16], onesf32[0:1, :], tbl[0:1, 48:64],
        start=True, stop=True), inc=s_pe)
    P.wait("vector", s_pe, pe_bc)
    v_bc16 = P.op("vector", lambda e: e.tensor_copy(
        bcast16[:], ps0[:, 0:16]), inc=s_dv)
    chain("vector", s_dv)

    fv = tbl[:, 64:320]
    fv3 = fv.rearrange("p (a b) -> p a b", a=NT)
    P.op("vector", lambda e: e.tensor_tensor(
        out=fv3, in0=tbl[:, 0:32].unsqueeze(2).broadcast_to([128, NT, 8]),
        in1=bcast16[:, 0:8].unsqueeze(1).broadcast_to([128, NT, 8]),
        op=ALU.mult), inc=s_dv)
    chain("vector", s_dv)
    kf = tscr[:, 0:256]
    gt = tscr[:, 256:512]
    for dst, shift in ((sinT, 0.0), (cosT, PI / 2.0)):
        if shift != 0.0:
            P.op("vector", lambda e, s=shift: e.tensor_scalar(
                out=kf, in0=fv, scalar1=s, scalar2=float(1.0 / TWO_PI),
                op0=ALU.add, op1=ALU.mult), inc=s_dv)
        else:
            P.op("vector", lambda e: e.tensor_scalar(
                out=kf, in0=fv, scalar1=float(1.0 / TWO_PI), scalar2=None,
                op0=ALU.mult), inc=s_dv)
        chain("vector", s_dv)
        P.op("vector", lambda e: e.tensor_copy(tbli[:], kf), inc=s_dv)
        chain("vector", s_dv)
        P.op("vector", lambda e: e.tensor_copy(kf, tbli[:]), inc=s_dv)
        chain("vector", s_dv)
        if shift != 0.0:
            P.op("vector", lambda e, s=shift: e.tensor_scalar(
                out=gt, in0=fv, scalar1=s, scalar2=None, op0=ALU.add),
                inc=s_dv)
            chain("vector", s_dv)
            P.op("vector", lambda e, d=dst: e.scalar_tensor_tensor(
                out=d[:], in0=kf, scalar=-TWO_PI, in1=gt,
                op0=ALU.mult, op1=ALU.add), inc=s_dv)
        else:
            P.op("vector", lambda e, d=dst: e.scalar_tensor_tensor(
                out=d[:], in0=kf, scalar=-TWO_PI, in1=fv,
                op0=ALU.mult, op1=ALU.add), inc=s_dv)
        chain("vector", s_dv)
        P.op("vector", lambda e, d=dst: e.tensor_scalar(
            out=gt, in0=d[:], scalar1=PI, scalar2=None, op0=ALU.is_ge),
            inc=s_dv)
        chain("vector", s_dv)
        P.op("vector", lambda e, d=dst: e.scalar_tensor_tensor(
            out=d[:], in0=gt, scalar=-TWO_PI, in1=d[:],
            op0=ALU.mult, op1=ALU.add), inc=s_dv)
        chain("vector", s_dv)
    v_angles = s_dv.n
    P.wait("scalar", s_dv, v_angles)
    for dst in (sinT, cosT):
        P.op("scalar", lambda e, d=dst: e.activation(
            out=d[:], in_=d[:], func=ACT.Sin), inc=s_sc)
        chain("scalar", s_sc)
    sc_tables = s_sc.n

    g_setup = s_gp.n
    v_setup = s_dv.n
    A.free("ps0tmp")
    A.free("tscr", "tbli", "tbl")

    YFM = A.sbuf("YFM", [128, NHL * T], BF16)
    KVTM = A.sbuf("KVTM", [128, NT * KVO], BF16)
    QFM = A.sbuf("QFM", [128, NHL * T], BF16)
    KFM = A.sbuf("KFM", [128, T], BF16)
    wq_sb = A.sbuf("wq_sb", [128, NI * QO], BF16)
    kv_sb = A.sbuf("kv_sb", [128, NI * KVO], BF16)
    P.op("gpsimd", lambda e: e.dma_start(
        out=wq_sb[:], in_=d_wq.rearrange("p a b -> p (a b)")), inc=s_gx[0], k=16)
    g_wq = s_gx[0].n
    P.op("gpsimd", lambda e: e.dma_start(
        out=kv_sb[:], in_=d_kv.rearrange("p a b -> p (a b)")), inc=s_gx[1], k=16)
    g_kv = s_gx[1].n

    QTM = A.sbuf("QTM", [128, NT * QO], BF16)
    xbuf = [A.sbuf(f"xbuf{i}", [128, 2048], F32) for i in range(2)]
    sq2 = [A.sbuf(f"sq2{i}", [128, 704], F32) for i in range(2)]
    z1tm = [A.sbuf(f"z1tm{i}", [128, 2048], BF16) for i in range(2)]
    z1f = [A.sbuf(f"z1f{i}", [128, 2048], BF16) for i in range(2)]
    psT = [A.psum(f"psT{i}", [128, 2048], BF16) for i in range(2)]
    psQ = [A.psum(f"psQ{i}", [128, 512], F32) for i in range(2)]
    psKV = [A.psum(f"psKV{i}", [128, 512], F32) for i in range(2)]

    P.wait("tensor", s_gx[0], g_wq)
    P.wait("tensor", s_gx[1], g_kv)
    P.wait("tensor", s_gp, g_setup)
    P.wait("tensor", s_dv, v_setup)
    P.wait("scalar", s_dv, v_angles)

    done_x_free = [0, 0]
    done_dv_sq = [0, 0]
    done_sc_z1f = [0, 0]
    done_sc_evac = [0, 0]
    done_pe_T = [0, 0]
    done_dv_z1f = [0, 0]
    done_pe_qkv = [0, 0]
    done_dv_evac = [0, 0]

    def emit_ln(tt, sl, src_tile, zdst, c0, sqbuf, prev_dv=0):
        P.op("vector", lambda e, s=src_tile, c=c0: e.tensor_reduce(
            out=srow[:, c:c + 1], in_=s, axis=AX.X, op=ALU.add), inc=s_dv)
        if prev_dv:
            P.wait("scalar", s_dv, prev_dv)
        chain("scalar", s_sc)
        P.op("scalar", lambda e, s=src_tile, c=c0, q=sqbuf: e.activation(
            out=q, in_=s, func=ACT.Square,
            accum_out=srow[:, c + 1:c + 2]), inc=s_sc)
        sc_sq = s_sc.n
        chain("vector", s_dv)
        P.op("vector", lambda e, c=c0: e.tensor_scalar(
            out=srow[:, c:c + 1], in0=srow[:, c:c + 1],
            scalar1=1.0 / 2048.0, scalar2=None, op0=ALU.mult), inc=s_dv)
        P.wait("vector", s_sc, sc_sq)
        P.op("vector", lambda e, c=c0: e.tensor_scalar(
            out=srow[:, c + 1:c + 2], in0=srow[:, c + 1:c + 2],
            scalar1=1.0 / 2048.0, scalar2=None, op0=ALU.mult), inc=s_dv)
        chain("vector", s_dv)
        P.op("vector", lambda e, c=c0: e.tensor_tensor(
            out=srow[:, c + 2:c + 3], in0=srow[:, c:c + 1],
            in1=srow[:, c:c + 1], op=ALU.mult), inc=s_dv)
        chain("vector", s_dv)
        P.op("vector", lambda e, c=c0: e.tensor_tensor(
            out=srow[:, c + 2:c + 3], in0=srow[:, c + 1:c + 2],
            in1=srow[:, c + 2:c + 3], op=ALU.subtract), inc=s_dv)
        v_var = s_dv.n
        P.wait("scalar", s_dv, v_var)
        P.op("scalar", lambda e, c=c0: e.activation(
            out=srow[:, c + 2:c + 3], in_=srow[:, c + 2:c + 3], func=ACT.Sqrt,
            bias=EPS_LN), inc=s_sc)
        sc_sd = s_sc.n
        P.wait("vector", s_sc, sc_sd)
        P.op("vector", lambda e, c=c0: e.reciprocal(
            out=srow[:, c + 2:c + 3], in_=srow[:, c + 2:c + 3]), inc=s_dv)
        P.op("vector", lambda e, c=c0: e.tensor_scalar(
            out=srow[:, c:c + 1], in0=srow[:, c:c + 1],
            scalar1=-1.0, scalar2=None, op0=ALU.mult), inc=s_dv)
        chain("vector", s_dv)
        P.op("vector", lambda e, s=src_tile, z=zdst, c=c0: e.tensor_scalar(
            out=z, in0=s, scalar1=srow[:, c:c + 1],
            scalar2=srow[:, c + 2:c + 3], op0=ALU.add, op1=ALU.mult),
            inc=s_dv)
        return s_dv.n

    P.label = "P1"
    apply_of = {}

    def p1_load_ln(tt):
        sl = tt % 2
        if done_x_free[sl]:
            P.wait("sync", s_dv, done_x_free[sl])
        P.op("sync", lambda e, tt=tt, sl=sl: e.dma_start(
            out=xbuf[sl][:], in_=d_xf[tt * 128:(tt + 1) * 128, :]),
            inc=s_sx[sl], k=16)
        v_x = s_sx[sl].n
        P.wait("vector", s_sx[sl], v_x)
        P.wait("scalar", s_sx[sl], v_x)
        if done_pe_T[sl]:
            P.wait("vector", s_pe, done_pe_T[sl])
            P.wait("scalar", s_pe, done_pe_T[sl])
        v_apply = emit_ln(tt, sl, xbuf[sl][:], z1tm[sl][:],
                          0 if sl == 0 else 4, z1tm[sl][:],
                          prev_dv=done_x_free[sl])
        done_x_free[sl] = v_apply
        apply_of[tt] = v_apply

    def p1_tile(tt):
        sl = tt % 2
        v_apply = apply_of[tt]
        P.wait("tensor", s_dv, v_apply)
        if done_dv_z1f[sl]:
            P.wait("tensor", s_dv, done_dv_z1f[sl])
        for ic in range(NI):
            P.op("tensor", lambda e, ic=ic, sl=sl: e.transpose(
                psT[sl][:, ic * 128:(ic + 1) * 128],
                z1tm[sl][:, ic * 128:(ic + 1) * 128], ident[:]),
                inc=s_pe if ic == NI - 1 else None)
        done_pe_T[sl] = s_pe.n
        P.wait("vector", s_pe, done_pe_T[sl])
        if done_pe_qkv[sl]:
            P.wait("vector", s_pe, done_pe_qkv[sl])
        P.op("vector", lambda e, sl=sl: e.tensor_copy(
            z1f[sl][:], psT[sl][:]), inc=s_dv)
        done_sc_z1f[sl] = s_dv.n
        P.wait("tensor", s_dv, done_sc_z1f[sl])
        if done_sc_evac[sl]:
            P.wait("tensor", s_dv, done_sc_evac[sl])
        for ic in range(NI):
            P.op("tensor", lambda e, ic=ic, sl=sl: e.matmul(
                psQ[sl][:], z1f[sl][:, ic * 128:(ic + 1) * 128],
                wq_sb[:, ic * QO:(ic + 1) * QO],
                start=(ic == 0), stop=(ic == NI - 1)), inc=None)
            P.op("tensor", lambda e, ic=ic, sl=sl: e.matmul(
                psKV[sl][:, 0:KVO], z1f[sl][:, ic * 128:(ic + 1) * 128],
                kv_sb[:, ic * KVO:(ic + 1) * KVO],
                start=(ic == 0), stop=(ic == NI - 1)),
                inc=s_pe if ic == NI - 1 else None)
        done_pe_qkv[sl] = s_pe.n
        P.wait("vector", s_pe, done_pe_qkv[sl])
        P.op("vector", lambda e, tt=tt, sl=sl: e.tensor_copy(
            QTM[:, tt * QO:(tt + 1) * QO], psQ[sl][:]), inc=s_dv)
        P.op("vector", lambda e, tt=tt, sl=sl: e.tensor_copy(
            KVTM[:, tt * KVO:(tt + 1) * KVO], psKV[sl][:, 0:KVO]), inc=s_dv)
        done_sc_evac[sl] = s_dv.n
        qs = tt * QO
        ks = tt * KVO
        sq = sq2[sl]
        if done_dv_sq[sl]:
            P.wait("scalar", s_dv, done_dv_sq[sl])
        P.wait("scalar", s_dv, done_sc_evac[sl])
        P.op("scalar", lambda e, qs=qs, sq=sq: e.activation(
            out=sq[:, 0:512], in_=QTM[:, qs:qs + 512], func=ACT.Square),
            inc=s_sc)
        P.op("scalar", lambda e, ks=ks, sq=sq: e.activation(
            out=sq[:, 512:640], in_=KVTM[:, ks:ks + 128], func=ACT.Square),
            inc=s_sc)
        sc_q = s_sc.n
        P.wait("vector", s_sc, sc_q)
        P.op("vector", lambda e, sq=sq: e.tensor_reduce(
            out=srow[:, 8:12],
            in_=sq[:, 0:512].rearrange("p (h d) -> p h d", h=4),
            axis=AX.X, op=ALU.add), inc=s_dv)
        P.op("vector", lambda e, sq=sq: e.tensor_reduce(
            out=srow[:, 12:13], in_=sq[:, 512:640], axis=AX.X, op=ALU.add),
            inc=s_dv)
        chain("vector", s_dv)
        P.op("vector", lambda e: e.tensor_scalar(
            out=srow[:, 8:13], in0=srow[:, 8:13], scalar1=1.0 / 128.0,
            scalar2=None, op0=ALU.mult), inc=s_dv)
        v_ms = s_dv.n
        P.wait("scalar", s_dv, v_ms)
        P.op("scalar", lambda e: e.activation(
            out=srow[:, 8:13], in_=srow[:, 8:13], func=ACT.Sqrt,
            bias=EPS_RMS), inc=s_sc)
        sc_rs = s_sc.n
        P.wait("vector", s_sc, sc_rs)
        P.op("vector", lambda e: e.reciprocal(
            out=srow[:, 8:13], in_=srow[:, 8:13]), inc=s_dv)
        chain("vector", s_dv)
        qv = QTM[:, qs:qs + 512].rearrange("p (h d) -> p h d", h=4)
        P.op("vector", lambda e, qv=qv: e.scalar_tensor_tensor(
            out=qv, in0=qv, scalar=1.0,
            in1=srow[:, 8:12].unsqueeze(2).broadcast_to([128, 4, 128]),
            op0=ALU.mult, op1=ALU.mult), inc=s_dv)
        P.op("vector", lambda e, ks=ks: e.tensor_scalar(
            out=KVTM[:, ks:ks + 128], in0=KVTM[:, ks:ks + 128],
            scalar1=srow[:, 12:13], scalar2=None, op0=ALU.mult), inc=s_dv)
        chain("vector", s_dv)
        qA = qv[:, :, 0:8]
        qB = qv[:, :, 8:16]
        kA = KVTM[:, ks:ks + 8]
        kB = KVTM[:, ks + 8:ks + 16]
        cosb = cosT[:, tt * 8:(tt + 1) * 8]
        sinb = sinT[:, tt * 8:(tt + 1) * 8]
        cosb4 = cosb.unsqueeze(1).broadcast_to([128, 4, 8])
        sinb4 = sinb.unsqueeze(1).broadcast_to([128, 4, 8])
        rA = rp[:, 0:32].rearrange("p (h d) -> p h d", h=4)
        rB = rp[:, 32:64].rearrange("p (h d) -> p h d", h=4)
        sA = sq[:, 640:672].rearrange("p (h d) -> p h d", h=4)
        sB = sq[:, 672:704].rearrange("p (h d) -> p h d", h=4)
        P.op("vector", lambda e, qA=qA, cosb4=cosb4, rA=rA: e.tensor_tensor(
            out=rA, in0=qA, in1=cosb4, op=ALU.mult))
        P.op("vector", lambda e, qB=qB, sinb4=sinb4, rB=rB: e.tensor_tensor(
            out=rB, in0=qB, in1=sinb4, op=ALU.mult))
        P.op("vector", lambda e, qA=qA, sinb4=sinb4, sA=sA: e.tensor_tensor(
            out=sA, in0=qA, in1=sinb4, op=ALU.mult))
        P.op("vector", lambda e, qB=qB, cosb4=cosb4, sB=sB: e.tensor_tensor(
            out=sB, in0=qB, in1=cosb4, op=ALU.mult))
        P.op("vector", lambda e, kA=kA, cosb=cosb: e.tensor_tensor(
            out=rp[:, 64:72], in0=kA, in1=cosb, op=ALU.mult))
        P.op("vector", lambda e, kB=kB, sinb=sinb: e.tensor_tensor(
            out=rp[:, 72:80], in0=kB, in1=sinb, op=ALU.mult))
        P.op("vector", lambda e, kA=kA, sinb=sinb: e.tensor_tensor(
            out=rp[:, 80:88], in0=kA, in1=sinb, op=ALU.mult))
        P.op("vector", lambda e, kB=kB, cosb=cosb: e.tensor_tensor(
            out=rp[:, 88:96], in0=kB, in1=cosb, op=ALU.mult), inc=s_dv)
        chain("vector", s_dv)
        P.op("vector", lambda e, qA=qA, rA=rA, rB=rB: e.tensor_tensor(
            out=qA, in0=rA, in1=rB, op=ALU.add))
        P.op("vector", lambda e, qB=qB, sA=sA, sB=sB: e.tensor_tensor(
            out=qB, in0=sA, in1=sB, op=ALU.subtract))
        P.op("vector", lambda e, kA=kA: e.tensor_tensor(
            out=kA, in0=rp[:, 64:72], in1=rp[:, 72:80], op=ALU.add))
        P.op("vector", lambda e, kB=kB: e.tensor_tensor(
            out=kB, in0=rp[:, 80:88], in1=rp[:, 88:96], op=ALU.subtract),
            inc=s_dv)
        chain("vector", s_dv)
        done_dv_sq[0] = s_dv.n



    p1_load_ln(0)
    for tt in range(NT):
        if tt + 1 < NT:
            p1_load_ln(tt + 1)
        p1_tile(tt)
    if upto == 1:
        dbg_dump("QTM", QTM[:])
        dbg_dump("KVTM", KVTM[:])

    if upto == 2:
        dbg_dump("QTM", QTM[:])
        dbg_dump("KVTM", KVTM[:])
    v_p2_done = s_dv.n

    P.label = "P3"
    if upto >= 3:
        P.wait("tensor", s_dv, v_p2_done)
        strip_free = [0] * 4
        strips = []
        for ttg in range(8):
            for h in range(NHL):
                strips.append(("q", ttg, h))
            strips.append(("k", ttg, 0))
        for sidx, (kind, ttg, h) in enumerate(strips):
            slot = sidx % 4
            bank = psT[slot // 2]
            coff = (slot % 2) * 1024
            if strip_free[slot]:
                P.wait("tensor", s_dv, strip_free[slot])
            for k in range(4):
                tt = ttg * 4 + k
                if kind == "q":
                    src = QTM[:, tt * QO + h * 128: tt * QO + (h + 1) * 128]
                else:
                    src = KVTM[:, tt * KVO: tt * KVO + 128]
                P.op("tensor", lambda e, b=bank, c=coff, k=k, s=src: e.transpose(
                    b[:, c + k * 128:c + (k + 1) * 128], s, ident[:]),
                    inc=s_pe if k == 3 else None)
            pe_s = s_pe.n
            P.wait("vector", s_pe, pe_s)
            if kind == "q":
                dst = QFM[:, h * T + ttg * 512: h * T + (ttg + 1) * 512]
            else:
                dst = KFM[:, ttg * 512:(ttg + 1) * 512]
            P.op("vector", lambda e, d=dst, b=bank, c=coff: e.tensor_copy(
                d, b[:, c:c + 512]), inc=s_dv)
            strip_free[slot] = s_dv.n
    v_p3_done = s_dv.n
    pe_p3 = s_pe.n
    if upto == 3:
        dbg_dump("QFM", QFM[:])
        dbg_dump("KFM", KFM[:])

    A.free("z1f1", "z1f0", "z1tm1", "z1tm0", "sq21", "sq20",
           "xbuf1", "xbuf0", "QTM", "kv_sb", "wq_sb")
    A.free("psKV1", "psKV0", "psQ1", "psQ0", "psT1", "psT0")

    wp_sb = A.sbuf("wp_sb", [128, 4 * D], BF16)
    Pst = [A.sbuf(f"Pst{i}", [128, 2048], BF16) for i in range(2)]
    PT = [A.sbuf(f"PT{i}", [128, 1024], BF16) for i in range(2)]
    P.label = "P4"
    if upto >= 4:
        psST = [A.psum(f"psST{i}", [128, 1024], F32) for i in range(2)]
        psY = [A.psum(f"psY{i}", [128, 512], F32) for i in range(2)]
        psl = [A.psum(f"psl{i}", [128, 512], F32) for i in range(2)]

        P.wait("tensor", s_dv, v_p3_done)
        done_sc_exp = [0, 0]
        done_pe_pt = [0, 0]
        done_dv_y = [0, 0]
        uidx = 0
        for h in range(NHL):
            for qc in range(8):
                n_st = 4 * (qc + 1)
                npair = n_st // 2
                par = uidx % 2
                qsl = QFM[:, h * T + qc * 512: h * T + (qc + 1) * 512]
                pair_sc = [0] * npair
                pair_dv = [0] * npair

                def emit_scores(p):
                    slp = p % 2
                    if done_sc_exp[slp]:
                        P.wait("tensor", s_sc, done_sc_exp[slp])
                    for z in range(2):
                        st = p * 2 + z
                        P.op("tensor", lambda e, st=st, z=z, slp=slp, q=qsl: e.matmul(
                            psST[slp][:, z * 512:(z + 1) * 512],
                            KFM[:, st * 128:(st + 1) * 128], q,
                            start=True, stop=True),
                            inc=s_pe if z == 1 else None)
                    return s_pe.n

                def emit_exp_mask(p, pe_val):
                    slp = p % 2
                    P.wait("scalar", s_pe, pe_val)
                    if done_pe_pt[slp]:
                        P.wait("scalar", s_pe, done_pe_pt[slp])
                    P.op("scalar", lambda e, slp=slp, h=h: e.activation(
                        out=PT[slp][:], in_=psST[slp][:], func=ACT.Exp,
                        bias=bcast16[:, 8 + h:9 + h],
                        scale=bcast16[:, 12 + h:13 + h]),
                        inc=s_sc)
                    done_sc_exp[slp] = s_sc.n
                    pair_sc[p] = s_sc.n
                    dvv = 0
                    for z in range(2):
                        st = p * 2 + z
                        j = st - 4 * qc
                        if 0 <= j < 4:
                            P.wait("vector", s_sc, pair_sc[p])
                            P.op("vector", lambda e, slp=slp, z=z, j=j: e.tensor_tensor(
                                out=PT[slp][:, z * 512:(z + 1) * 512],
                                in0=PT[slp][:, z * 512:(z + 1) * 512],
                                in1=masks[:, j * 512:(j + 1) * 512],
                                op=ALU.mult), inc=s_dv)
                            dvv = s_dv.n
                    pair_dv[p] = dvv

                def emit_lpv(p):
                    slp = p % 2
                    P.wait("tensor", s_sc, pair_sc[p])
                    if pair_dv[p]:
                        P.wait("tensor", s_dv, pair_dv[p])
                    for z in range(2):
                        st = p * 2 + z
                        first = st == 0
                        last = st == n_st - 1
                        P.op("tensor", lambda e, slp=slp, z=z, par=par,
                             first=first, last=last: e.matmul(
                                 psl[par][0:1, :], onesb[:, 0:1],
                                 PT[slp][:, z * 512:(z + 1) * 512],
                                 start=first, stop=last), inc=None)
                        P.op("tensor", lambda e, slp=slp, z=z, st=st, par=par,
                             first=first, last=last: e.matmul(
                                 psY[par][:],
                                 KVTM[:, st * KVO + 128:(st + 1) * KVO],
                                 PT[slp][:, z * 512:(z + 1) * 512],
                                 start=first, stop=last),
                             inc=s_pe if z == 1 else None)
                    done_pe_pt[slp] = s_pe.n
                    return s_pe.n

                if done_dv_y[par]:
                    P.wait("tensor", s_dv, done_dv_y[par])
                pev = emit_scores(0)
                emit_exp_mask(0, pev)
                for p in range(1, npair):
                    pev = emit_scores(p)
                    emit_exp_mask(p, pev)
                    emit_lpv(p - 1)
                pe_last = emit_lpv(npair - 1)

                P.wait("vector", s_pe, pe_last)
                P.op("vector", lambda e, par=par: e.tensor_copy(
                    lrow[:], psl[par][0:1, :]), inc=s_dv)
                chain("vector", s_dv)
                P.op("vector", lambda e: e.reciprocal(
                    out=rrow[:], in_=lrow[:]), inc=s_dv)
                v_r = s_dv.n
                P.wait("tensor", s_dv, v_r)
                pe_b = P.op("tensor", lambda e, par=par: e.matmul(
                    psl[par][:], onesf32[0:1, :], rrow[0:1, :],
                    start=True, stop=True), inc=s_pe)
                P.wait("vector", s_pe, pe_b)
                P.op("vector", lambda e, par=par: e.tensor_copy(
                    rbc[:], psl[par][:]), inc=s_dv)
                chain("vector", s_dv)
                P.op("vector", lambda e, h=h, qc=qc, par=par: e.scalar_tensor_tensor(
                    out=YFM[:, h * T + qc * 512:h * T + (qc + 1) * 512],
                    in0=psY[par][:], scalar=1.0, in1=rbc[:],
                    op0=ALU.mult, op1=ALU.mult), inc=s_dv)
                done_dv_y[par] = s_dv.n
                uidx += 1
        A.free("psl1", "psl0", "psY1", "psY0", "psST1", "psST0")
    v_p4_done = s_dv.n
    if upto == 4:
        dbg_dump("YFM", YFM[:])
    A.free("PT1", "PT0")
    A.free("sinT", "cosT", "masks")

    P.label = "P5"
    if upto >= 5:
        psW = [A.psum(f"psW{i}", [128, 512], F32) for i in range(2)]
        P.wait("gpsimd", s_pe, pe_p3)
        P.op("gpsimd", lambda e: e.dma_start(
            out=wp_sb[:], in_=d_wp.rearrange("p a b -> p (a b)")),
            inc=s_gx[2], k=16)
        g_wp = s_gx[2].n
        P.wait("tensor", s_gx[2], g_wp)
        P.wait("tensor", s_dv, v_p4_done)
        done_dv_w = [0, 0]
        done_pst_dma = [0, 0]
        chunk_ready = {}
        tt_order = [8 * r + 2 * k + z for k in range(4) for r in range(4)
                    for z in range(2)]
        for ti, tt in enumerate(tt_order):
            sl = ti % 2
            if done_pst_dma[sl]:
                P.wait("vector", s_sx[2 + sl], done_pst_dma[sl])
            for hoq in range(4):
                wsl = (tt * 4 + hoq) % 2
                if done_dv_w[wsl]:
                    P.wait("tensor", s_dv, done_dv_w[wsl])
                for dc in range(4):
                    P.op("tensor", lambda e, tt=tt, hoq=hoq, dc=dc, wsl=wsl: e.matmul(
                        psW[wsl][:],
                        YFM[:, dc * T + tt * 128: dc * T + (tt + 1) * 128],
                        wp_sb[:, dc * D + hoq * 512: dc * D + (hoq + 1) * 512],
                        start=(dc == 0), stop=(dc == 3)),
                        inc=s_pe if dc == 3 else None)
                pe_w = s_pe.n
                P.wait("vector", s_pe, pe_w)
                P.op("vector", lambda e, sl=sl, hoq=hoq, wsl=wsl: e.tensor_copy(
                    Pst[sl][:, hoq * 512:(hoq + 1) * 512], psW[wsl][:]),
                    inc=s_dv)
                done_dv_w[wsl] = s_dv.n
            v_tile = s_dv.n
            P.wait("sync", s_dv, v_tile)
            kq, rq, zq = (tt % 8) // 2, tt // 8, tt % 2
            P.op("sync", lambda e, kq=kq, rq=rq, zq=zq, sl=sl: e.dma_start(
                out=d_prs[kq, rq, zq * 128:(zq + 1) * 128, :], in_=Pst[sl][:]),
                inc=s_sx[2 + sl], k=16)
            done_pst_dma[sl] = s_sx[2 + sl].n
            if ti % 8 == 7:
                chunk_ready[ti // 8] = (s_sx[2].n, s_sx[3].n)
        A.free("psW1", "psW0")
    sx2_prs, sx3_prs = s_sx[2].n, s_sx[3].n
    A.free("Pst1", "Pst0", "wp_sb", "kv_sb_dummy") if False else None
    A.free("Pst1", "Pst0", "wp_sb")
    A.free("KFM", "QFM", "KVTM", "YFM")

    rw_sb = A.sbuf("rw_sb", [128, NI * NE], BF16, side="right")
    sgu_sb = [A.sbuf(f"sgu_sb{i}", [128, 2 * NI * 128], BF16, side="right") for i in range(2)]
    sd_sb = [A.sbuf(f"sd_sb{i}", [128, NIO * 512], BF16, side="right") for i in range(2)]
    sgu_ready = {}
    sd_ready = {}
    cc_done = 0
    P.label = "PX"
    if upto >= 5:
        P.wait("gpsimd", s_dv, v_p4_done)
        P.op("gpsimd", lambda e: e.dma_start(
            out=rw_sb[:], in_=d_rw.rearrange("p a b -> p (a b)")),
            inc=s_gx[3], k=16)
        g_rw = s_gx[3].n
        P.op("gpsimd", lambda e: e.dma_start(
            out=sgu_sb[0][:, 0:NI * 128],
            in_=d_sg[0].rearrange("p a b -> p (a b)")), inc=s_gx[4], k=16)
        P.op("gpsimd", lambda e: e.dma_start(
            out=sgu_sb[0][:, NI * 128:],
            in_=d_su[0].rearrange("p a b -> p (a b)")), inc=s_gx[4], k=16)
        sgu_ready[0] = (4, s_gx[4].n)
        P.op("gpsimd", lambda e: e.dma_start(
            out=sd_sb[0][:], in_=d_sd[0].rearrange("p a b -> p (a b)")),
            inc=s_gx[5], k=16)
        sd_ready[0] = (5, s_gx[5].n)
        cc_chunk = {}
        for k in range(4):
            P.wait("gpsimd", s_sx[2], chunk_ready[k][0])
            P.wait("gpsimd", s_sx[3], chunk_ready[k][1])
            cc_chunk[k] = P.op("gpsimd", lambda e, k=k: e.collective_compute(
                "ReduceScatter", ALU.add,
                replica_groups=[[0, 1, 2, 3], [4, 5, 6, 7]],
                ins=[d_prs[k]],
                outs=[d_pq[k * 256:(k + 1) * 256, :]]), inc=s_cc)
        cc_done = s_cc.n

    X2 = A.sbuf("X2", [128, NQT * 2048], F32)
    z2f = A.sbuf("z2f", [128, NI * 1024], BF16)
    P.label = "P6"
    if upto >= 6:
        z2tm = [A.sbuf(f"z2tm{i}", [128, 2048], BF16) for i in range(2)]
        sqscr6 = A.sbuf("sqscr6", [128, 2048], F32)
        psT2 = [A.psum(f"psT2{i}", [128, 2048], BF16) for i in range(2)]
        done_pe_T2 = [0, 0]
        done_dv_z2 = [0, 0]
        done_accum = [0, 0]
        ln_guard = [0]
        P.wait("sync", s_sx[2], sx2_prs)
        P.wait("sync", s_sx[3], sx3_prs)
        P.wait("sync", s_pe, s_pe.n)
        for tt in range(NQT):
            sl = tt % 2
            X2t = X2[:, tt * 2048:(tt + 1) * 2048]
            if done_accum[sl]:
                P.wait("sync", s_gx[sl], done_accum[sl])
            P.op("sync", lambda e, tt=tt: e.dma_start(
                out=X2[:, tt * 2048:(tt + 1) * 2048],
                in_=d_xo[tt * 128:(tt + 1) * 128, :]), inc=s_sx[sl], k=16)
            sxv = s_sx[sl].n
            P.wait("gpsimd", s_sx[sl], sxv)
            P.wait("gpsimd", s_cc, cc_chunk[tt // 2])
            P.op("gpsimd", lambda e, tt=tt: e.dma_start(
                out=X2[:, tt * 2048:(tt + 1) * 2048],
                in_=d_pq[tt * 128:(tt + 1) * 128, :],
                accum_op=ALU.add), inc=s_gx[sl], k=16)
            done_accum[sl] = s_gx[sl].n
            P.wait("vector", s_gx[sl], done_accum[sl])
            P.wait("scalar", s_gx[sl], done_accum[sl])
            if done_pe_T2[sl]:
                P.wait("vector", s_pe, done_pe_T2[sl])
                P.wait("scalar", s_pe, done_pe_T2[sl])
            v_apply = emit_ln(tt, sl, X2t, z2tm[sl][:],
                              0 if sl == 0 else 4, z2tm[sl][:],
                              prev_dv=ln_guard[0])
            c0 = 0 if sl == 0 else 4
            chain("vector", s_dv)
            P.op("vector", lambda e, X2t=X2t, c=c0: e.tensor_scalar(
                out=sqscr6[:, 0:2048], in0=X2t, scalar1=srow[:, c:c + 1],
                scalar2=srow[:, c + 2:c + 3], op0=ALU.add, op1=ALU.mult),
                inc=s_dv)
            chain("vector", s_dv)
            P.op("vector", lambda e, X2t=X2t: e.tensor_tensor(
                out=X2t, in0=X2t, in1=sqscr6[:, 0:2048], op=ALU.add), inc=s_dv)
            ln_guard[0] = s_dv.n
            P.wait("tensor", s_dv, v_apply)
            if done_dv_z2[sl]:
                P.wait("tensor", s_dv, done_dv_z2[sl])
            for ic in range(NI):
                P.op("tensor", lambda e, ic=ic, sl=sl: e.transpose(
                    psT2[sl][:, ic * 128:(ic + 1) * 128],
                    z2tm[sl][:, ic * 128:(ic + 1) * 128], ident[:]),
                    inc=s_pe if ic == NI - 1 else None)
            done_pe_T2[sl] = s_pe.n
            P.wait("vector", s_pe, done_pe_T2[sl])
            zv = z2f[:].rearrange("p (a b) -> p a b", a=NI)[:, :, tt * 128:(tt + 1) * 128]
            P.op("vector", lambda e, sl=sl, zv=zv: e.tensor_copy(
                zv, psT2[sl][:].rearrange("p (a b) -> p a b", a=NI)), inc=s_dv)
            done_dv_z2[sl] = s_dv.n
        A.free("sqscr6")
        A.free("z2tm1", "z2tm0")
        A.free("psT21", "psT20")
    v_p6_done = s_dv.n
    if upto == 6:
        dbg_dump("X2", X2[:])
        dbg_dump("z2f", z2f[:])

    done_sgu_use_g = [0, 0]
    done_sd_use_g = [0, 0]
    HSH = A.sbuf("HSH", [128, NIO * 1024], BF16)
    P.label = "P7"
    if upto >= 7:
        psR = [A.psum(f"psR{i}", [128, 512], F32) for i in range(2)]
        psG = [A.psum(f"psG{i}", [128, 512], F32) for i in range(2)]
        psU = [A.psum(f"psU{i}", [128, 512], F32) for i in range(2)]
        psD = [A.psum(f"psD{i}", [128, 512], F32) for i in range(2)]
        P.wait("tensor", s_dv, v_p6_done)
        P.wait("tensor", s_gx[3], g_rw)
        done_dv_r = [0, 0]
        for tt in range(NQT):
            rsl = tt % 2
            if done_dv_r[rsl]:
                P.wait("tensor", s_dv, done_dv_r[rsl])
            for hc in range(NI):
                P.op("tensor", lambda e, tt=tt, hc=hc, rsl=rsl: e.matmul(
                    psR[rsl][:, 0:8],
                    z2f[:, hc * 1024 + tt * 128: hc * 1024 + (tt + 1) * 128],
                    rw_sb[:, hc * 8:(hc + 1) * 8],
                    start=(hc == 0), stop=(hc == NI - 1)),
                    inc=s_pe if hc == NI - 1 else None)
            pe_r = s_pe.n
            P.wait("vector", s_pe, pe_r)
            P.op("vector", lambda e, tt=tt, rsl=rsl: e.tensor_copy(
                LG[:, tt * 8:(tt + 1) * 8], psR[rsl][:, 0:8]), inc=s_dv)
            done_dv_r[rsl] = s_dv.n
        chain("vector", s_dv)
        LG3 = LG[:].rearrange("p (a b) -> p a b", a=8)
        CW3 = combW[:].rearrange("p (a b) -> p a b", a=8)
        Rr = lambda c0: rscr[:, c0 * 8:c0 * 8 + 8]
        R3 = lambda c0: rscr[:, 8 + c0 * 64:8 + c0 * 64 + 64].rearrange(
            "p (a b) -> p a b", a=8)
        bc = lambda ap: ap.unsqueeze(2).broadcast_to([128, 8, 8])
        P.op("vector", lambda e: e.tensor_reduce(
            out=Rr(0), in_=LG3, axis=AX.X, op=ALU.max), inc=s_dv)
        chain("vector", s_dv)
        P.op("vector", lambda e: e.tensor_tensor(
            out=R3(0), in0=LG3, in1=bc(Rr(0)), op=ALU.subtract), inc=s_dv)
        v_sub = s_dv.n
        P.wait("scalar", s_dv, v_sub)
        P.op("scalar", lambda e: e.activation(
            out=rscr[:, 8:72], in_=rscr[:, 8:72], func=ACT.Exp), inc=s_sc)
        sc_e = s_sc.n
        P.wait("vector", s_sc, sc_e)
        P.op("vector", lambda e: e.tensor_reduce(
            out=Rr(0), in_=R3(0), axis=AX.X, op=ALU.add), inc=s_dv)
        chain("vector", s_dv)
        P.op("vector", lambda e: e.reciprocal(out=Rr(0), in_=Rr(0)), inc=s_dv)
        chain("vector", s_dv)
        P.op("vector", lambda e: e.tensor_tensor(
            out=R3(0), in0=R3(0), in1=bc(Rr(0)), op=ALU.mult), inc=s_dv)
        chain("vector", s_dv)
        P.op("vector", lambda e: e.tensor_reduce(
            out=Rr(0), in_=R3(0), axis=AX.X, op=ALU.max), inc=s_dv)
        chain("vector", s_dv)
        P.op("vector", lambda e: e.tensor_tensor(
            out=R3(1), in0=R3(0), in1=bc(Rr(0)), op=ALU.is_ge), inc=s_dv)
        chain("vector", s_dv)
        P.op("vector", lambda e: e.tensor_scalar(
            out=R3(2), in0=R3(1), scalar1=-1.0, scalar2=1.0,
            op0=ALU.mult, op1=ALU.add), inc=s_dv)
        chain("vector", s_dv)
        P.op("vector", lambda e: e.tensor_tensor(
            out=R3(3), in0=R3(0), in1=R3(2), op=ALU.mult), inc=s_dv)
        chain("vector", s_dv)
        P.op("vector", lambda e: e.tensor_reduce(
            out=Rr(0), in_=R3(3), axis=AX.X, op=ALU.max), inc=s_dv)
        chain("vector", s_dv)
        P.op("vector", lambda e: e.tensor_tensor(
            out=R3(4), in0=R3(3), in1=bc(Rr(0)), op=ALU.is_ge), inc=s_dv)
        chain("vector", s_dv)
        P.op("vector", lambda e: e.tensor_tensor(
            out=R3(1), in0=R3(1), in1=R3(4), op=ALU.add), inc=s_dv)
        chain("vector", s_dv)
        P.op("vector", lambda e: e.tensor_tensor(
            out=CW3, in0=R3(0), in1=R3(1), op=ALU.mult), inc=s_dv)
        v_comb = s_dv.n
        if "combW" in d_dbg:
            dbg_dump("combW", combW[:])
            dbg_dump("LG", LG[:])

        done_sgu_use_g[0] = done_sgu_use_g[1] = 0
        done_sgu_use = done_sgu_use_g
        done_gu_cons = [[0, 0], [0, 0]]
        for c in range(NIO):
            sl = c % 2
            if c + 1 < NIO:
                osl = (c + 1) % 2
                if done_sgu_use[osl]:
                    P.wait("gpsimd", s_pe, done_sgu_use[osl])
                gsem = 4 if osl == 0 else 6
                P.op("gpsimd", lambda e, c=c, osl=osl: e.dma_start(
                    out=sgu_sb[osl][:, 0:NI * 128],
                    in_=d_sg[c + 1].rearrange("p a b -> p (a b)")),
                    inc=s_gx[gsem], k=16)
                P.op("gpsimd", lambda e, c=c, osl=osl: e.dma_start(
                    out=sgu_sb[osl][:, NI * 128:],
                    in_=d_su[c + 1].rearrange("p a b -> p (a b)")),
                    inc=s_gx[gsem], k=16)
                sgu_ready[c + 1] = (gsem, s_gx[gsem].n)
            P.wait("tensor", s_gx[sgu_ready[c][0]], sgu_ready[c][1])
            for half in range(2):
                if done_gu_cons[half][0]:
                    P.wait("tensor", s_sc, done_gu_cons[half][0])
                    P.wait("tensor", s_dv, done_gu_cons[half][1])
                for hc in range(NI):
                    P.op("tensor", lambda e, sl=sl, hc=hc, half=half: e.matmul(
                        psG[half][:],
                        sgu_sb[sl][:, hc * 128:(hc + 1) * 128],
                        z2f[:, hc * 1024 + half * 512: hc * 1024 + half * 512 + 512],
                        start=(hc == 0), stop=(hc == NI - 1)), inc=None)
                    P.op("tensor", lambda e, sl=sl, hc=hc, half=half: e.matmul(
                        psU[half][:],
                        sgu_sb[sl][:, NI * 128 + hc * 128: NI * 128 + (hc + 1) * 128],
                        z2f[:, hc * 1024 + half * 512: hc * 1024 + half * 512 + 512],
                        start=(hc == 0), stop=(hc == NI - 1)),
                        inc=s_pe if hc == NI - 1 else None)
                pe_gu = s_pe.n
                P.wait("scalar", s_pe, pe_gu)
                P.op("scalar", lambda e, half=half: e.activation(
                    out=sgb[:, half * 512:(half + 1) * 512], in_=psG[half][:],
                    func=ACT.Sigmoid), inc=s_sc)
                sc_sg = s_sc.n
                P.wait("vector", s_sc, sc_sg)
                P.op("vector", lambda e, c=c, half=half: e.scalar_tensor_tensor(
                    out=HSH[:, c * 1024 + half * 512: c * 1024 + half * 512 + 512],
                    in0=psU[half][:], scalar=1.0,
                    in1=sgb[:, half * 512:(half + 1) * 512],
                    op0=ALU.mult, op1=ALU.mult), inc=s_dv)
                done_gu_cons[half] = [sc_sg, s_dv.n]
            done_sgu_use[sl] = s_pe.n
        v_hsh = s_dv.n

        done_dv_d = [0, 0]
        done_sd_use_g[0] = done_sd_use_g[1] = 0
        done_sd_use = done_sd_use_g
        for hoq in range(4):
            sl = hoq % 2
            if hoq + 1 < 4:
                osl = (hoq + 1) % 2
                if done_sd_use[osl]:
                    P.wait("gpsimd", s_pe, done_sd_use[osl])
                dsem = 5 if osl == 0 else 7
                P.op("gpsimd", lambda e, hoq=hoq, osl=osl: e.dma_start(
                    out=sd_sb[osl][:],
                    in_=d_sd[hoq + 1].rearrange("p a b -> p (a b)")),
                    inc=s_gx[dsem], k=16)
                sd_ready[hoq + 1] = (dsem, s_gx[dsem].n)
            P.wait("tensor", s_gx[sd_ready[hoq][0]], sd_ready[hoq][1])
            P.wait("tensor", s_dv, v_hsh)
            for tt in range(NQT):
                dsl = (hoq * NQT + tt) % 2
                if done_dv_d[dsl]:
                    P.wait("tensor", s_dv, done_dv_d[dsl])
                for c in range(NIO):
                    P.op("tensor", lambda e, c=c, tt=tt, sl=sl, dsl=dsl: e.matmul(
                        psD[dsl][:],
                        HSH[:, c * 1024 + tt * 128: c * 1024 + (tt + 1) * 128],
                        sd_sb[sl][:, c * 512:(c + 1) * 512],
                        start=(c == 0), stop=(c == NIO - 1)),
                        inc=s_pe if c == NIO - 1 else None)
                pe_d = s_pe.n
                done_sd_use[sl] = pe_d
                P.wait("vector", s_pe, pe_d)
                P.op("vector", lambda e, tt=tt, hoq=hoq, dsl=dsl: e.tensor_tensor(
                    out=X2[:, tt * 2048 + hoq * 512: tt * 2048 + (hoq + 1) * 512],
                    in0=X2[:, tt * 2048 + hoq * 512: tt * 2048 + (hoq + 1) * 512],
                    in1=psD[dsl][:], op=ALU.add), inc=s_dv)
                done_dv_d[dsl] = s_dv.n
    v_p7_done = s_dv.n
    if upto == 7:
        dbg_dump("X2", X2[:])
    A.free("HSH")
    A.free("sd_sb1", "sd_sb0", "sgu_sb1", "sgu_sb0")

    P.label = "P8"
    if upto >= 8:
        HE = [A.sbuf(f"HE{i}", [128, NIO * 1024], BF16) for i in range(2)]
        wgu_sb = [A.sbuf(f"wgu_sb{i}", [128, 2 * NI * 128], BF16) for i in range(2)]
        wd_sb = [A.sbuf(f"wd_sb{i}", [128, NIO * 512], BF16) for i in range(2)]
        wgu_ready = {}
        wd_ready = {}
        done_wgu_use = list(done_sgu_use_g)
        done_wd_use = list(done_sd_use_g)
        done_he_read = [0, 0]
        done_gu_cons2 = [[0, 0], [0, 0]]
        done_dv_d2 = [0, 0]

        def prefetch_gu(idx):
            if idx >= NE * NIO:
                return
            ne_, nc_ = divmod(idx, NIO)
            osl = idx % 2
            if done_wgu_use[osl]:
                P.wait("gpsimd", s_pe, done_wgu_use[osl])
                done_wgu_use[osl] = 0
            gsem = 4 if osl == 0 else 6
            P.op("gpsimd", lambda e, ne_=ne_, nc_=nc_, osl=osl: e.dma_start(
                out=wgu_sb[osl][:, 0:NI * 128],
                in_=d_wg[ne_, nc_].rearrange("p a b -> p (a b)")),
                inc=s_gx[gsem], k=16)
            P.op("gpsimd", lambda e, ne_=ne_, nc_=nc_, osl=osl: e.dma_start(
                out=wgu_sb[osl][:, NI * 128:],
                in_=d_wu[ne_, nc_].rearrange("p a b -> p (a b)")),
                inc=s_gx[gsem], k=16)
            wgu_ready[idx] = (gsem, s_gx[gsem].n)

        def prefetch_wd(idx):
            if idx >= NE * 4:
                return
            ne_, nq_ = divmod(idx, 4)
            osl = idx % 2
            if done_wd_use[osl]:
                P.wait("gpsimd", s_pe, done_wd_use[osl])
                done_wd_use[osl] = 0
            dsem = 5 if osl == 0 else 7
            P.op("gpsimd", lambda e, ne_=ne_, nq_=nq_, osl=osl: e.dma_start(
                out=wd_sb[osl][:],
                in_=d_wd[ne_, nq_].rearrange("p a b -> p (a b)")),
                inc=s_gx[dsem], k=16)
            wd_ready[idx] = (dsem, s_gx[dsem].n)

        P.wait("gpsimd", s_pe, s_pe.n)
        P.wait("gpsimd", s_dv, s_dv.n)
        prefetch_gu(0)
        prefetch_wd(0)
        for e_i in range(NE):
            esl = e_i % 2
            if done_he_read[esl]:
                P.wait("vector", s_pe, done_he_read[esl])
            for c in range(NIO):
                gidx = e_i * NIO + c
                sl = gidx % 2
                prefetch_gu(gidx + 1)
                P.wait("tensor", s_gx[wgu_ready[gidx][0]], wgu_ready[gidx][1])
                for half in range(2):
                    if done_gu_cons2[half][0]:
                        P.wait("tensor", s_sc, done_gu_cons2[half][0])
                        P.wait("tensor", s_dv, done_gu_cons2[half][1])
                    for hc in range(NI):
                        P.op("tensor", lambda e, sl=sl, hc=hc, half=half: e.matmul(
                            psG[half][:],
                            wgu_sb[sl][:, hc * 128:(hc + 1) * 128],
                            z2f[:, hc * 1024 + half * 512: hc * 1024 + half * 512 + 512],
                            start=(hc == 0), stop=(hc == NI - 1)), inc=None)
                        P.op("tensor", lambda e, sl=sl, hc=hc, half=half: e.matmul(
                            psU[half][:],
                            wgu_sb[sl][:, NI * 128 + hc * 128: NI * 128 + (hc + 1) * 128],
                            z2f[:, hc * 1024 + half * 512: hc * 1024 + half * 512 + 512],
                            start=(hc == 0), stop=(hc == NI - 1)),
                            inc=s_pe if hc == NI - 1 else None)
                    pe_gu = s_pe.n
                    P.wait("scalar", s_pe, pe_gu)
                    P.op("scalar", lambda e, half=half: e.activation(
                        out=sgb[:, half * 512:(half + 1) * 512],
                        in_=psG[half][:], func=ACT.Sigmoid), inc=s_sc)
                    sc_sg = s_sc.n
                    P.wait("vector", s_sc, sc_sg)
                    P.op("vector", lambda e, c=c, half=half, esl=esl: e.scalar_tensor_tensor(
                        out=HE[esl][:, c * 1024 + half * 512: c * 1024 + half * 512 + 512],
                        in0=psU[half][:], scalar=1.0,
                        in1=sgb[:, half * 512:(half + 1) * 512],
                        op0=ALU.mult, op1=ALU.mult), inc=s_dv)
                    done_gu_cons2[half] = [sc_sg, s_dv.n]
                done_wgu_use[sl] = s_pe.n
            v_he = s_dv.n
            for hoq in range(4):
                didx = e_i * 4 + hoq
                sl = didx % 2
                prefetch_wd(didx + 1)
                P.wait("tensor", s_gx[wd_ready[didx][0]], wd_ready[didx][1])
                P.wait("tensor", s_dv, v_he)
                for tt in range(NQT):
                    dsl = tt % 2
                    if done_dv_d2[dsl]:
                        P.wait("tensor", s_dv, done_dv_d2[dsl])
                    for c in range(NIO):
                        P.op("tensor", lambda e, c=c, tt=tt, sl=sl, dsl=dsl,
                             esl=esl: e.matmul(
                                 psD[dsl][:],
                                 HE[esl][:, c * 1024 + tt * 128: c * 1024 + (tt + 1) * 128],
                                 wd_sb[sl][:, c * 512:(c + 1) * 512],
                                 start=(c == 0), stop=(c == NIO - 1)),
                             inc=s_pe if c == NIO - 1 else None)
                    pe_d = s_pe.n
                    done_wd_use[sl] = pe_d
                    P.wait("vector", s_pe, pe_d)
                    P.op("vector", lambda e, tt=tt, hoq=hoq, dsl=dsl, e_i=e_i: e.scalar_tensor_tensor(
                        out=X2[:, tt * 2048 + hoq * 512: tt * 2048 + (hoq + 1) * 512],
                        in0=psD[dsl][:],
                        scalar=combW[:, tt * 8 + e_i: tt * 8 + e_i + 1],
                        in1=X2[:, tt * 2048 + hoq * 512: tt * 2048 + (hoq + 1) * 512],
                        op0=ALU.mult, op1=ALU.add), inc=s_dv)
                    done_dv_d2[dsl] = s_dv.n
                done_he_read[esl] = s_pe.n

    if upto >= 8:
        v_final = s_dv.n
        P.wait("sync", s_dv, v_final)
        P.op("sync", lambda e: e.dma_start(
            out=d_out.rearrange("(a p) d -> p a d", p=128),
            in_=X2[:].rearrange("p (a d) -> p a d", a=NQT)), inc=s_out, k=16)
    if s_out.n:
        P.wait("sync", s_out, s_out.n)

    block = es.enter_context(nc.Block())
    thunks = P.thunks

    def replay(eng_name, e):
        cur = None
        sid = None
        for label, th in thunks[eng_name]:
            if label != cur:
                if cur is not None:
                    nc.leave_named_scope(f"{cur}.{eng_name}", sid, False)
                cur = label
                sid, _ = nc.enter_named_scope(f"{cur}.{eng_name}", False)
            th(e)
        if cur is not None:
            nc.leave_named_scope(f"{cur}.{eng_name}", sid, False)

    @block.sync
    def _(e):
        replay("sync", e)

    @block.scalar
    def _(e):
        replay("scalar", e)

    @block.vector
    def _(e):
        replay("vector", e)

    @block.tensor
    def _(e):
        replay("tensor", e)

    @block.gpsimd
    def _(e):
        replay("gpsimd", e)

    es.close()
    A.close()
    return nc


_NC_CACHE = {}


def _get_nc(upto=99, debug=()):
    key = (upto, tuple(debug))
    if key not in _NC_CACHE:
        _NC_CACHE[key] = _build_program(upto=upto, debug=debug)
    return _NC_CACHE[key]


def _host_layout(inputs):
    x = np.ascontiguousarray(np.asarray(inputs["L0_attn_x"]))
    wq = np.asarray(inputs["L0_attn_wq"])
    wk = np.asarray(inputs["L0_attn_wk"])
    wv = np.asarray(inputs["L0_attn_wv"])
    wp = np.asarray(inputs["L0_attn_wp"])
    qg = np.asarray(inputs["L0_attn_qg"])
    rw = np.asarray(inputs["L0_ffn_router_w"])
    sg = np.asarray(inputs["L0_ffn_shared_gate"])
    su = np.asarray(inputs["L0_ffn_shared_up"])
    sd = np.asarray(inputs["L0_ffn_shared_down"])
    wg = np.asarray(inputs["L0_ffn_w_gate"])
    wu = np.asarray(inputs["L0_ffn_w_up"])
    wd = np.asarray(inputs["L0_ffn_w_down"])

    rw_r = np.ascontiguousarray(rw.reshape(NE, NI, 128).transpose(2, 1, 0))
    sg_r = np.ascontiguousarray(sg.reshape(NIO, 128, NI, 128).transpose(0, 3, 2, 1))
    su_r = np.ascontiguousarray(su.reshape(NIO, 128, NI, 128).transpose(0, 3, 2, 1))
    sd_r = np.ascontiguousarray(sd.reshape(4, 512, NIO, 128).transpose(0, 3, 2, 1))
    wg_r = np.ascontiguousarray(wg.reshape(NE, NIO, 128, NI, 128).transpose(0, 1, 4, 3, 2))
    wu_r = np.ascontiguousarray(wu.reshape(NE, NIO, 128, NI, 128).transpose(0, 1, 4, 3, 2))
    wd_r = np.ascontiguousarray(wd.reshape(NE, 4, 512, NIO, 128).transpose(0, 1, 4, 3, 2))

    in_maps = []
    for c in range(8):
        b, g = c // 4, c % 4
        wq_s = wq[QO * g:QO * (g + 1), :]
        kv_s = np.concatenate([wk[128 * g:128 * (g + 1), :],
                               wv[128 * g:128 * (g + 1), :]], axis=0)
        wq_r = np.ascontiguousarray(wq_s.reshape(QO, NI, 128).transpose(2, 1, 0))
        kv_r = np.ascontiguousarray(kv_s.reshape(KVO, NI, 128).transpose(2, 1, 0))
        wp_s = wp[:, QO * g:QO * (g + 1)]
        wp_r = np.ascontiguousarray(wp_s.reshape(D, 4, 128).transpose(2, 1, 0))
        in_maps.append({
            "xf": np.ascontiguousarray(x[b]),
            "xo": np.ascontiguousarray(x[b, 1024 * g:1024 * (g + 1)]),
            "qgg": np.ascontiguousarray(qg[NHL * g:NHL * (g + 1)].reshape(1, NHL)),
            "wq_r": wq_r, "kv_r": kv_r, "wp_r": wp_r,
            "rw_r": rw_r, "sg_r": sg_r, "su_r": su_r, "sd_r": sd_r,
            "wg_r": wg_r, "wu_r": wu_r, "wd_r": wd_r,
        })
    return in_maps


def kernel(**inputs):
    nc = _get_nc()
    in_maps = _host_layout(inputs)
    res = run_bass_kernel_spmd(nc, in_maps, core_ids=list(range(8)))
    out = np.empty((B, T, D), np.float32)
    for c in range(8):
        b, g = c // 4, c % 4
        out[b, 1024 * g:1024 * (g + 1)] = res.results[c]["out"]
    return out


# version 12
# speedup vs baseline: 1.2118x; 1.1759x over previous
import math
import os
from contextlib import ExitStack

import numpy as np

import concourse.bass as bass
import concourse.mybir as mybir
from concourse.bass_utils import run_bass_kernel_spmd

F32 = mybir.dt.float32
BF16 = mybir.dt.bfloat16
I32 = mybir.dt.int32
ALU = mybir.AluOpType
ACT = mybir.ActivationFunctionType
AX = mybir.AxisListType

B, T, D = 2, 4096, 2048
NT = T // 128
NQT = 8
NI = D // 128
QO = 512
KVO = 256
NHL = 4
IO = 1408
NIO = IO // 128
NE = 8
EPS_LN = 1e-5
EPS_RMS = float(np.finfo(np.float32).eps)
SQRT_HD = math.sqrt(128.0)
TWO_PI = float(2.0 * math.pi)
PI = float(math.pi)
LOG_BASE = float(math.log(10000.0))

ENGINES = ("sync", "scalar", "vector", "tensor", "gpsimd")


class Sem:
    def __init__(self, es, nc, name):
        self.h = es.enter_context(nc.semaphore(name))
        self.n = 0


class Alloc:

    def __init__(self, nc):
        self.nc = nc
        self.live = {}
        self.order = []

    def sbuf(self, name, shape, dt, side="left"):
        g = self.nc.sbuf_tensor(name, shape, dt, side=side)
        h = g.__enter__()
        self.live[name] = g
        self.order.append(name)
        return h

    def psum(self, name, shape, dt, side="left"):
        g = self.nc.psum_tensor(name, shape, dt, side=side)
        h = g.__enter__()
        self.live[name] = g
        self.order.append(name)
        return h

    def free(self, *names):
        for n in names:
            self.live.pop(n).__exit__(None, None, None)
            self.order.remove(n)

    def close(self):
        for n in reversed(self.order):
            self.live.pop(n).__exit__(None, None, None)
        self.order = []


class Prog:
    def __init__(self):
        self.thunks = {e: [] for e in ENGINES}
        self.label = "P0"

    def emit(self, eng, f):
        self.thunks[eng].append((self.label, f))

    def wait(self, eng, sem, val=None):
        v = sem.n if val is None else val
        if v > 0:
            self.emit(eng, lambda e, h=sem.h, v=v: e.wait_ge(h, v))

    def op(self, eng, builder, inc=None, k=1):
        if inc is not None:
            inc.n += k

        def th(e, b=builder, s=inc, kk=k):
            ins = b(e)
            if s is not None:
                ins.then_inc(s.h, kk)

        self.emit(eng, th)
        return inc.n if inc is not None else None


def _build_program(upto=99, debug=()):
    nc = bass.Bass("TRN2", target_bir_lowering=False)

    d_xf = nc.dram_tensor("xf", [T, D], F32, kind="ExternalInput").ap()
    d_xo = nc.dram_tensor("xo", [1024, D], F32, kind="ExternalInput").ap()
    d_qg = nc.dram_tensor("qgg", [1, NHL], F32, kind="ExternalInput").ap()
    d_wq = nc.dram_tensor("wq_r", [128, NI, QO], F32, kind="ExternalInput").ap()
    d_kv = nc.dram_tensor("kv_r", [128, NI, KVO], F32, kind="ExternalInput").ap()
    d_wp = nc.dram_tensor("wp_r", [128, 4, D], F32, kind="ExternalInput").ap()
    d_rw = nc.dram_tensor("rw_r", [128, NI, NE], F32, kind="ExternalInput").ap()
    d_sg = nc.dram_tensor("sg_r", [NIO, 128, NI, 128], F32, kind="ExternalInput").ap()
    d_su = nc.dram_tensor("su_r", [NIO, 128, NI, 128], F32, kind="ExternalInput").ap()
    d_sd = nc.dram_tensor("sd_r", [4, 128, NIO, 512], F32, kind="ExternalInput").ap()
    d_wg = nc.dram_tensor("wg_r", [NE, NIO, 128, NI, 128], F32, kind="ExternalInput").ap()
    d_wu = nc.dram_tensor("wu_r", [NE, NIO, 128, NI, 128], F32, kind="ExternalInput").ap()
    d_wd = nc.dram_tensor("wd_r", [NE, 4, 128, NIO, 512], F32, kind="ExternalInput").ap()
    d_prs = nc.dram_tensor("prs", [4, 4, 256, D], BF16).ap()
    d_pq = nc.dram_tensor("pq", [1024, D], BF16).ap()
    d_out = nc.dram_tensor("out", [1024, D], F32, kind="ExternalOutput").ap()
    d_dbg = {}
    for name, shape, dt in debug:
        d_dbg[name] = nc.dram_tensor("dbg_" + name, list(shape), dt,
                                     kind="ExternalOutput").ap()

    es = ExitStack()
    A = Alloc(nc)

    cpool = A.sbuf("cpool", [128, 8], F32)
    ident = A.sbuf("ident", [128, 128], BF16)
    onesb = A.sbuf("onesb", [128, 8], BF16)
    onesf32 = A.sbuf("onesf32", [1, 128], F32)
    bcast16 = A.sbuf("bcast16", [128, 16], F32)
    srow = A.sbuf("srow", [128, 16], F32)
    rp = A.sbuf("rp", [128, 96], F32)
    qgrow = A.sbuf("qgrow", [1, NHL], F32)
    lrow = A.sbuf("lrow", [1, 512], F32)
    rrow = A.sbuf("rrow", [1, 512], F32)
    rbc = A.sbuf("rbc", [128, 512], F32)
    LG = A.sbuf("LG", [128, 64], F32)
    MSKf = A.sbuf("MSKf", [128, 64], F32)
    MSKb = A.sbuf("MSKb", [128, 64], BF16)
    onesrow = A.sbuf("onesrow", [1, 128], BF16)
    combW = A.sbuf("combW", [128, 64], F32)
    rscr = A.sbuf("rscr", [128, 336], F32)
    sgb = A.sbuf("sgb", [128, 1024], F32)
    masks = A.sbuf("masks", [128, 4 * 512], BF16, side="right")
    cosT = A.sbuf("cosT", [128, NT * 8], F32, side="right")
    sinT = A.sbuf("sinT", [128, NT * 8], F32, side="right")
    tbl = A.sbuf("tbl", [128, 320], F32, side="right")
    tbli = A.sbuf("tbli", [128, 256], I32, side="right")
    tscr = A.sbuf("tscr", [128, 512], F32, side="right")

    CONSTS = {0.0: 0, 1.0: 1, EPS_LN: 2, EPS_RMS: 3,
              -LOG_BASE / 8.0: 4}
    for val, i in CONSTS.items():
        nc.const_aps.register(F32, val, cpool[:, i:i + 1])

    P = Prog()
    s_pe = Sem(es, nc, "s_pe")
    s_dv = Sem(es, nc, "s_dv")
    s_sc = Sem(es, nc, "s_sc")
    s_gp = Sem(es, nc, "s_gp")
    s_cc = Sem(es, nc, "s_cc")
    s_sx = [Sem(es, nc, f"s_sx{i}") for i in range(4)]
    s_gx = [Sem(es, nc, f"s_gx{i}") for i in range(8)]
    s_out = Sem(es, nc, "s_out")

    def chain(eng, sem):
        P.wait(eng, sem)

    def dbg_dump(name, src_ap):
        if name in d_dbg:
            for s in (s_dv, s_pe, s_sc, s_gp):
                P.wait("sync", s)
            P.op("sync", lambda e, d=d_dbg[name], s=src_ap: e.dma_start(
                out=d, in_=s), inc=s_out, k=16)

    for val, i in CONSTS.items():
        P.op("vector", lambda e, a=cpool[:, i:i + 1], v=val: e.memset(a, v))
    P.op("vector", lambda e: e.memset(srow[:], 0.0))
    P.op("vector", lambda e: e.memset(onesf32[:], 1.0))
    P.op("vector", lambda e: e.memset(onesrow[:], 1.0))
    P.op("vector", lambda e: e.memset(rp[:], 0.0))
    v_consts = P.op("vector", lambda e: e.memset(tbl[:], 0.0), inc=s_dv)

    P.op("gpsimd", lambda e: e.memset(onesb[:], 1.0), inc=s_gp)
    chain("gpsimd", s_gp)
    P.op("gpsimd", lambda e: e.affine_select(
        out=ident[:], in_=onesb[:, 0:1].broadcast_to([128, 128]),
        pattern=[[1, 128]], compare_op=ALU.is_equal, fill=0.0,
        base=0, channel_multiplier=-1), inc=s_gp)
    chain("gpsimd", s_gp)
    for j in range(4):
        P.op("gpsimd", lambda e, j=j: e.affine_select(
            out=masks[:, j * 512:(j + 1) * 512],
            in_=onesb[:, 0:1].broadcast_to([128, 512]),
            pattern=[[1, 512]], compare_op=ALU.is_ge, fill=0.0,
            base=-128 * j, channel_multiplier=-1), inc=s_gp)
        chain("gpsimd", s_gp)
    P.wait("gpsimd", s_dv, v_consts)
    P.op("gpsimd", lambda e: e.iota(
        tbl[:, 0:32], pattern=[[128, 32]], base=0, channel_multiplier=1,
        allow_small_or_imprecise_dtypes=True), inc=s_gp)
    P.op("gpsimd", lambda e: e.iota(
        tbl[0:1, 32:40], pattern=[[1, 8]], base=0, channel_multiplier=0,
        allow_small_or_imprecise_dtypes=True), inc=s_gp)
    g_iota = s_gp.n

    P.wait("scalar", s_gp, g_iota)
    P.wait("scalar", s_dv, v_consts)
    sc_inv = P.op("scalar", lambda e: e.activation(
        out=tbl[0:1, 48:56], in_=tbl[0:1, 32:40], func=ACT.Exp,
        scale=-LOG_BASE / 8.0), inc=s_sc)
    P.op("sync", lambda e: e.dma_start(out=qgrow[:], in_=d_qg), inc=s_sx[2], k=16)
    P.wait("scalar", s_sx[2], 16)
    sc_absqg = P.op("scalar", lambda e: e.activation(
        out=tbl[0:1, 56:60], in_=qgrow[:], func=ACT.Abs), inc=s_sc)
    P.wait("vector", s_sc, sc_absqg)
    P.op("vector", lambda e: e.tensor_scalar(
        out=tbl[0:1, 56:60], in0=tbl[0:1, 56:60], scalar1=-SQRT_HD,
        scalar2=None, op0=ALU.mult), inc=s_dv)
    P.op("vector", lambda e: e.tensor_scalar(
        out=tbl[0:1, 60:64], in0=qgrow[:], scalar1=1.0 / SQRT_HD,
        scalar2=None, op0=ALU.mult), inc=s_dv)
    v_qg = s_dv.n
    ps0 = A.psum("ps0tmp", [128, 512], F32)
    P.wait("tensor", s_dv, v_qg)
    P.wait("tensor", s_sc, sc_inv)
    pe_bc = P.op("tensor", lambda e: e.matmul(
        ps0[:, 0:16], onesf32[0:1, :], tbl[0:1, 48:64],
        start=True, stop=True), inc=s_pe)
    P.wait("vector", s_pe, pe_bc)
    v_bc16 = P.op("vector", lambda e: e.tensor_copy(
        bcast16[:], ps0[:, 0:16]), inc=s_dv)
    chain("vector", s_dv)

    fv = tbl[:, 64:320]
    fv3 = fv.rearrange("p (a b) -> p a b", a=NT)
    P.op("vector", lambda e: e.tensor_tensor(
        out=fv3, in0=tbl[:, 0:32].unsqueeze(2).broadcast_to([128, NT, 8]),
        in1=bcast16[:, 0:8].unsqueeze(1).broadcast_to([128, NT, 8]),
        op=ALU.mult), inc=s_dv)
    chain("vector", s_dv)
    kf = tscr[:, 0:256]
    gt = tscr[:, 256:512]
    for dst, shift in ((sinT, 0.0), (cosT, PI / 2.0)):
        if shift != 0.0:
            P.op("vector", lambda e, s=shift: e.tensor_scalar(
                out=kf, in0=fv, scalar1=s, scalar2=float(1.0 / TWO_PI),
                op0=ALU.add, op1=ALU.mult), inc=s_dv)
        else:
            P.op("vector", lambda e: e.tensor_scalar(
                out=kf, in0=fv, scalar1=float(1.0 / TWO_PI), scalar2=None,
                op0=ALU.mult), inc=s_dv)
        chain("vector", s_dv)
        P.op("vector", lambda e: e.tensor_copy(tbli[:], kf), inc=s_dv)
        chain("vector", s_dv)
        P.op("vector", lambda e: e.tensor_copy(kf, tbli[:]), inc=s_dv)
        chain("vector", s_dv)
        if shift != 0.0:
            P.op("vector", lambda e, s=shift: e.tensor_scalar(
                out=gt, in0=fv, scalar1=s, scalar2=None, op0=ALU.add),
                inc=s_dv)
            chain("vector", s_dv)
            P.op("vector", lambda e, d=dst: e.scalar_tensor_tensor(
                out=d[:], in0=kf, scalar=-TWO_PI, in1=gt,
                op0=ALU.mult, op1=ALU.add), inc=s_dv)
        else:
            P.op("vector", lambda e, d=dst: e.scalar_tensor_tensor(
                out=d[:], in0=kf, scalar=-TWO_PI, in1=fv,
                op0=ALU.mult, op1=ALU.add), inc=s_dv)
        chain("vector", s_dv)
        P.op("vector", lambda e, d=dst: e.tensor_scalar(
            out=gt, in0=d[:], scalar1=PI, scalar2=None, op0=ALU.is_ge),
            inc=s_dv)
        chain("vector", s_dv)
        P.op("vector", lambda e, d=dst: e.scalar_tensor_tensor(
            out=d[:], in0=gt, scalar=-TWO_PI, in1=d[:],
            op0=ALU.mult, op1=ALU.add), inc=s_dv)
        chain("vector", s_dv)
    v_angles = s_dv.n
    P.wait("scalar", s_dv, v_angles)
    for dst in (sinT, cosT):
        P.op("scalar", lambda e, d=dst: e.activation(
            out=d[:], in_=d[:], func=ACT.Sin), inc=s_sc)
        chain("scalar", s_sc)
    sc_tables = s_sc.n

    g_setup = s_gp.n
    v_setup = s_dv.n
    A.free("ps0tmp")
    A.free("tscr", "tbli", "tbl")

    YFM = A.sbuf("YFM", [128, NHL * T], BF16)
    KVTM = A.sbuf("KVTM", [128, NT * KVO], BF16)
    QFM = A.sbuf("QFM", [128, NHL * T], BF16)
    KFM = A.sbuf("KFM", [128, T], BF16)
    wq_sb = A.sbuf("wq_sb", [128, NI * QO], BF16)
    kv_sb = A.sbuf("kv_sb", [128, NI * KVO], BF16)
    P.op("gpsimd", lambda e: e.dma_start(
        out=wq_sb[:], in_=d_wq.rearrange("p a b -> p (a b)")), inc=s_gx[0], k=16)
    g_wq = s_gx[0].n
    P.op("gpsimd", lambda e: e.dma_start(
        out=kv_sb[:], in_=d_kv.rearrange("p a b -> p (a b)")), inc=s_gx[1], k=16)
    g_kv = s_gx[1].n

    QTM = A.sbuf("QTM", [128, NT * QO], BF16)
    xbuf = [A.sbuf(f"xbuf{i}", [128, 2048], F32) for i in range(2)]
    sq2 = [A.sbuf(f"sq2{i}", [128, 704], F32) for i in range(2)]
    z1tm = [A.sbuf(f"z1tm{i}", [128, 2048], BF16) for i in range(2)]
    z1f = [A.sbuf(f"z1f{i}", [128, 2048], BF16) for i in range(2)]
    psT = [A.psum(f"psT{i}", [128, 2048], BF16) for i in range(2)]
    psQ = [A.psum(f"psQ{i}", [128, 512], F32) for i in range(2)]
    psKV = [A.psum(f"psKV{i}", [128, 512], F32) for i in range(2)]

    P.wait("tensor", s_gx[0], g_wq)
    P.wait("tensor", s_gx[1], g_kv)
    P.wait("tensor", s_gp, g_setup)
    P.wait("tensor", s_dv, v_setup)
    P.wait("scalar", s_dv, v_angles)

    done_x_free = [0, 0]
    done_dv_sq = [0, 0]
    done_sc_z1f = [0, 0]
    done_sc_evac = [0, 0]
    done_pe_T = [0, 0]
    done_dv_z1f = [0, 0]
    done_pe_qkv = [0, 0]
    done_dv_evac = [0, 0]

    def emit_ln(tt, sl, src_tile, zdst, c0, sqbuf, prev_dv=0):
        P.op("vector", lambda e, s=src_tile, c=c0: e.tensor_reduce(
            out=srow[:, c:c + 1], in_=s, axis=AX.X, op=ALU.add), inc=s_dv)
        if prev_dv:
            P.wait("scalar", s_dv, prev_dv)
        chain("scalar", s_sc)
        P.op("scalar", lambda e, s=src_tile, c=c0, q=sqbuf: e.activation(
            out=q, in_=s, func=ACT.Square,
            accum_out=srow[:, c + 1:c + 2]), inc=s_sc)
        sc_sq = s_sc.n
        chain("vector", s_dv)
        P.op("vector", lambda e, c=c0: e.tensor_scalar(
            out=srow[:, c:c + 1], in0=srow[:, c:c + 1],
            scalar1=1.0 / 2048.0, scalar2=None, op0=ALU.mult), inc=s_dv)
        P.wait("vector", s_sc, sc_sq)
        P.op("vector", lambda e, c=c0: e.tensor_scalar(
            out=srow[:, c + 1:c + 2], in0=srow[:, c + 1:c + 2],
            scalar1=1.0 / 2048.0, scalar2=None, op0=ALU.mult), inc=s_dv)
        chain("vector", s_dv)
        P.op("vector", lambda e, c=c0: e.tensor_tensor(
            out=srow[:, c + 2:c + 3], in0=srow[:, c:c + 1],
            in1=srow[:, c:c + 1], op=ALU.mult), inc=s_dv)
        chain("vector", s_dv)
        P.op("vector", lambda e, c=c0: e.tensor_tensor(
            out=srow[:, c + 2:c + 3], in0=srow[:, c + 1:c + 2],
            in1=srow[:, c + 2:c + 3], op=ALU.subtract), inc=s_dv)
        v_var = s_dv.n
        P.wait("scalar", s_dv, v_var)
        P.op("scalar", lambda e, c=c0: e.activation(
            out=srow[:, c + 2:c + 3], in_=srow[:, c + 2:c + 3], func=ACT.Sqrt,
            bias=EPS_LN), inc=s_sc)
        sc_sd = s_sc.n
        P.wait("vector", s_sc, sc_sd)
        P.op("vector", lambda e, c=c0: e.reciprocal(
            out=srow[:, c + 2:c + 3], in_=srow[:, c + 2:c + 3]), inc=s_dv)
        P.op("vector", lambda e, c=c0: e.tensor_scalar(
            out=srow[:, c:c + 1], in0=srow[:, c:c + 1],
            scalar1=-1.0, scalar2=None, op0=ALU.mult), inc=s_dv)
        chain("vector", s_dv)
        P.op("vector", lambda e, s=src_tile, z=zdst, c=c0: e.tensor_scalar(
            out=z, in0=s, scalar1=srow[:, c:c + 1],
            scalar2=srow[:, c + 2:c + 3], op0=ALU.add, op1=ALU.mult),
            inc=s_dv)
        return s_dv.n

    P.label = "P1"
    apply_of = {}

    def p1_load_ln(tt):
        sl = tt % 2
        if done_x_free[sl]:
            P.wait("sync", s_dv, done_x_free[sl])
        P.op("sync", lambda e, tt=tt, sl=sl: e.dma_start(
            out=xbuf[sl][:], in_=d_xf[tt * 128:(tt + 1) * 128, :]),
            inc=s_sx[sl], k=16)
        v_x = s_sx[sl].n
        P.wait("vector", s_sx[sl], v_x)
        P.wait("scalar", s_sx[sl], v_x)
        if done_pe_T[sl]:
            P.wait("vector", s_pe, done_pe_T[sl])
            P.wait("scalar", s_pe, done_pe_T[sl])
        v_apply = emit_ln(tt, sl, xbuf[sl][:], z1tm[sl][:],
                          0 if sl == 0 else 4, z1tm[sl][:],
                          prev_dv=done_x_free[sl])
        done_x_free[sl] = v_apply
        apply_of[tt] = v_apply

    def p1_tile(tt):
        sl = tt % 2
        v_apply = apply_of[tt]
        P.wait("tensor", s_dv, v_apply)
        if done_dv_z1f[sl]:
            P.wait("tensor", s_dv, done_dv_z1f[sl])
        for ic in range(NI):
            P.op("tensor", lambda e, ic=ic, sl=sl: e.transpose(
                psT[sl][:, ic * 128:(ic + 1) * 128],
                z1tm[sl][:, ic * 128:(ic + 1) * 128], ident[:]),
                inc=s_pe if ic == NI - 1 else None)
        done_pe_T[sl] = s_pe.n
        P.wait("vector", s_pe, done_pe_T[sl])
        if done_pe_qkv[sl]:
            P.wait("vector", s_pe, done_pe_qkv[sl])
        P.op("vector", lambda e, sl=sl: e.tensor_copy(
            z1f[sl][:], psT[sl][:]), inc=s_dv)
        done_sc_z1f[sl] = s_dv.n
        P.wait("tensor", s_dv, done_sc_z1f[sl])
        if done_sc_evac[sl]:
            P.wait("tensor", s_dv, done_sc_evac[sl])
        for ic in range(NI):
            P.op("tensor", lambda e, ic=ic, sl=sl: e.matmul(
                psQ[sl][:], z1f[sl][:, ic * 128:(ic + 1) * 128],
                wq_sb[:, ic * QO:(ic + 1) * QO],
                start=(ic == 0), stop=(ic == NI - 1)), inc=None)
            P.op("tensor", lambda e, ic=ic, sl=sl: e.matmul(
                psKV[sl][:, 0:KVO], z1f[sl][:, ic * 128:(ic + 1) * 128],
                kv_sb[:, ic * KVO:(ic + 1) * KVO],
                start=(ic == 0), stop=(ic == NI - 1)),
                inc=s_pe if ic == NI - 1 else None)
        done_pe_qkv[sl] = s_pe.n
        P.wait("vector", s_pe, done_pe_qkv[sl])
        P.op("vector", lambda e, tt=tt, sl=sl: e.tensor_copy(
            QTM[:, tt * QO:(tt + 1) * QO], psQ[sl][:]), inc=s_dv)
        P.op("vector", lambda e, tt=tt, sl=sl: e.tensor_copy(
            KVTM[:, tt * KVO:(tt + 1) * KVO], psKV[sl][:, 0:KVO]), inc=s_dv)
        done_sc_evac[sl] = s_dv.n
        qs = tt * QO
        ks = tt * KVO
        sq = sq2[sl]
        if done_dv_sq[sl]:
            P.wait("scalar", s_dv, done_dv_sq[sl])
        P.wait("scalar", s_dv, done_sc_evac[sl])
        P.op("scalar", lambda e, qs=qs, sq=sq: e.activation(
            out=sq[:, 0:512], in_=QTM[:, qs:qs + 512], func=ACT.Square),
            inc=s_sc)
        P.op("scalar", lambda e, ks=ks, sq=sq: e.activation(
            out=sq[:, 512:640], in_=KVTM[:, ks:ks + 128], func=ACT.Square),
            inc=s_sc)
        sc_q = s_sc.n
        P.wait("vector", s_sc, sc_q)
        P.op("vector", lambda e, sq=sq: e.tensor_reduce(
            out=srow[:, 8:12],
            in_=sq[:, 0:512].rearrange("p (h d) -> p h d", h=4),
            axis=AX.X, op=ALU.add), inc=s_dv)
        P.op("vector", lambda e, sq=sq: e.tensor_reduce(
            out=srow[:, 12:13], in_=sq[:, 512:640], axis=AX.X, op=ALU.add),
            inc=s_dv)
        chain("vector", s_dv)
        P.op("vector", lambda e: e.tensor_scalar(
            out=srow[:, 8:13], in0=srow[:, 8:13], scalar1=1.0 / 128.0,
            scalar2=None, op0=ALU.mult), inc=s_dv)
        v_ms = s_dv.n
        P.wait("scalar", s_dv, v_ms)
        P.op("scalar", lambda e: e.activation(
            out=srow[:, 8:13], in_=srow[:, 8:13], func=ACT.Sqrt,
            bias=EPS_RMS), inc=s_sc)
        sc_rs = s_sc.n
        P.wait("vector", s_sc, sc_rs)
        P.op("vector", lambda e: e.reciprocal(
            out=srow[:, 8:13], in_=srow[:, 8:13]), inc=s_dv)
        chain("vector", s_dv)
        qv = QTM[:, qs:qs + 512].rearrange("p (h d) -> p h d", h=4)
        P.op("vector", lambda e, qv=qv: e.scalar_tensor_tensor(
            out=qv, in0=qv, scalar=1.0,
            in1=srow[:, 8:12].unsqueeze(2).broadcast_to([128, 4, 128]),
            op0=ALU.mult, op1=ALU.mult), inc=s_dv)
        P.op("vector", lambda e, ks=ks: e.tensor_scalar(
            out=KVTM[:, ks:ks + 128], in0=KVTM[:, ks:ks + 128],
            scalar1=srow[:, 12:13], scalar2=None, op0=ALU.mult), inc=s_dv)
        chain("vector", s_dv)
        qA = qv[:, :, 0:8]
        qB = qv[:, :, 8:16]
        kA = KVTM[:, ks:ks + 8]
        kB = KVTM[:, ks + 8:ks + 16]
        cosb = cosT[:, tt * 8:(tt + 1) * 8]
        sinb = sinT[:, tt * 8:(tt + 1) * 8]
        cosb4 = cosb.unsqueeze(1).broadcast_to([128, 4, 8])
        sinb4 = sinb.unsqueeze(1).broadcast_to([128, 4, 8])
        rA = rp[:, 0:32].rearrange("p (h d) -> p h d", h=4)
        rB = rp[:, 32:64].rearrange("p (h d) -> p h d", h=4)
        sA = sq[:, 640:672].rearrange("p (h d) -> p h d", h=4)
        sB = sq[:, 672:704].rearrange("p (h d) -> p h d", h=4)
        P.op("vector", lambda e, qA=qA, cosb4=cosb4, rA=rA: e.tensor_tensor(
            out=rA, in0=qA, in1=cosb4, op=ALU.mult))
        P.op("vector", lambda e, qB=qB, sinb4=sinb4, rB=rB: e.tensor_tensor(
            out=rB, in0=qB, in1=sinb4, op=ALU.mult))
        P.op("vector", lambda e, qA=qA, sinb4=sinb4, sA=sA: e.tensor_tensor(
            out=sA, in0=qA, in1=sinb4, op=ALU.mult))
        P.op("vector", lambda e, qB=qB, cosb4=cosb4, sB=sB: e.tensor_tensor(
            out=sB, in0=qB, in1=cosb4, op=ALU.mult))
        P.op("vector", lambda e, kA=kA, cosb=cosb: e.tensor_tensor(
            out=rp[:, 64:72], in0=kA, in1=cosb, op=ALU.mult))
        P.op("vector", lambda e, kB=kB, sinb=sinb: e.tensor_tensor(
            out=rp[:, 72:80], in0=kB, in1=sinb, op=ALU.mult))
        P.op("vector", lambda e, kA=kA, sinb=sinb: e.tensor_tensor(
            out=rp[:, 80:88], in0=kA, in1=sinb, op=ALU.mult))
        P.op("vector", lambda e, kB=kB, cosb=cosb: e.tensor_tensor(
            out=rp[:, 88:96], in0=kB, in1=cosb, op=ALU.mult), inc=s_dv)
        chain("vector", s_dv)
        P.op("vector", lambda e, qA=qA, rA=rA, rB=rB: e.tensor_tensor(
            out=qA, in0=rA, in1=rB, op=ALU.add))
        P.op("vector", lambda e, qB=qB, sA=sA, sB=sB: e.tensor_tensor(
            out=qB, in0=sA, in1=sB, op=ALU.subtract))
        P.op("vector", lambda e, kA=kA: e.tensor_tensor(
            out=kA, in0=rp[:, 64:72], in1=rp[:, 72:80], op=ALU.add))
        P.op("vector", lambda e, kB=kB: e.tensor_tensor(
            out=kB, in0=rp[:, 80:88], in1=rp[:, 88:96], op=ALU.subtract),
            inc=s_dv)
        chain("vector", s_dv)
        done_dv_sq[0] = s_dv.n



    p1_load_ln(0)
    for tt in range(NT):
        if tt + 1 < NT:
            p1_load_ln(tt + 1)
        p1_tile(tt)
    if upto == 1:
        dbg_dump("QTM", QTM[:])
        dbg_dump("KVTM", KVTM[:])

    if upto == 2:
        dbg_dump("QTM", QTM[:])
        dbg_dump("KVTM", KVTM[:])
    v_p2_done = s_dv.n

    P.label = "P3"
    if upto >= 3:
        P.wait("tensor", s_dv, v_p2_done)
        strip_free = [0] * 4
        strips = []
        for ttg in range(8):
            for h in range(NHL):
                strips.append(("q", ttg, h))
            strips.append(("k", ttg, 0))
        for sidx, (kind, ttg, h) in enumerate(strips):
            slot = sidx % 4
            bank = psT[slot // 2]
            coff = (slot % 2) * 1024
            if strip_free[slot]:
                P.wait("tensor", s_dv, strip_free[slot])
            for k in range(4):
                tt = ttg * 4 + k
                if kind == "q":
                    src = QTM[:, tt * QO + h * 128: tt * QO + (h + 1) * 128]
                else:
                    src = KVTM[:, tt * KVO: tt * KVO + 128]
                P.op("tensor", lambda e, b=bank, c=coff, k=k, s=src: e.transpose(
                    b[:, c + k * 128:c + (k + 1) * 128], s, ident[:]),
                    inc=s_pe if k == 3 else None)
            pe_s = s_pe.n
            P.wait("vector", s_pe, pe_s)
            if kind == "q":
                dst = QFM[:, h * T + ttg * 512: h * T + (ttg + 1) * 512]
            else:
                dst = KFM[:, ttg * 512:(ttg + 1) * 512]
            P.op("vector", lambda e, d=dst, b=bank, c=coff: e.tensor_copy(
                d, b[:, c:c + 512]), inc=s_dv)
            strip_free[slot] = s_dv.n
    v_p3_done = s_dv.n
    pe_p3 = s_pe.n
    if upto == 3:
        dbg_dump("QFM", QFM[:])
        dbg_dump("KFM", KFM[:])

    A.free("z1f1", "z1f0", "z1tm1", "z1tm0", "sq21", "sq20",
           "xbuf1", "xbuf0", "QTM", "kv_sb", "wq_sb")
    A.free("psKV1", "psKV0", "psQ1", "psQ0", "psT1", "psT0")

    wp_sb = A.sbuf("wp_sb", [128, 4 * D], BF16)
    Pst = [A.sbuf(f"Pst{i}", [128, 2048], BF16) for i in range(2)]
    PT = [A.sbuf(f"PT{i}", [128, 1024], BF16) for i in range(2)]
    P.label = "P4"
    if upto >= 4:
        psST = [A.psum(f"psST{i}", [128, 1024], F32) for i in range(2)]
        psY = [A.psum(f"psY{i}", [128, 512], F32) for i in range(2)]
        psl = [A.psum(f"psl{i}", [128, 512], F32) for i in range(2)]

        P.wait("tensor", s_dv, v_p3_done)
        done_sc_exp = [0, 0]
        done_pe_pt = [0, 0]
        done_dv_y = [0, 0]
        uidx = 0
        for h in range(NHL):
            for qc in range(8):
                n_st = 4 * (qc + 1)
                npair = n_st // 2
                par = uidx % 2
                qsl = QFM[:, h * T + qc * 512: h * T + (qc + 1) * 512]
                pair_sc = [0] * npair
                pair_dv = [0] * npair

                def emit_scores(p):
                    slp = p % 2
                    if done_sc_exp[slp]:
                        P.wait("tensor", s_sc, done_sc_exp[slp])
                    for z in range(2):
                        st = p * 2 + z
                        P.op("tensor", lambda e, st=st, z=z, slp=slp, q=qsl: e.matmul(
                            psST[slp][:, z * 512:(z + 1) * 512],
                            KFM[:, st * 128:(st + 1) * 128], q,
                            start=True, stop=True),
                            inc=s_pe if z == 1 else None)
                    return s_pe.n

                def emit_exp_mask(p, pe_val):
                    slp = p % 2
                    P.wait("scalar", s_pe, pe_val)
                    if done_pe_pt[slp]:
                        P.wait("scalar", s_pe, done_pe_pt[slp])
                    P.op("scalar", lambda e, slp=slp, h=h: e.activation(
                        out=PT[slp][:], in_=psST[slp][:], func=ACT.Exp,
                        bias=bcast16[:, 8 + h:9 + h],
                        scale=bcast16[:, 12 + h:13 + h]),
                        inc=s_sc)
                    done_sc_exp[slp] = s_sc.n
                    pair_sc[p] = s_sc.n
                    dvv = 0
                    for z in range(2):
                        st = p * 2 + z
                        j = st - 4 * qc
                        if 0 <= j < 4:
                            P.wait("vector", s_sc, pair_sc[p])
                            P.op("vector", lambda e, slp=slp, z=z, j=j: e.tensor_tensor(
                                out=PT[slp][:, z * 512:(z + 1) * 512],
                                in0=PT[slp][:, z * 512:(z + 1) * 512],
                                in1=masks[:, j * 512:(j + 1) * 512],
                                op=ALU.mult), inc=s_dv)
                            dvv = s_dv.n
                    pair_dv[p] = dvv

                def emit_lpv(p):
                    slp = p % 2
                    P.wait("tensor", s_sc, pair_sc[p])
                    if pair_dv[p]:
                        P.wait("tensor", s_dv, pair_dv[p])
                    for z in range(2):
                        st = p * 2 + z
                        first = st == 0
                        last = st == n_st - 1
                        P.op("tensor", lambda e, slp=slp, z=z, par=par,
                             first=first, last=last: e.matmul(
                                 psl[par][0:1, :], onesb[:, 0:1],
                                 PT[slp][:, z * 512:(z + 1) * 512],
                                 start=first, stop=last), inc=None)
                        P.op("tensor", lambda e, slp=slp, z=z, st=st, par=par,
                             first=first, last=last: e.matmul(
                                 psY[par][:],
                                 KVTM[:, st * KVO + 128:(st + 1) * KVO],
                                 PT[slp][:, z * 512:(z + 1) * 512],
                                 start=first, stop=last),
                             inc=s_pe if z == 1 else None)
                    done_pe_pt[slp] = s_pe.n
                    return s_pe.n

                if done_dv_y[par]:
                    P.wait("tensor", s_dv, done_dv_y[par])
                pev = emit_scores(0)
                emit_exp_mask(0, pev)
                for p in range(1, npair):
                    pev = emit_scores(p)
                    emit_exp_mask(p, pev)
                    emit_lpv(p - 1)
                pe_last = emit_lpv(npair - 1)

                P.wait("vector", s_pe, pe_last)
                P.op("vector", lambda e, par=par: e.tensor_copy(
                    lrow[:], psl[par][0:1, :]), inc=s_dv)
                chain("vector", s_dv)
                P.op("vector", lambda e: e.reciprocal(
                    out=rrow[:], in_=lrow[:]), inc=s_dv)
                v_r = s_dv.n
                P.wait("tensor", s_dv, v_r)
                pe_b = P.op("tensor", lambda e, par=par: e.matmul(
                    psl[par][:], onesf32[0:1, :], rrow[0:1, :],
                    start=True, stop=True), inc=s_pe)
                P.wait("vector", s_pe, pe_b)
                P.op("vector", lambda e, par=par: e.tensor_copy(
                    rbc[:], psl[par][:]), inc=s_dv)
                chain("vector", s_dv)
                P.op("vector", lambda e, h=h, qc=qc, par=par: e.scalar_tensor_tensor(
                    out=YFM[:, h * T + qc * 512:h * T + (qc + 1) * 512],
                    in0=psY[par][:], scalar=1.0, in1=rbc[:],
                    op0=ALU.mult, op1=ALU.mult), inc=s_dv)
                done_dv_y[par] = s_dv.n
                uidx += 1
        A.free("psl1", "psl0", "psY1", "psY0", "psST1", "psST0")
    v_p4_done = s_dv.n
    if upto == 4:
        dbg_dump("YFM", YFM[:])
    A.free("PT1", "PT0")
    A.free("sinT", "cosT", "masks")

    P.label = "P5"
    if upto >= 5:
        psW = [A.psum(f"psW{i}", [128, 512], F32) for i in range(2)]
        P.wait("gpsimd", s_pe, pe_p3)
        P.op("gpsimd", lambda e: e.dma_start(
            out=wp_sb[:], in_=d_wp.rearrange("p a b -> p (a b)")),
            inc=s_gx[2], k=16)
        g_wp = s_gx[2].n
        P.wait("tensor", s_gx[2], g_wp)
        P.wait("tensor", s_dv, v_p4_done)
        done_dv_w = [0, 0]
        done_pst_dma = [0, 0]
        chunk_ready = {}
        tt_order = [8 * r + 2 * k + z for k in range(4) for r in range(4)
                    for z in range(2)]
        for ti, tt in enumerate(tt_order):
            sl = ti % 2
            if done_pst_dma[sl]:
                P.wait("vector", s_sx[2 + sl], done_pst_dma[sl])
            for hoq in range(4):
                wsl = (tt * 4 + hoq) % 2
                if done_dv_w[wsl]:
                    P.wait("tensor", s_dv, done_dv_w[wsl])
                for dc in range(4):
                    P.op("tensor", lambda e, tt=tt, hoq=hoq, dc=dc, wsl=wsl: e.matmul(
                        psW[wsl][:],
                        YFM[:, dc * T + tt * 128: dc * T + (tt + 1) * 128],
                        wp_sb[:, dc * D + hoq * 512: dc * D + (hoq + 1) * 512],
                        start=(dc == 0), stop=(dc == 3)),
                        inc=s_pe if dc == 3 else None)
                pe_w = s_pe.n
                P.wait("vector", s_pe, pe_w)
                P.op("vector", lambda e, sl=sl, hoq=hoq, wsl=wsl: e.tensor_copy(
                    Pst[sl][:, hoq * 512:(hoq + 1) * 512], psW[wsl][:]),
                    inc=s_dv)
                done_dv_w[wsl] = s_dv.n
            v_tile = s_dv.n
            P.wait("sync", s_dv, v_tile)
            kq, rq, zq = (tt % 8) // 2, tt // 8, tt % 2
            P.op("sync", lambda e, kq=kq, rq=rq, zq=zq, sl=sl: e.dma_start(
                out=d_prs[kq, rq, zq * 128:(zq + 1) * 128, :], in_=Pst[sl][:]),
                inc=s_sx[2 + sl], k=16)
            done_pst_dma[sl] = s_sx[2 + sl].n
            if ti % 8 == 7:
                chunk_ready[ti // 8] = (s_sx[2].n, s_sx[3].n)
        A.free("psW1", "psW0")
    sx2_prs, sx3_prs = s_sx[2].n, s_sx[3].n
    A.free("Pst1", "Pst0", "wp_sb", "kv_sb_dummy") if False else None
    A.free("Pst1", "Pst0", "wp_sb")
    A.free("KFM", "QFM", "KVTM", "YFM")

    rw_sb = A.sbuf("rw_sb", [128, NI * NE], BF16, side="right")
    sgu_sb = [A.sbuf(f"sgu_sb{i}", [128, 2 * NI * 128], BF16, side="right") for i in range(2)]
    sd_sb = [A.sbuf(f"sd_sb{i}", [128, NIO * 512], BF16, side="right") for i in range(2)]
    sgu_ready = {}
    sd_ready = {}
    cc_done = 0
    P.label = "PX"
    if upto >= 5:
        P.wait("gpsimd", s_dv, v_p4_done)
        P.op("gpsimd", lambda e: e.dma_start(
            out=rw_sb[:], in_=d_rw.rearrange("p a b -> p (a b)")),
            inc=s_gx[3], k=16)
        g_rw = s_gx[3].n
        P.op("gpsimd", lambda e: e.dma_start(
            out=sgu_sb[0][:, 0:NI * 128],
            in_=d_sg[0].rearrange("p a b -> p (a b)")), inc=s_gx[4], k=16)
        P.op("gpsimd", lambda e: e.dma_start(
            out=sgu_sb[0][:, NI * 128:],
            in_=d_su[0].rearrange("p a b -> p (a b)")), inc=s_gx[4], k=16)
        sgu_ready[0] = (4, s_gx[4].n)
        P.op("gpsimd", lambda e: e.dma_start(
            out=sd_sb[0][:], in_=d_sd[0].rearrange("p a b -> p (a b)")),
            inc=s_gx[5], k=16)
        sd_ready[0] = (5, s_gx[5].n)
        cc_chunk = {}
        for k in range(4):
            P.wait("gpsimd", s_sx[2], chunk_ready[k][0])
            P.wait("gpsimd", s_sx[3], chunk_ready[k][1])
            cc_chunk[k] = P.op("gpsimd", lambda e, k=k: e.collective_compute(
                "ReduceScatter", ALU.add,
                replica_groups=[[0, 1, 2, 3], [4, 5, 6, 7]],
                ins=[d_prs[k]],
                outs=[d_pq[k * 256:(k + 1) * 256, :]]), inc=s_cc)
        cc_done = s_cc.n

    X2 = A.sbuf("X2", [128, NQT * 2048], F32)
    z2tm_all = A.sbuf("z2tm_all", [128, NQT * 2048], BF16)
    z2f = A.sbuf("z2f", [128, NI * 1024], BF16)
    P.label = "P6"
    if upto >= 6:
        sqscr6 = A.sbuf("sqscr6", [128, 2048], F32)
        psT2 = [A.psum(f"psT2{i}", [128, 2048], BF16) for i in range(2)]
        done_pe_T2 = [0, 0]
        done_dv_z2 = [0, 0]
        done_accum = [0, 0]
        ln_guard = [0]
        P.wait("sync", s_sx[2], sx2_prs)
        P.wait("sync", s_sx[3], sx3_prs)
        P.wait("sync", s_pe, s_pe.n)
        for tt in range(NQT):
            sl = tt % 2
            X2t = X2[:, tt * 2048:(tt + 1) * 2048]
            if done_accum[sl]:
                P.wait("sync", s_gx[sl], done_accum[sl])
            P.op("sync", lambda e, tt=tt: e.dma_start(
                out=X2[:, tt * 2048:(tt + 1) * 2048],
                in_=d_xo[tt * 128:(tt + 1) * 128, :]), inc=s_sx[sl], k=16)
            sxv = s_sx[sl].n
            P.wait("gpsimd", s_sx[sl], sxv)
            P.wait("gpsimd", s_cc, cc_chunk[tt // 2])
            P.op("gpsimd", lambda e, tt=tt: e.dma_start(
                out=X2[:, tt * 2048:(tt + 1) * 2048],
                in_=d_pq[tt * 128:(tt + 1) * 128, :],
                accum_op=ALU.add), inc=s_gx[sl], k=16)
            done_accum[sl] = s_gx[sl].n
            P.wait("vector", s_gx[sl], done_accum[sl])
            P.wait("scalar", s_gx[sl], done_accum[sl])
            z2t = z2tm_all[:, tt * 2048:(tt + 1) * 2048]
            v_apply = emit_ln(tt, sl, X2t, z2t,
                              0 if sl == 0 else 4, z2t,
                              prev_dv=ln_guard[0])
            c0 = 0 if sl == 0 else 4
            chain("vector", s_dv)
            P.op("vector", lambda e, X2t=X2t, c=c0: e.tensor_scalar(
                out=sqscr6[:, 0:2048], in0=X2t, scalar1=srow[:, c:c + 1],
                scalar2=srow[:, c + 2:c + 3], op0=ALU.add, op1=ALU.mult),
                inc=s_dv)
            chain("vector", s_dv)
            P.op("vector", lambda e, X2t=X2t: e.tensor_tensor(
                out=X2t, in0=X2t, in1=sqscr6[:, 0:2048], op=ALU.add), inc=s_dv)
            ln_guard[0] = s_dv.n
            P.wait("tensor", s_dv, v_apply)
            if done_dv_z2[sl]:
                P.wait("tensor", s_dv, done_dv_z2[sl])
            for ic in range(NI):
                P.op("tensor", lambda e, ic=ic, tt=tt, sl=sl: e.transpose(
                    psT2[sl][:, ic * 128:(ic + 1) * 128],
                    z2tm_all[:, tt * 2048 + ic * 128: tt * 2048 + (ic + 1) * 128],
                    ident[:]),
                    inc=s_pe if ic == NI - 1 else None)
            done_pe_T2[sl] = s_pe.n
            P.wait("vector", s_pe, done_pe_T2[sl])
            zv = z2f[:].rearrange("p (a b) -> p a b", a=NI)[:, :, tt * 128:(tt + 1) * 128]
            P.op("vector", lambda e, sl=sl, zv=zv: e.tensor_copy(
                zv, psT2[sl][:].rearrange("p (a b) -> p a b", a=NI)), inc=s_dv)
            done_dv_z2[sl] = s_dv.n
        A.free("sqscr6")
        A.free("psT21", "psT20")
    v_p6_done = s_dv.n
    if upto == 6:
        dbg_dump("X2", X2[:])
        dbg_dump("z2f", z2f[:])

    done_sgu_use_g = [0, 0]
    done_sd_use_g = [0, 0]
    HSH = A.sbuf("HSH", [128, NIO * 1024], BF16)
    P.label = "P7"
    if upto >= 7:
        psR = [A.psum(f"psR{i}", [128, 512], F32) for i in range(2)]
        psG = [A.psum(f"psG{i}", [128, 512], F32) for i in range(2)]
        psU = [A.psum(f"psU{i}", [128, 512], F32) for i in range(2)]
        psD = [A.psum(f"psD{i}", [128, 512], F32) for i in range(2)]
        P.wait("tensor", s_dv, v_p6_done)
        P.wait("tensor", s_gx[3], g_rw)
        done_dv_r = [0, 0]
        for tt in range(NQT):
            rsl = tt % 2
            if done_dv_r[rsl]:
                P.wait("tensor", s_dv, done_dv_r[rsl])
            for hc in range(NI):
                P.op("tensor", lambda e, tt=tt, hc=hc, rsl=rsl: e.matmul(
                    psR[rsl][:, 0:8],
                    z2f[:, hc * 1024 + tt * 128: hc * 1024 + (tt + 1) * 128],
                    rw_sb[:, hc * 8:(hc + 1) * 8],
                    start=(hc == 0), stop=(hc == NI - 1)),
                    inc=s_pe if hc == NI - 1 else None)
            pe_r = s_pe.n
            P.wait("vector", s_pe, pe_r)
            P.op("vector", lambda e, tt=tt, rsl=rsl: e.tensor_copy(
                LG[:, tt * 8:(tt + 1) * 8], psR[rsl][:, 0:8]), inc=s_dv)
            done_dv_r[rsl] = s_dv.n
        chain("vector", s_dv)
        LG3 = LG[:].rearrange("p (a b) -> p a b", a=8)
        CW3 = combW[:].rearrange("p (a b) -> p a b", a=8)
        Rr = lambda c0: rscr[:, c0 * 8:c0 * 8 + 8]
        R3 = lambda c0: rscr[:, 8 + c0 * 64:8 + c0 * 64 + 64].rearrange(
            "p (a b) -> p a b", a=8)
        bc = lambda ap: ap.unsqueeze(2).broadcast_to([128, 8, 8])
        P.op("vector", lambda e: e.tensor_reduce(
            out=Rr(0), in_=LG3, axis=AX.X, op=ALU.max), inc=s_dv)
        chain("vector", s_dv)
        P.op("vector", lambda e: e.tensor_tensor(
            out=R3(0), in0=LG3, in1=bc(Rr(0)), op=ALU.subtract), inc=s_dv)
        v_sub = s_dv.n
        P.wait("scalar", s_dv, v_sub)
        P.op("scalar", lambda e: e.activation(
            out=rscr[:, 8:72], in_=rscr[:, 8:72], func=ACT.Exp), inc=s_sc)
        sc_e = s_sc.n
        P.wait("vector", s_sc, sc_e)
        P.op("vector", lambda e: e.tensor_reduce(
            out=Rr(0), in_=R3(0), axis=AX.X, op=ALU.add), inc=s_dv)
        chain("vector", s_dv)
        P.op("vector", lambda e: e.reciprocal(out=Rr(0), in_=Rr(0)), inc=s_dv)
        chain("vector", s_dv)
        P.op("vector", lambda e: e.tensor_tensor(
            out=R3(0), in0=R3(0), in1=bc(Rr(0)), op=ALU.mult), inc=s_dv)
        chain("vector", s_dv)
        P.op("vector", lambda e: e.tensor_reduce(
            out=Rr(0), in_=R3(0), axis=AX.X, op=ALU.max), inc=s_dv)
        chain("vector", s_dv)
        P.op("vector", lambda e: e.tensor_tensor(
            out=R3(1), in0=R3(0), in1=bc(Rr(0)), op=ALU.is_ge), inc=s_dv)
        chain("vector", s_dv)
        P.op("vector", lambda e: e.tensor_scalar(
            out=R3(2), in0=R3(1), scalar1=-1.0, scalar2=1.0,
            op0=ALU.mult, op1=ALU.add), inc=s_dv)
        chain("vector", s_dv)
        P.op("vector", lambda e: e.tensor_tensor(
            out=R3(3), in0=R3(0), in1=R3(2), op=ALU.mult), inc=s_dv)
        chain("vector", s_dv)
        P.op("vector", lambda e: e.tensor_reduce(
            out=Rr(0), in_=R3(3), axis=AX.X, op=ALU.max), inc=s_dv)
        chain("vector", s_dv)
        P.op("vector", lambda e: e.tensor_tensor(
            out=R3(4), in0=R3(3), in1=bc(Rr(0)), op=ALU.is_ge), inc=s_dv)
        chain("vector", s_dv)
        P.op("vector", lambda e: e.tensor_tensor(
            out=R3(1), in0=R3(1), in1=R3(4), op=ALU.add), inc=s_dv)
        chain("vector", s_dv)
        P.op("vector", lambda e: e.tensor_tensor(
            out=CW3, in0=R3(0), in1=R3(1), op=ALU.mult), inc=s_dv)
        P.op("vector", lambda e: e.tensor_copy(MSKf[:], rscr[:, 72:136]), inc=s_dv)
        chain("vector", s_dv)
        P.op("vector", lambda e: e.tensor_copy(MSKb[:], MSKf[:]), inc=s_dv)
        v_comb = s_dv.n
        if "combW" in d_dbg:
            dbg_dump("combW", combW[:])
            dbg_dump("LG", LG[:])

        done_sgu_use_g[0] = done_sgu_use_g[1] = 0
        done_sgu_use = done_sgu_use_g
        done_gu_cons = [[0, 0], [0, 0]]
        for c in range(NIO):
            sl = c % 2
            if c + 1 < NIO:
                osl = (c + 1) % 2
                if done_sgu_use[osl]:
                    P.wait("gpsimd", s_pe, done_sgu_use[osl])
                gsem = 4 if osl == 0 else 6
                P.op("gpsimd", lambda e, c=c, osl=osl: e.dma_start(
                    out=sgu_sb[osl][:, 0:NI * 128],
                    in_=d_sg[c + 1].rearrange("p a b -> p (a b)")),
                    inc=s_gx[gsem], k=16)
                P.op("gpsimd", lambda e, c=c, osl=osl: e.dma_start(
                    out=sgu_sb[osl][:, NI * 128:],
                    in_=d_su[c + 1].rearrange("p a b -> p (a b)")),
                    inc=s_gx[gsem], k=16)
                sgu_ready[c + 1] = (gsem, s_gx[gsem].n)
            P.wait("tensor", s_gx[sgu_ready[c][0]], sgu_ready[c][1])
            for half in range(2):
                if done_gu_cons[half][0]:
                    P.wait("tensor", s_sc, done_gu_cons[half][0])
                    P.wait("tensor", s_dv, done_gu_cons[half][1])
                for hc in range(NI):
                    P.op("tensor", lambda e, sl=sl, hc=hc, half=half: e.matmul(
                        psG[half][:],
                        sgu_sb[sl][:, hc * 128:(hc + 1) * 128],
                        z2f[:, hc * 1024 + half * 512: hc * 1024 + half * 512 + 512],
                        start=(hc == 0), stop=(hc == NI - 1)), inc=None)
                    P.op("tensor", lambda e, sl=sl, hc=hc, half=half: e.matmul(
                        psU[half][:],
                        sgu_sb[sl][:, NI * 128 + hc * 128: NI * 128 + (hc + 1) * 128],
                        z2f[:, hc * 1024 + half * 512: hc * 1024 + half * 512 + 512],
                        start=(hc == 0), stop=(hc == NI - 1)),
                        inc=s_pe if hc == NI - 1 else None)
                pe_gu = s_pe.n
                P.wait("scalar", s_pe, pe_gu)
                P.op("scalar", lambda e, half=half: e.activation(
                    out=sgb[:, half * 512:(half + 1) * 512], in_=psG[half][:],
                    func=ACT.Sigmoid), inc=s_sc)
                sc_sg = s_sc.n
                P.wait("vector", s_sc, sc_sg)
                P.op("vector", lambda e, c=c, half=half: e.scalar_tensor_tensor(
                    out=HSH[:, c * 1024 + half * 512: c * 1024 + half * 512 + 512],
                    in0=psU[half][:], scalar=1.0,
                    in1=sgb[:, half * 512:(half + 1) * 512],
                    op0=ALU.mult, op1=ALU.mult), inc=s_dv)
                done_gu_cons[half] = [sc_sg, s_dv.n]
            done_sgu_use[sl] = s_pe.n
        v_hsh = s_dv.n

        done_dv_d = [0, 0]
        done_sd_use_g[0] = done_sd_use_g[1] = 0
        done_sd_use = done_sd_use_g
        for hoq in range(4):
            sl = hoq % 2
            if hoq + 1 < 4:
                osl = (hoq + 1) % 2
                if done_sd_use[osl]:
                    P.wait("gpsimd", s_pe, done_sd_use[osl])
                dsem = 5 if osl == 0 else 7
                P.op("gpsimd", lambda e, hoq=hoq, osl=osl: e.dma_start(
                    out=sd_sb[osl][:],
                    in_=d_sd[hoq + 1].rearrange("p a b -> p (a b)")),
                    inc=s_gx[dsem], k=16)
                sd_ready[hoq + 1] = (dsem, s_gx[dsem].n)
            P.wait("tensor", s_gx[sd_ready[hoq][0]], sd_ready[hoq][1])
            P.wait("tensor", s_dv, v_hsh)
            for tt in range(NQT):
                dsl = (hoq * NQT + tt) % 2
                if done_dv_d[dsl]:
                    P.wait("tensor", s_dv, done_dv_d[dsl])
                for c in range(NIO):
                    P.op("tensor", lambda e, c=c, tt=tt, sl=sl, dsl=dsl: e.matmul(
                        psD[dsl][:],
                        HSH[:, c * 1024 + tt * 128: c * 1024 + (tt + 1) * 128],
                        sd_sb[sl][:, c * 512:(c + 1) * 512],
                        start=(c == 0), stop=(c == NIO - 1)),
                        inc=s_pe if c == NIO - 1 else None)
                pe_d = s_pe.n
                done_sd_use[sl] = pe_d
                P.wait("vector", s_pe, pe_d)
                P.op("vector", lambda e, tt=tt, hoq=hoq, dsl=dsl: e.tensor_tensor(
                    out=X2[:, tt * 2048 + hoq * 512: tt * 2048 + (hoq + 1) * 512],
                    in0=X2[:, tt * 2048 + hoq * 512: tt * 2048 + (hoq + 1) * 512],
                    in1=psD[dsl][:], op=ALU.add), inc=s_dv)
                done_dv_d[dsl] = s_dv.n
    v_p7_done = s_dv.n
    if upto == 7:
        dbg_dump("X2", X2[:])
    A.free("HSH")
    A.free("z2f")
    if upto >= 7:
        A.free("psD1", "psD0", "psU1", "psU0", "psG1", "psG0",
               "psR1", "psR0")
    A.free("sd_sb1", "sd_sb0", "sgu_sb1", "sgu_sb0")

    P.label = "P8"
    CAP = 384
    if upto >= 8:
        ciota = A.sbuf("ciota", [128, CAP], F32)
        R_ALL = A.sbuf("R_ALL", [128, 64], F32)
        LDS = A.sbuf("LDS", [1, 64], F32)
        BASES = A.sbuf("BASES", [1, 64], F32)
        triL = A.sbuf("triL", [128, 128], BF16)
        S_e = A.sbuf("S_e", [128, 8 * CAP], BF16)
        ST_e = A.sbuf("ST_e", [128, 24 * 128], BF16)
        z2g = A.sbuf("z2g", [128, NI * CAP], BF16)
        he_g = A.sbuf("he_g", [128, NIO * CAP], BF16)
        rg = A.sbuf("rg", [128, 3 * 2048], BF16)
        wgu_sb = [A.sbuf(f"wgu_sb{i}", [128, 2 * NI * 128], BF16) for i in range(2)]
        wd_sb = [A.sbuf(f"wd_sb{i}", [128, NIO * 512], BF16) for i in range(2)]
        psA = A.psum("psA", [128, 512], F32)
        psB = A.psum("psB", [128, 512], F32)
        psC = A.psum("psC", [128, 512], F32)
        psD2 = A.psum("psD2", [128, 512], F32)
        psE = A.psum("psE", [128, 512], F32)
        psF = A.psum("psF", [128, 512], F32)
        psTb = [A.psum(f"psTb{i}", [128, 512], BF16) for i in range(2)]

        wgu_ready = {}
        wd_ready = {}
        done_wgu_use = list(done_sgu_use_g)
        done_wd_use = list(done_sd_use_g)

        def prefetch_gu(idx):
            if idx >= NE * NIO:
                return
            ne_, nc_ = divmod(idx, NIO)
            osl = idx % 2
            if done_wgu_use[osl]:
                P.wait("gpsimd", s_pe, done_wgu_use[osl])
                done_wgu_use[osl] = 0
            gsem = 4 if osl == 0 else 6
            P.op("gpsimd", lambda e, ne_=ne_, nc_=nc_, osl=osl: e.dma_start(
                out=wgu_sb[osl][:, 0:NI * 128],
                in_=d_wg[ne_, nc_].rearrange("p a b -> p (a b)")),
                inc=s_gx[gsem], k=16)
            P.op("gpsimd", lambda e, ne_=ne_, nc_=nc_, osl=osl: e.dma_start(
                out=wgu_sb[osl][:, NI * 128:],
                in_=d_wu[ne_, nc_].rearrange("p a b -> p (a b)")),
                inc=s_gx[gsem], k=16)
            wgu_ready[idx] = (gsem, s_gx[gsem].n)

        def prefetch_wd(idx):
            if idx >= NE * 4:
                return
            ne_, nq_ = divmod(idx, 4)
            osl = idx % 2
            if done_wd_use[osl]:
                P.wait("gpsimd", s_pe, done_wd_use[osl])
                done_wd_use[osl] = 0
            dsem = 5 if osl == 0 else 7
            P.op("gpsimd", lambda e, ne_=ne_, nq_=nq_, osl=osl: e.dma_start(
                out=wd_sb[osl][:],
                in_=d_wd[ne_, nq_].rearrange("p a b -> p (a b)")),
                inc=s_gx[dsem], k=16)
            wd_ready[idx] = (dsem, s_gx[dsem].n)

        P.wait("gpsimd", s_pe, s_pe.n)
        P.wait("gpsimd", s_dv, s_dv.n)
        prefetch_gu(0)
        prefetch_wd(0)

        P.op("gpsimd", lambda e: e.iota(
            LDS[0:1, 0:64], pattern=[[1, 64]], base=0, channel_multiplier=0,
            allow_small_or_imprecise_dtypes=True), inc=s_gp)
        chain("gpsimd", s_gp)
        P.op("gpsimd", lambda e: e.affine_select(
            out=triL[:], in_=onesb[:, 0:1].broadcast_to([128, 128]),
            pattern=[[1, 128]], compare_op=ALU.is_ge, fill=0.0,
            base=-1, channel_multiplier=-1), inc=s_gp)
        g_tri = s_gp.n
        P.op("gpsimd", lambda e: e.iota(
            ciota[:], pattern=[[1, CAP]], base=0, channel_multiplier=0,
            allow_small_or_imprecise_dtypes=True), inc=s_gp)
        g_ciota = s_gp.n
        P.wait("tensor", s_gp, g_ciota)
        P.wait("tensor", s_dv, v_comb)
        for tt in range(NQT):
            if tt:
                P.wait("tensor", s_dv, s_dv.n)
            P.op("tensor", lambda e, tt=tt: e.matmul(
                psA[0:1, 0:8], onesb[:, 0:1], MSKb[:, tt * 8:(tt + 1) * 8],
                start=True, stop=True), inc=s_pe)
            pe_ld = s_pe.n
            P.wait("vector", s_pe, pe_ld)
            P.op("vector", lambda e, tt=tt: e.tensor_copy(
                LDS[0:1, tt * 8:(tt + 1) * 8], psA[0:1, 0:8]), inc=s_dv)
            chain("vector", s_dv)
        P.op("vector", lambda e: e.memset(BASES[0:1, 0:8], 0.0), inc=s_dv)
        chain("vector", s_dv)
        for tt in range(1, NQT):
            P.op("vector", lambda e, tt=tt: e.tensor_tensor(
                out=BASES[0:1, tt * 8:(tt + 1) * 8],
                in0=BASES[0:1, (tt - 1) * 8:tt * 8],
                in1=LDS[0:1, (tt - 1) * 8:tt * 8], op=ALU.add), inc=s_dv)
            chain("vector", s_dv)
        v_bases = s_dv.n
        P.wait("tensor", s_dv, v_bases)
        for tt in range(NQT):
            if tt:
                P.wait("tensor", s_dv, s_dv.n)
            P.op("tensor", lambda e, tt=tt: e.matmul(
                psB[:, 0:8], triL[:], MSKb[:, tt * 8:(tt + 1) * 8],
                start=True, stop=False), inc=None)
            P.op("tensor", lambda e, tt=tt: e.matmul(
                psB[:, 0:8], onesf32[0:1, :], BASES[0:1, tt * 8:(tt + 1) * 8],
                start=False, stop=True), inc=s_pe)
            pe_r = s_pe.n
            P.wait("vector", s_pe, pe_r)
            P.op("vector", lambda e, tt=tt: e.tensor_copy(
                R_ALL[:, tt * 8:(tt + 1) * 8], psB[:, 0:8]), inc=s_dv)
            chain("vector", s_dv)
        v_rall = s_dv.n

        done_gu_cons2 = [[0, 0], [0, 0]]
        done_s_read = 0
        done_st_read = 0
        done_z2g_read = 0
        done_heg_read = 0
        done_rg_read = 0
        done_dv_x2 = [0, 0]
        for e_i in range(NE):
            if done_s_read:
                P.wait("vector", s_pe, done_s_read)
            for tt in range(NQT):
                P.op("vector", lambda e, tt=tt, e_i=e_i: e.tensor_scalar(
                    out=S_e[:, tt * CAP:(tt + 1) * CAP], in0=ciota[:],
                    scalar1=R_ALL[:, tt * 8 + e_i:tt * 8 + e_i + 1],
                    scalar2=MSKf[:, tt * 8 + e_i:tt * 8 + e_i + 1],
                    op0=ALU.is_equal, op1=ALU.mult), inc=s_dv)
            v_s = s_dv.n
            P.wait("tensor", s_dv, v_s)
            if done_st_read:
                P.wait("tensor", s_pe, done_st_read)
            st_copy = [0, 0]
            for tt in range(NQT):
                psl_t = psTb[tt % 2]
                if st_copy[tt % 2]:
                    P.wait("tensor", s_dv, st_copy[tt % 2])
                for cc in range(3):
                    P.op("tensor", lambda e, tt=tt, cc=cc, psl_t=psl_t: e.transpose(
                        psl_t[:, cc * 128:(cc + 1) * 128],
                        S_e[:, tt * CAP + cc * 128: tt * CAP + (cc + 1) * 128],
                        ident[:]), inc=s_pe if cc == 2 else None)
                pe_t = s_pe.n
                P.wait("vector", s_pe, pe_t)
                P.op("vector", lambda e, tt=tt, psl_t=psl_t: e.tensor_copy(
                    ST_e[:, tt * 384:(tt + 1) * 384], psl_t[:, 0:384]), inc=s_dv)
                st_copy[tt % 2] = s_dv.n
            v_st = s_dv.n

            if done_z2g_read:
                P.wait("tensor", s_pe, done_z2g_read)
            g_copy = [0, 0]
            for hc in range(NI):
                pg = psA if hc % 2 == 0 else psB
                if g_copy[hc % 2]:
                    P.wait("tensor", s_dv, g_copy[hc % 2])
                for tile in range(NQT):
                    P.op("tensor", lambda e, hc=hc, tile=tile, pg=pg: e.matmul(
                        pg[:, 0:CAP],
                        z2tm_all[:, tile * 2048 + hc * 128: tile * 2048 + (hc + 1) * 128],
                        S_e[:, tile * CAP:(tile + 1) * CAP],
                        start=(tile == 0), stop=(tile == NQT - 1)),
                        inc=s_pe if tile == NQT - 1 else None)
                pe_g = s_pe.n
                if hc == NI - 1:
                    done_s_read = pe_g
                P.wait("vector", s_pe, pe_g)
                P.op("vector", lambda e, hc=hc, pg=pg: e.tensor_copy(
                    z2g[:, hc * CAP:(hc + 1) * CAP], pg[:, 0:CAP]), inc=s_dv)
                g_copy[hc % 2] = s_dv.n
            v_z2g = s_dv.n

            P.wait("tensor", s_dv, v_z2g)
            if done_heg_read:
                P.wait("vector", s_pe, done_heg_read)
            for c in range(NIO):
                gidx = e_i * NIO + c
                sl = gidx % 2
                prefetch_gu(gidx + 1)
                P.wait("tensor", s_gx[wgu_ready[gidx][0]], wgu_ready[gidx][1])
                half = c % 2
                pg = psC if half == 0 else psE
                pu = psD2 if half == 0 else psF
                if done_gu_cons2[half][0]:
                    P.wait("tensor", s_sc, done_gu_cons2[half][0])
                    P.wait("tensor", s_dv, done_gu_cons2[half][1])
                for hc in range(NI):
                    P.op("tensor", lambda e, sl=sl, hc=hc, pg=pg: e.matmul(
                        pg[:, 0:CAP],
                        wgu_sb[sl][:, hc * 128:(hc + 1) * 128],
                        z2g[:, hc * CAP:(hc + 1) * CAP],
                        start=(hc == 0), stop=(hc == NI - 1)), inc=None)
                    P.op("tensor", lambda e, sl=sl, hc=hc, pu=pu: e.matmul(
                        pu[:, 0:CAP],
                        wgu_sb[sl][:, NI * 128 + hc * 128: NI * 128 + (hc + 1) * 128],
                        z2g[:, hc * CAP:(hc + 1) * CAP],
                        start=(hc == 0), stop=(hc == NI - 1)),
                        inc=s_pe if hc == NI - 1 else None)
                pe_gu = s_pe.n
                if c == NIO - 1:
                    done_z2g_read = pe_gu
                done_wgu_use[sl] = pe_gu
                P.wait("scalar", s_pe, pe_gu)
                P.op("scalar", lambda e, pg=pg, half=half: e.activation(
                    out=sgb[:, half * 512:half * 512 + CAP], in_=pg[:, 0:CAP],
                    func=ACT.Sigmoid), inc=s_sc)
                sc_sg = s_sc.n
                P.wait("vector", s_sc, sc_sg)
                P.op("vector", lambda e, c=c, pu=pu, half=half: e.scalar_tensor_tensor(
                    out=he_g[:, c * CAP:(c + 1) * CAP],
                    in0=pu[:, 0:CAP], scalar=1.0,
                    in1=sgb[:, half * 512:half * 512 + CAP],
                    op0=ALU.mult, op1=ALU.mult), inc=s_dv)
                done_gu_cons2[half] = [sc_sg, s_dv.n]
            v_heg = s_dv.n

            P.wait("tensor", s_dv, v_heg)
            if done_rg_read:
                P.wait("vector", s_pe, done_rg_read)
            d_copy = [s_dv.n, s_dv.n]
            for hoq in range(4):
                didx = e_i * 4 + hoq
                sl = didx % 2
                prefetch_wd(didx + 1)
                P.wait("tensor", s_gx[wd_ready[didx][0]], wd_ready[didx][1])
                for cc in range(3):
                    pd = psA if (hoq * 3 + cc) % 2 == 0 else psB
                    if d_copy[(hoq * 3 + cc) % 2]:
                        P.wait("tensor", s_dv, d_copy[(hoq * 3 + cc) % 2])
                    for io in range(NIO):
                        P.op("tensor", lambda e, io=io, cc=cc, hoq=hoq, sl=sl,
                             pd=pd: e.matmul(
                                 pd[:],
                                 he_g[:, io * CAP + cc * 128: io * CAP + (cc + 1) * 128],
                                 wd_sb[sl][:, io * 512:(io + 1) * 512],
                                 start=(io == 0), stop=(io == NIO - 1)),
                             inc=s_pe if io == NIO - 1 else None)
                    pe_d = s_pe.n
                    if hoq == 3 and cc == 2:
                        done_heg_read = pe_d
                    done_wd_use[sl] = pe_d
                    P.wait("vector", s_pe, pe_d)
                    P.op("vector", lambda e, cc=cc, hoq=hoq, pd=pd: e.tensor_copy(
                        rg[:, cc * 2048 + hoq * 512: cc * 2048 + (hoq + 1) * 512],
                        pd[:]), inc=s_dv)
                    d_copy[(hoq * 3 + cc) % 2] = s_dv.n
            v_rg = s_dv.n

            P.wait("tensor", s_dv, max(v_rg, v_st, v_heg))
            for tt in range(NQT):
                for hoq in range(4):
                    par = (tt * 4 + hoq) % 2
                    ps_s = psC if par == 0 else psE
                    if done_dv_x2[par]:
                        P.wait("tensor", s_dv, done_dv_x2[par])
                    for cc in range(3):
                        P.op("tensor", lambda e, tt=tt, cc=cc, hoq=hoq,
                             ps_s=ps_s: e.matmul(
                                 ps_s[:],
                                 ST_e[:, (tt * 3 + cc) * 128:(tt * 3 + cc + 1) * 128],
                                 rg[:, cc * 2048 + hoq * 512: cc * 2048 + (hoq + 1) * 512],
                                 start=(cc == 0), stop=(cc == 2)),
                             inc=s_pe if cc == 2 else None)
                    pe_s = s_pe.n
                    if tt == NQT - 1 and hoq == 3:
                        done_st_read = pe_s
                        done_rg_read = pe_s
                    P.wait("vector", s_pe, pe_s)
                    P.op("vector", lambda e, tt=tt, hoq=hoq, ps_s=ps_s,
                         e_i=e_i: e.scalar_tensor_tensor(
                             out=X2[:, tt * 2048 + hoq * 512: tt * 2048 + (hoq + 1) * 512],
                             in0=ps_s[:],
                             scalar=combW[:, tt * 8 + e_i: tt * 8 + e_i + 1],
                             in1=X2[:, tt * 2048 + hoq * 512: tt * 2048 + (hoq + 1) * 512],
                             op0=ALU.mult, op1=ALU.add), inc=s_dv)
                    done_dv_x2[par] = s_dv.n

    if upto >= 8:
        v_final = s_dv.n
        P.wait("sync", s_dv, v_final)
        P.op("sync", lambda e: e.dma_start(
            out=d_out.rearrange("(a p) d -> p a d", p=128),
            in_=X2[:].rearrange("p (a d) -> p a d", a=NQT)), inc=s_out, k=16)
    if s_out.n:
        P.wait("sync", s_out, s_out.n)

    block = es.enter_context(nc.Block())
    thunks = P.thunks

    def replay(eng_name, e):
        cur = None
        sid = None
        for label, th in thunks[eng_name]:
            if label != cur:
                if cur is not None:
                    nc.leave_named_scope(f"{cur}.{eng_name}", sid, False)
                cur = label
                sid, _ = nc.enter_named_scope(f"{cur}.{eng_name}", False)
            th(e)
        if cur is not None:
            nc.leave_named_scope(f"{cur}.{eng_name}", sid, False)

    @block.sync
    def _(e):
        replay("sync", e)

    @block.scalar
    def _(e):
        replay("scalar", e)

    @block.vector
    def _(e):
        replay("vector", e)

    @block.tensor
    def _(e):
        replay("tensor", e)

    @block.gpsimd
    def _(e):
        replay("gpsimd", e)

    es.close()
    A.close()
    return nc


_NC_CACHE = {}


def _get_nc(upto=99, debug=()):
    key = (upto, tuple(debug))
    if key not in _NC_CACHE:
        _NC_CACHE[key] = _build_program(upto=upto, debug=debug)
    return _NC_CACHE[key]


def _host_layout(inputs):
    x = np.ascontiguousarray(np.asarray(inputs["L0_attn_x"]))
    wq = np.asarray(inputs["L0_attn_wq"])
    wk = np.asarray(inputs["L0_attn_wk"])
    wv = np.asarray(inputs["L0_attn_wv"])
    wp = np.asarray(inputs["L0_attn_wp"])
    qg = np.asarray(inputs["L0_attn_qg"])
    rw = np.asarray(inputs["L0_ffn_router_w"])
    sg = np.asarray(inputs["L0_ffn_shared_gate"])
    su = np.asarray(inputs["L0_ffn_shared_up"])
    sd = np.asarray(inputs["L0_ffn_shared_down"])
    wg = np.asarray(inputs["L0_ffn_w_gate"])
    wu = np.asarray(inputs["L0_ffn_w_up"])
    wd = np.asarray(inputs["L0_ffn_w_down"])

    rw_r = np.ascontiguousarray(rw.reshape(NE, NI, 128).transpose(2, 1, 0))
    sg_r = np.ascontiguousarray(sg.reshape(NIO, 128, NI, 128).transpose(0, 3, 2, 1))
    su_r = np.ascontiguousarray(su.reshape(NIO, 128, NI, 128).transpose(0, 3, 2, 1))
    sd_r = np.ascontiguousarray(sd.reshape(4, 512, NIO, 128).transpose(0, 3, 2, 1))
    wg_r = np.ascontiguousarray(wg.reshape(NE, NIO, 128, NI, 128).transpose(0, 1, 4, 3, 2))
    wu_r = np.ascontiguousarray(wu.reshape(NE, NIO, 128, NI, 128).transpose(0, 1, 4, 3, 2))
    wd_r = np.ascontiguousarray(wd.reshape(NE, 4, 512, NIO, 128).transpose(0, 1, 4, 3, 2))

    in_maps = []
    for c in range(8):
        b, g = c // 4, c % 4
        wq_s = wq[QO * g:QO * (g + 1), :]
        kv_s = np.concatenate([wk[128 * g:128 * (g + 1), :],
                               wv[128 * g:128 * (g + 1), :]], axis=0)
        wq_r = np.ascontiguousarray(wq_s.reshape(QO, NI, 128).transpose(2, 1, 0))
        kv_r = np.ascontiguousarray(kv_s.reshape(KVO, NI, 128).transpose(2, 1, 0))
        wp_s = wp[:, QO * g:QO * (g + 1)]
        wp_r = np.ascontiguousarray(wp_s.reshape(D, 4, 128).transpose(2, 1, 0))
        in_maps.append({
            "xf": np.ascontiguousarray(x[b]),
            "xo": np.ascontiguousarray(x[b, 1024 * g:1024 * (g + 1)]),
            "qgg": np.ascontiguousarray(qg[NHL * g:NHL * (g + 1)].reshape(1, NHL)),
            "wq_r": wq_r, "kv_r": kv_r, "wp_r": wp_r,
            "rw_r": rw_r, "sg_r": sg_r, "su_r": su_r, "sd_r": sd_r,
            "wg_r": wg_r, "wu_r": wu_r, "wd_r": wd_r,
        })
    return in_maps


def kernel(**inputs):
    nc = _get_nc()
    in_maps = _host_layout(inputs)
    res = run_bass_kernel_spmd(nc, in_maps, core_ids=list(range(8)))
    out = np.empty((B, T, D), np.float32)
    for c in range(8):
        b, g = c // 4, c % 4
        out[b, 1024 * g:1024 * (g + 1)] = res.results[c]["out"]
    return out
